# version 02
# baseline (speedup 1.0000x reference)
import numpy as np
import ml_dtypes
import concourse.bass as bass
import concourse.mybir as mybir
from concourse.bass_utils import run_bass_kernel_spmd

F32 = mybir.dt.float32
BF16 = mybir.dt.bfloat16
AF = mybir.ActivationFunctionType
ALU = mybir.AluOpType
AX = mybir.AxisListType

MAGIC = 12582912.0
EPS = 1e-5
S = 8192
D = 1024
T = 512
NCH = S // T
N16 = 16
DFF = 4096
NE = 8

_NPHASE = 4


class KB:
    def __init__(self, nphase=_NPHASE, dbg=False):
        self.dbg = dbg
        self.nphase = nphase
        nc = bass.Bass("TRN2", target_bir_lowering=False,
                       detect_race_conditions=False)
        self.nc = nc
        self.c = {}
        self.sems = {}
        self._cms = {}
        self._alloc_seq = []

        di = {}
        def inp(name, shape, dt=F32):
            di[name] = nc.dram_tensor(name, list(shape), dt, kind="ExternalInput").ap()
        inp("u", (S, D))
        inp("A_log", (1, 16)); inp("xpw", (96, D)); inp("dtw", (D, 64))
        inp("dtb", (8, 128)); inp("Dv", (8, 128))
        inp("Win", (D, DFF)); inp("bin", (32, 128))
        inp("Wout", (DFF, D)); inp("bout", (8, 128))
        inp("Wq", (D, D)); inp("Wk", (D, D)); inp("Wv", (D, D)); inp("Wo", (D, D))
        inp("bo", (8, 128)); inp("gam", (8, 128)); inp("bet", (8, 128))
        inp("Wr", (D, 8)); inp("br", (1, 8))
        inp("W1", (NE, D, D)); inp("b1", (NE, 8, 128))
        inp("W2", (NE, D, D)); inp("b2", (NE, 8, 128))
        self.di = di
        self.out = nc.dram_tensor("out", [S, D], F32, kind="ExternalOutput").ap()

        self.xh1 = nc.dram_tensor("xh1", [D, S], F32).ap()
        self.xh2 = nc.dram_tensor("xh2", [D, S], F32).ap()
        self.xh3 = nc.dram_tensor("xh3", [D, S], F32).ap()
        self.hh = nc.dram_tensor("hh", [DFF, S], BF16).ap()
        self.kh = nc.dram_tensor("kh", [D, S], F32).ap()
        self.qh = nc.dram_tensor("qh", [D, S], BF16).ap()
        self.sth = nc.dram_tensor("sth", [8, S], F32).ap()
        self.cc_in = nc.dram_tensor("cc_in", [1, 1], F32)
        self.cc_out = nc.dram_tensor("cc_out", [1, 1], F32)
        self.cc_in2 = nc.dram_tensor("cc_in2", [1, 1], F32)
        self.cc_out2 = nc.dram_tensor("cc_out2", [1, 1], F32)

        for s in ["L", "G", "P", "V", "S", "ST", "CC"]:
            self.sems[s] = nc.semaphore(s).__enter__()
            self.c[s] = 0
        self.blk = nc.Block().__enter__()

    def inc(self, instr, sem, n=1):
        instr.then_inc(self.sems[sem], n)
        self.c[sem] += n
        return instr

    def w(self, eng, *sems):
        for s in sems:
            if self.c[s] > 0:
                eng.wait_ge(self.sems[s], self.c[s])

    def wv(self, eng, sem, val):
        if val > 0:
            eng.wait_ge(self.sems[sem], val)

    def sb(self, name, shape, dt=F32):
        t = self.nc.sbuf_tensor(name, list(shape), dt)
        self._cms[name] = t
        self._alloc_seq.append(name)
        return t.__enter__()

    def ps(self, name, shape, dt=F32):
        t = self.nc.psum_tensor(name, list(shape), dt)
        self._cms[name] = t
        self._alloc_seq.append(name)
        return t.__enter__()

    def free(self, *names):
        order = {n: i for i, n in enumerate(self._alloc_seq)}
        for n in sorted(names, key=lambda n: -order[n]):
            cm = self._cms.pop(n, None)
            if cm is not None:
                cm.__exit__(None, None, None)

    def sec(self, eng, fn):
        getattr(self.blk, eng)(fn)

    def fin(self):
        self.blk.__exit__(None, None, None)
        return self.nc

    def dump(self, items):
        if not self.dbg:
            return
        b = self
        def dmp(sync):
            b.w(sync, "V", "S", "P", "G")
            for nm, ap_, shp, dt in items:
                o = self.nc.dram_tensor("dbg_" + nm, list(shp), dt,
                                        kind="ExternalOutput").ap()
                b.inc(sync.dma_start(out=o, in_=ap_), "ST", 16)
        self.sec("sync", dmp)
        for e in ["vector", "scalar", "tensor", "gpsimd"]:
            def bar(eng):
                b.w(eng, "ST")
            self.sec(e, bar)

    def probe_xh2(self, tag):
        if not self.dbg:
            return
        b = self
        tmp = self.sb("pxh2_" + tag, [128, 4096], F32)
        def ldp(sync):
            b.w(sync, "V", "S", "P", "ST", "G")
            for g in range(8):
                b.inc(sync.dma_start(out=tmp[:, g * 512:(g + 1) * 512],
                                     in_=b.xh2[g * 128:(g + 1) * 128, 7680:8192]), "L", 16)
        self.sec("sync", ldp)
        self.dump([("PX_" + tag, tmp[:], [128, 4096], F32)])
        self.free("pxh2_" + tag)

    def rstd(self, s, rt, vt):
        s.activation(rt[:], vt[:], AF.Ln, bias=self.c_eps)
        return s.activation(rt[:], rt[:], AF.Exp, scale=self.c_mhalf)

    def prep_common(self):
        b = self
        self.ident_f = self.sb("ident_f", [128, 128], F32)
        self.ident_b = self.sb("ident_b", [128, 128], BF16)
        self.ones1b = self.sb("ones1b", [1, 128], BF16)
        self.ones1f = self.sb("ones1f", [1, 128], F32)
        self.ones128b = self.sb("ones128b", [128, 128], BF16)
        self.iota_p = self.sb("iota_p", [128, 128], F32)
        self.A_rep = self.sb("A_rep", [128, 16], F32)
        self.Dcol = self.sb("Dcol", [128, 8], F32)
        self.dtbcol = self.sb("dtbcol", [128, 8], F32)
        self.sel16 = self.sb("sel16", [16, 16 * 128], BF16)
        self.WxpT = self.sb("WxpT", [128, 8 * 96], BF16)
        self.WdtT = self.sb("WdtT", [64, 8 * 128], BF16)
        self.tmp_w = self.sb("tmp_w", [128, 1024], F32)
        self.arow = self.sb("arow", [1, 16], F32)
        cst = self.sb("cst", [128, 8], F32)
        self.c_eps = cst[:, 0:1]
        self.c_mhalf = cst[:, 1:2]
        self.c_invD = cst[:, 2:3]
        self.c_half = cst[:, 3:4]
        self.c_invDff = cst[:, 4:5]
        p_pr = self.ps("p_prep", [128, 512], F32)
        p_a = self.ps("p_a", [128, 16], F32)

        s16a = self.sb("s16a", [16, 16 * 128], F32)
        s16b = self.sb("s16b", [16, 16 * 128], F32)

        def g1(g):
            g.iota(self.iota_p[:], pattern=[[0, 128]], base=0, channel_multiplier=1,
                   allow_small_or_imprecise_dtypes=True)
            g.iota(self.ident_f[:], pattern=[[1, 128]], base=0, channel_multiplier=0,
                   allow_small_or_imprecise_dtypes=True)
            g.iota(s16a[:], pattern=[[1, 16], [0, 128]], base=0, channel_multiplier=0,
                   allow_small_or_imprecise_dtypes=True)
            b.inc(g.iota(s16b[:], pattern=[[0, 16], [0, 128]], base=0, channel_multiplier=1,
                         allow_small_or_imprecise_dtypes=True), "G")
        self.sec("gpsimd", g1)

        def s1(sync):
            b.inc(sync.dma_start(out=self.arow[:], in_=b.di["A_log"]), "L", 16)
            for g in range(8):
                b.inc(sync.dma_start(out=self.Dcol[:, g:g + 1],
                                     in_=b.di["Dv"][g:g + 1, :]), "L", 16)
                b.inc(sync.dma_start(out=self.dtbcol[:, g:g + 1],
                                     in_=b.di["dtb"][g:g + 1, :]), "L", 16)
            b.inc(sync.dma_start(out=self.tmp_w[0:96, :], in_=b.di["xpw"]), "L", 16)
        self.sec("sync", s1)

        def v1(v):
            b.w(v, "G", "L")
            v.tensor_tensor(s16a[:], s16a[:], s16b[:], op=ALU.is_equal)
            v.tensor_copy(self.sel16[:], s16a[:])
            v.memset(cst[:, 0:1], EPS)
            v.memset(cst[:, 1:2], -0.5)
            v.memset(cst[:, 2:3], 1.0 / D)
            v.memset(cst[:, 3:4], 0.5)
            v.memset(cst[:, 4:5], 1.0 / DFF)
            v.tensor_tensor(self.ident_f[:], self.iota_p[:], self.ident_f[:], op=ALU.is_equal)
            v.tensor_copy(self.ident_b[:], self.ident_f[:])
            v.memset(self.ones1b[:], 1.0)
            v.memset(self.ones1f[:], 1.0)
            b.inc(v.memset(self.ones128b[:], 1.0), "V")
        self.sec("vector", v1)

        def s2(s):
            b.w(s, "L", "V")
            b.inc(s.activation(self.arow[:], self.arow[:], AF.Exp), "S")
        self.sec("scalar", s2)
        def v2(v):
            b.w(v, "S")
            b.inc(v.tensor_scalar(self.arow[:], self.arow[:], -1.0, None, op0=ALU.mult), "V")
        self.sec("vector", v2)
        def t2(t):
            b.w(t, "V")
            b.inc(t.matmul(p_a[:], self.ones1f[:], self.arow[:], start=True, stop=True), "P")
        self.sec("tensor", t2)
        def v3(v):
            b.w(v, "P")
            b.inc(v.tensor_copy(self.A_rep[:], p_a[:]), "V")
        self.sec("vector", v3)

        for g in range(8):
            def tf(t, g=g):
                b.w(t, "V", "L", "S")
                b.inc(t.transpose(p_pr[:, 0:96], self.tmp_w[0:96, g * 128:(g + 1) * 128],
                                  self.ident_f[0:96, 0:96]), "P")
            self.sec("tensor", tf)
            def sf(s, g=g):
                b.w(s, "P")
                b.inc(s.copy(self.WxpT[:, g * 96:(g + 1) * 96], p_pr[:, 0:96]), "S")
            self.sec("scalar", sf)
        for g in range(8):
            def lf(sync, g=g):
                b.w(sync, "P")
                b.inc(sync.dma_start(out=self.tmp_w[:, 0:64],
                                     in_=b.di["dtw"][g * 128:(g + 1) * 128, :]), "L", 16)
            self.sec("sync", lf)
            def tf(t, g=g):
                b.w(t, "L", "S")
                b.inc(t.transpose(p_pr[0:64, 0:128], self.tmp_w[:, 0:64], self.ident_f[:]), "P")
            self.sec("tensor", tf)
            def sf(s, g=g):
                b.w(s, "P")
                b.inc(s.copy(self.WdtT[:, g * 128:(g + 1) * 128], p_pr[0:64, 0:128]), "S")
            self.sec("scalar", sf)
        self.free("p_prep", "p_a", "s16a", "s16b")

    def phase_A(self):
        b = self
        amax1 = self.sb("amax1", [128, 1], F32)
        u_tm = self.sb("u_tm", [128, 4096], F32)
        z0b_tm = self.sb("z0b_tm", [128, 4096], BF16)
        z0f = self.sb("z0f", [128, 4096], F32)
        z0bf = self.sb("z0bf", [128, 4096], BF16)
        r0 = self.sb("r0", [128, 4096], F32)
        delta = self.sb("delta", [128, 4096], F32)
        du = self.sb("du", [128, 4096], F32)
        hblk = self.sb("hblk", [128, 8 * 512], F32)
        yfm = self.sb("yfm", [128, 4096], F32)
        dA_t = self.sb("dA_t", [128, 2 * 512], F32)
        dBu_t = self.sb("dBu_t", [128, 2 * 512], F32)
        ytmp = self.sb("ytmp", [128, 2 * 512], F32)
        state = self.sb("state", [128, 128], F32)
        dtr_b = self.sb("dtr_b", [64, 512], BF16)
        B_b = self.sb("B_b", [16, 512], BF16)
        C_b = self.sb("C_b", [16, 512], BF16)
        mst = self.sb("mst", [128, 4], F32)
        vst = self.sb("vst", [128, 4], F32)
        rst = self.sb("rst", [128, 4], F32)
        nmr = self.sb("nmr", [128, 4], F32)
        sc4 = self.sb("sc4", [128, 4], F32)
        bn6 = self.sb("bn6", [128, 12], F32)
        bn2 = self.sb("bn2", [128, 2], F32)
        m1t = self.sb("m1t", [128, 512], F32)
        r1t = self.sb("r1t", [128, 512], F32)
        sc512 = self.sb("sc512", [128, 512], F32)
        z1t = self.sb("z1t", [128, 512], F32)

        p_trb = self.ps("pA_trb", [128, 512], BF16)
        p_misc = self.ps("pA_misc", [128, 512], F32)
        p_xp = self.ps("pA_xp", [80, 512], F32)
        p_xpc = self.ps("pA_xpc", [16, 512], F32)
        p_bb = self.ps("pA_bb", [128, 512], F32)
        p_cb = self.ps("pA_cb", [128, 512], F32)
        p_s1 = self.ps("pA_s1", [128, 512], F32)
        p_s2 = self.ps("pA_s2", [128, 512], F32)

        def vinit(v):
            v.memset(state[:], 0.0)
            b.inc(v.memset(amax1[:], 0.0), "V")
        self.sec("vector", vinit)

        for ch in range(NCH):
            t0 = ch * T

            def ld(sync, t0=t0):
                b.w(sync, "V", "P")
                for i in range(4):
                    b.inc(sync.dma_start(
                        out=u_tm[:, i * 1024:(i + 1) * 1024],
                        in_=b.di["u"][t0 + i * 128: t0 + (i + 1) * 128, :]), "L", 16)
            self.sec("sync", ld)

            def vln(v):
                b.w(v, "L")
                for i in range(4):
                    v.bn_stats(bn6[:, 0:6], u_tm[:, i * 1024:i * 1024 + 512])
                    b.inc(v.bn_stats(bn6[:, 6:12], u_tm[:, i * 1024 + 512:(i + 1) * 1024]), "V")
                    b.w(v, "V")
                    b.inc(v.bn_aggr(bn2[:], bn6[:]), "V")
                    b.w(v, "V")
                    v.tensor_copy(mst[:, i:i + 1], bn2[:, 0:1])
                    b.inc(v.tensor_copy(vst[:, i:i + 1], bn2[:, 1:2]), "V")
            self.sec("vector", vln)

            def sln(s):
                b.w(s, "V")
                b.inc(self.rstd(s, rst, vst), "S")
            self.sec("scalar", sln)

            def vln2(v):
                b.w(v, "S")
                b.inc(v.scalar_tensor_tensor(nmr[:], mst[:], -1.0, rst[:],
                                             op0=ALU.mult, op1=ALU.mult), "V")
            self.sec("vector", vln2)

            def sz0(s):
                b.w(s, "V")
                for i in range(4):
                    sl = slice(i * 1024, (i + 1) * 1024)
                    b.inc(s.activation(z0b_tm[:, sl], u_tm[:, sl], AF.Identity,
                                       bias=nmr[:, i:i + 1], scale=rst[:, i:i + 1]), "S")
            self.sec("scalar", sz0)

            for g in range(8):
                def ttr(t, g=g):
                    b.w(t, "S", "V")
                    for i in range(4):
                        b.inc(t.transpose(
                            p_trb[:, i * 128:(i + 1) * 128],
                            z0b_tm[:, i * 1024 + g * 128: i * 1024 + (g + 1) * 128],
                            self.ident_b[:]), "P")
                self.sec("tensor", ttr)
                def sev(s, g=g):
                    b.w(s, "P")
                    s.copy(z0f[:, g * 512:(g + 1) * 512], p_trb[:])
                    b.inc(s.copy(z0bf[:, g * 512:(g + 1) * 512], p_trb[:]), "S")
                self.sec("scalar", sev)
                def ttr2(t, g=g):
                    b.w(t, "S")
                    for i in range(4):
                        b.inc(t.transpose(
                            p_misc[:, i * 128:(i + 1) * 128],
                            u_tm[:, i * 1024 + g * 128: i * 1024 + (g + 1) * 128],
                            self.ident_f[:]), "P")
                self.sec("tensor", ttr2)
                def vr0(v, g=g):
                    b.w(v, "P", "S", "ST")
                    b.inc(v.scalar_tensor_tensor(r0[:, g * 512:(g + 1) * 512],
                                                 z0f[:, g * 512:(g + 1) * 512],
                                                 self.Dcol[:, g:g + 1], p_misc[:],
                                                 op0=ALU.mult, op1=ALU.add), "V")
                self.sec("vector", vr0)

            def txp(t):
                b.w(t, "S", "V")
                for g in range(8):
                    b.inc(t.matmul(p_xp[:], self.WxpT[:, g * 96:g * 96 + 80],
                                   z0bf[:, g * 512:(g + 1) * 512],
                                   start=(g == 0), stop=(g == 7)), "P")
                for g in range(8):
                    b.inc(t.matmul(p_xpc[:], self.WxpT[:, g * 96 + 80:(g + 1) * 96],
                                   z0bf[:, g * 512:(g + 1) * 512],
                                   start=(g == 0), stop=(g == 7)), "P")
            self.sec("tensor", txp)
            def sxp(s):
                b.w(s, "P")
                b.inc(s.copy(dtr_b[:], p_xp[0:64, :]), "S")
            self.sec("scalar", sxp)
            def vxp(v):
                b.w(v, "P")
                v.tensor_copy(B_b[:], p_xp[64:80, :])
                b.inc(v.tensor_copy(C_b[:], p_xpc[0:16, :]), "V")
            self.sec("vector", vxp)

            for g in range(8):
                def tdl(t, g=g):
                    b.w(t, "S", "V")
                    b.inc(t.matmul(p_misc[:], self.WdtT[:, g * 128:(g + 1) * 128],
                                   dtr_b[:], start=True, stop=True), "P")
                self.sec("tensor", tdl)
                def sdl(s, g=g):
                    b.w(s, "P")
                    sl = slice(g * 512, (g + 1) * 512)
                    s.activation(du[:, sl], p_misc[:], AF.Abs, bias=self.dtbcol[:, g:g + 1])
                    s.activation(du[:, sl], du[:, sl], AF.Exp, scale=-1.0)
                    s.activation(du[:, sl], du[:, sl], AF.Ln, bias=1.0)
                    b.inc(s.activation(delta[:, sl], p_misc[:], AF.Relu,
                                       bias=self.dtbcol[:, g:g + 1]), "S")
                self.sec("scalar", sdl)
                def vdl(v, g=g):
                    b.w(v, "S")
                    sl = slice(g * 512, (g + 1) * 512)
                    v.tensor_tensor(delta[:, sl], delta[:, sl], du[:, sl], op=ALU.add)
                    b.inc(v.tensor_tensor(du[:, sl], delta[:, sl], z0f[:, sl],
                                          op=ALU.mult), "V")
                self.sec("vector", vdl)

            Vbase = self.c["V"]
            Sfull = self.c["S"]
            vdone = []
            for n in range(N16):
                def tbc(t, n=n):
                    if n == 0:
                        b.w(t, "V", "S")
                    else:
                        b.wv(t, "V", vdone[n * 8 - 1])
                    b.inc(t.matmul(p_bb[:], self.sel16[:, n * 128:(n + 1) * 128],
                                   B_b[:], start=True, stop=True), "P")
                    b.inc(t.matmul(p_cb[:], self.sel16[:, n * 128:(n + 1) * 128],
                                   C_b[:], start=True, stop=True), "P")
                self.sec("tensor", tbc)
                Pn = self.c["P"]
                for g in range(8):
                    k = n * 8 + g
                    par = k % 2
                    def sda(s, n=n, g=g, k=k, par=par):
                        if k >= 2:
                            b.wv(s, "V", vdone[k - 2])
                        else:
                            b.wv(s, "V", Vbase)
                        b.inc(s.activation(dA_t[:, par * 512:(par + 1) * 512],
                                           delta[:, g * 512:(g + 1) * 512],
                                           AF.Exp, scale=self.A_rep[:, n:n + 1]), "S")
                    self.sec("scalar", sda)
                    Sk = self.c["S"]
                    def vsc(v, n=n, g=g, k=k, par=par, Sk=Sk, Pn=Pn):
                        b.wv(v, "S", Sk)
                        b.wv(v, "P", Pn)
                        gsl = slice(g * 512, (g + 1) * 512)
                        v.tensor_tensor(dBu_t[:, par * 512:(par + 1) * 512],
                                        du[:, gsl], p_bb[:], op=ALU.mult)
                        v.tensor_tensor_scan(hblk[:, gsl],
                                             dA_t[:, par * 512:(par + 1) * 512],
                                             dBu_t[:, par * 512:(par + 1) * 512],
                                             state[:, g * 16 + n:g * 16 + n + 1],
                                             op0=ALU.mult, op1=ALU.add)
                        v.tensor_tensor(ytmp[:, par * 512:(par + 1) * 512],
                                        hblk[:, gsl], p_cb[:], op=ALU.mult)
                        if n == 0:
                            b.inc(v.tensor_copy(yfm[:, gsl],
                                                ytmp[:, par * 512:(par + 1) * 512]), "V")
                        else:
                            b.inc(v.tensor_tensor(yfm[:, gsl], yfm[:, gsl],
                                                  ytmp[:, par * 512:(par + 1) * 512],
                                                  op=ALU.add), "V")
                    self.sec("vector", vsc)
                    vdone.append(self.c["V"])
                def vst_up(v, n=n):
                    b.inc(v.tensor_copy(state[0:128, n:n + 113:16],
                                        hblk[0:128, 511:4096:512]), "V")
                self.sec("vector", vst_up)

            for g in range(8):
                def vx1(v, g=g):
                    gsl = slice(g * 512, (g + 1) * 512)
                    b.inc(v.tensor_tensor(r0[:, gsl], r0[:, gsl], yfm[:, gsl],
                                          op=ALU.add), "V")
                self.sec("vector", vx1)
            def sst(s):
                b.w(s, "V")
                for g in range(8):
                    gsl = slice(g * 512, (g + 1) * 512)
                    s.copy(z0bf[:, gsl], r0[:, gsl])
                    b.inc(s.activation(z0b_tm[:, gsl], r0[:, gsl], AF.Square), "S")
            self.sec("scalar", sst)
            def tst(t):
                b.w(t, "S")
                for g in range(8):
                    gsl = slice(g * 512, (g + 1) * 512)
                    b.inc(t.matmul(p_s1[:], self.ones128b[:], z0bf[:, gsl],
                                   start=(g == 0), stop=(g == 7)), "P")
                    b.inc(t.matmul(p_s2[:], self.ones128b[:], z0b_tm[:, gsl],
                                   start=(g == 0), stop=(g == 7)), "P")
            self.sec("tensor", tst)
            def sm1(s):
                b.w(s, "P", "V", "ST")
                s.mul(m1t[:], p_s1[:], self.c_invD)
                b.inc(s.mul(sc512[:], p_s2[:], self.c_invD), "S")
            self.sec("scalar", sm1)
            def vv1(v):
                b.w(v, "S")
                v.tensor_tensor(z1t[:], m1t[:], m1t[:], op=ALU.mult)
                b.inc(v.tensor_tensor(sc512[:], sc512[:], z1t[:], op=ALU.subtract), "V")
            self.sec("vector", vv1)
            def sr1(s):
                b.w(s, "V", "ST")
                b.inc(self.rstd(s, r1t, sc512), "S")
            self.sec("scalar", sr1)
            for g in range(8):
                def vam(v, g=g):
                    if g == 0:
                        b.w(v, "S")
                    gsl = slice(g * 512, (g + 1) * 512)
                    v.tensor_tensor(z1t[:], r0[:, gsl], m1t[:], op=ALU.subtract)
                    v.tensor_tensor(z1t[:], z1t[:], r1t[:], op=ALU.mult)
                    b.inc(v.reduce_max(sc4[:, 0:1], z1t[:], axis=AX.X,
                                       apply_absolute_value=True), "V")
                    b.w(v, "V")
                    b.inc(v.tensor_tensor(amax1[:], amax1[:], sc4[:, 0:1], op=ALU.max), "V")
                self.sec("vector", vam)
            def st1(sync, t0=t0):
                b.w(sync, "V")
                for g in range(8):
                    gsl = slice(g * 512, (g + 1) * 512)
                    b.inc(sync.dma_start(out=b.xh1[g * 128:(g + 1) * 128, t0:t0 + T],
                                         in_=r0[:, gsl]), "ST", 16)
                b.inc(sync.dma_start(out=b.sth[0:1, t0:t0 + T], in_=m1t[0:1, :]), "ST", 16)
                b.inc(sync.dma_start(out=b.sth[1:2, t0:t0 + T], in_=r1t[0:1, :]), "ST", 16)
            self.sec("sync", st1)

            if self.dbg and ch == 0:
                dumps = [("z0f", z0f, F32), ("delta", delta, F32), ("du", du, F32),
                         ("yfm", yfm, F32), ("x1", r0, F32), ("hblk", hblk, F32),
                         ("u_tm", u_tm, F32)]
                small = [("z0b_tm", z0b_tm, [128, 4096]), ("rst", rst, [128, 4]),
                         ("nmr", nmr, [128, 4]), ("z0bf", z0bf, [128, 4096]),
                         ("A_rep", self.A_rep, [128, 16]), ("B_b", B_b, [16, 512]),
                         ("C_b", C_b, [16, 512]), ("dtr_b", dtr_b, [64, 512]),
                         ("m1t", m1t, [128, 512]), ("r1t", r1t, [128, 512]),
                         ("state", state, [128, 128])]
                def dmp(sync):
                    b.w(sync, "V", "S", "P")
                    for nm, tens, dt in dumps:
                        ap = self.nc.dram_tensor("dbg_" + nm, [128, 4096], dt,
                                                 kind="ExternalOutput").ap()
                        b.inc(sync.dma_start(out=ap, in_=tens[:]), "ST", 16)
                    for nm, tens, shp in small:
                        ap = self.nc.dram_tensor("dbg_" + nm, list(shp),
                                                 BF16 if tens.dtype == BF16 else F32,
                                                 kind="ExternalOutput").ap()
                        b.inc(sync.dma_start(out=ap, in_=tens[:]), "ST", 16)
                self.sec("sync", dmp)
                for e in ["vector", "scalar", "tensor", "gpsimd"]:
                    def bar(eng):
                        b.w(eng, "ST")
                    self.sec(e, bar)

        self._amax1 = amax1
        self.free("u_tm", "z0b_tm", "z0f", "z0bf", "r0", "delta", "du", "hblk",
                  "yfm", "dA_t", "dBu_t", "ytmp", "state", "dtr_b", "B_b", "C_b",
                  "mst", "vst", "rst", "nmr", "sc4", "bn6", "bn2",
                  "m1t", "r1t", "sc512", "z1t",
                  "pA_trb", "pA_misc", "pA_xp", "pA_xpc", "pA_bb", "pA_cb", "pA_s1", "pA_s2")


    def allreduce_max_scalar(self, amax_col, cin, cout, sxs_col, inv_col, tag):
        b = self
        p_sc = self.ps("p_sc" + tag, [128, 128], F32)
        sc_row = self.sb("sc_row" + tag, [1, 128], F32)
        sc_one = self.sb("sc_one" + tag, [1, 2], F32)

        sc_sq = self.sb("sc_sq" + tag, [128, 128], F32)
        def s0(s):
            b.w(s, "V", "P", "ST")
            b.inc(s.activation(sc_sq[:], self.ident_f[:], AF.Identity,
                               bias=amax_col[:], scale=0.0), "S")
        self.sec("scalar", s0)
        def t1(t):
            b.w(t, "S")
            b.inc(t.transpose(p_sc[:], sc_sq[:], self.ident_f[:]), "P")
        self.sec("tensor", t1)
        def v1(v):
            b.w(v, "P")
            b.inc(v.reduce_max(sc_one[:, 0:1], p_sc[0:1, 0:128], axis=AX.X), "V")
            b.w(v, "V")
        self.sec("vector", v1)
        def g1(g):
            b.w(g, "V")
            b.inc(g.dma_start(out=cin.ap(), in_=sc_one[:, 0:1]), "CC", 16)
            b.w(g, "CC")
            b.inc(g.collective_compute(
                "AllReduce", ALU.max, replica_groups=[list(range(8))],
                ins=[cin.ap()], outs=[cout.ap()]), "CC", 1)
            b.w(g, "CC")
            b.inc(g.dma_start(out=sc_one[:, 1:2], in_=cout.ap()), "CC", 16)
        self.sec("gpsimd", g1)
        def v2(v):
            b.w(v, "CC")
            b.inc(v.reciprocal(sc_one[:, 0:1], sc_one[:, 1:2]), "V")
            b.w(v, "V")
            b.inc(v.tensor_scalar(sc_one[:, 0:1], sc_one[:, 0:1], 32767.0, None,
                                  op0=ALU.mult), "V")
            b.w(v, "V")
            b.inc(v.reciprocal(sc_one[:, 1:2], sc_one[:, 0:1]), "V")
            b.w(v, "V")
        self.sec("vector", v2)
        def t2(t):
            b.w(t, "V")
            b.inc(t.matmul(p_sc[:, 0:2], self.ones1f[:], sc_one[:], start=True,
                           stop=True), "P")
        self.sec("tensor", t2)
        def v3(v):
            b.w(v, "P")
            v.tensor_copy(sxs_col[:], p_sc[:, 0:1])
            b.inc(v.tensor_copy(inv_col[:], p_sc[:, 1:2]), "V")
        self.sec("vector", v3)
        self.free("p_sc" + tag, "sc_row" + tag, "sc_one" + tag, "sc_sq" + tag)

    def phase_B(self):
        b = self
        sx_col = self.sb("sx_col", [128, 1], F32)
        isx_col = self.sb("isx_col", [128, 1], F32)
        sh_col = self.sb("sh_col", [128, 1], F32)
        ish_col = self.sb("ish_col", [128, 1], F32)
        amax2 = self.sb("amax2", [128, 1], F32)
        bin_col = self.sb("bin_col", [128, 32], F32)
        bout_col = self.sb("bout_col", [128, 8], F32)

        self.allreduce_max_scalar(self._amax1, self.cc_in, self.cc_out,
                                  sx_col, isx_col, "1")

        def vz(v):
            b.w(v, "ST", "S")
            b.inc(v.memset(amax2[:], 0.0), "V")
        self.sec("vector", vz)
        def ldb(sync):
            b.w(sync, "ST", "V", "S", "P")
            for j in range(32):
                b.inc(sync.dma_start(out=bin_col[:, j:j + 1],
                                     in_=b.di["bin"][j:j + 1, :]), "L", 16)
            for j in range(8):
                b.inc(sync.dma_start(out=bout_col[:, j:j + 1],
                                     in_=b.di["bout"][j:j + 1, :]), "L", 16)
        self.sec("sync", ldb)

        W_outq = self.sb("W_outq", [128, 32 * 1024], BF16)
        wsrc = self.sb("wsrcB", [128, 1024], F32)
        wq32 = self.sb("wq32", [128, 1024], F32)
        am_c = self.sb("am_cB", [128, 1], F32)
        scl_c = self.sb("scl_cB", [128, 1], F32)
        inv_c = self.sb("inv_cB", [128, 1], F32)
        for et in range(32):
            def lw(sync, et=et):
                b.w(sync, "S", "V")
                if et == 0:
                    b.w(sync, "ST", "P")
                b.inc(sync.dma_start(out=wsrc[:],
                                     in_=b.di["Wout"][et * 128:(et + 1) * 128, :]), "L", 16)
            self.sec("sync", lw)
            def vq(v, et=et):
                b.w(v, "L")
                b.inc(v.reduce_max(am_c[:], wsrc[:], axis=AX.X,
                                   apply_absolute_value=True), "V")
                b.w(v, "V")
                b.inc(v.reciprocal(scl_c[:], am_c[:]), "V")
                b.w(v, "V")
                b.inc(v.tensor_scalar(scl_c[:], scl_c[:], 127.0, None, op0=ALU.mult), "V")
                b.w(v, "V")
                b.inc(v.reciprocal(inv_c[:], scl_c[:]), "V")
                b.w(v, "V")
            self.sec("vector", vq)
            def sq(s, et=et):
                b.w(s, "V")
                b.inc(s.activation(wq32[:], wsrc[:], AF.Identity, scale=scl_c[:]), "S")
            self.sec("scalar", sq)
            def vr(v, et=et):
                b.w(v, "S")
                b.inc(v.tensor_scalar(wq32[:], wq32[:], MAGIC, MAGIC,
                                      op0=ALU.add, op1=ALU.subtract), "V")
            self.sec("vector", vr)
            def sq2(s, et=et):
                b.w(s, "V")
                b.inc(s.activation(W_outq[:, et * 1024:(et + 1) * 1024], wq32[:],
                                   AF.Identity, scale=inv_c[:]), "S")
            self.sec("scalar", sq2)

        W_inq = self.sb("W_inq", [128, 8 * 4096], BF16)
        weT = self.sb("weT", [128, 1024], F32)
        weTq = self.sb("weTq", [128, 1024], BF16)
        p_tw = self.ps("p_twB", [128, 512], F32)
        p_twb = self.ps("p_twbB", [128, 512], BF16)
        for ec in range(32):
            def lw(sync, ec=ec):
                b.w(sync, "P")
                if ec == 0:
                    b.w(sync, "S", "V")
                for g in range(8):
                    b.inc(sync.dma_start(
                        out=wsrc[:, g * 128:(g + 1) * 128],
                        in_=b.di["Win"][g * 128:(g + 1) * 128, ec * 128:(ec + 1) * 128]),
                        "L", 16)
            self.sec("sync", lw)
            for h in range(2):
                def tt(t, ec=ec, h=h):
                    b.w(t, "L", "S")
                    for j in range(4):
                        g = h * 4 + j
                        b.inc(t.transpose(p_tw[:, j * 128:(j + 1) * 128],
                                          wsrc[:, g * 128:(g + 1) * 128],
                                          self.ident_f[:]), "P")
                self.sec("tensor", tt)
                def se(s, ec=ec, h=h):
                    b.w(s, "P")
                    b.inc(s.copy(weT[:, h * 512:(h + 1) * 512], p_tw[:]), "S")
                self.sec("scalar", se)
            def vq(v, ec=ec):
                b.w(v, "S")
                b.inc(v.reduce_max(am_c[:], weT[:], axis=AX.X,
                                   apply_absolute_value=True), "V")
                b.w(v, "V")
                b.inc(v.reciprocal(scl_c[:], am_c[:]), "V")
                b.w(v, "V")
                b.inc(v.tensor_scalar(scl_c[:], scl_c[:], 127.0, None, op0=ALU.mult), "V")
                b.w(v, "V")
                b.inc(v.reciprocal(inv_c[:], scl_c[:]), "V")
                b.w(v, "V")
            self.sec("vector", vq)
            def sq(s, ec=ec):
                b.w(s, "V")
                b.inc(s.activation(wq32[:], weT[:], AF.Identity, scale=scl_c[:]), "S")
            self.sec("scalar", sq)
            def vr(v, ec=ec):
                b.w(v, "S")
                b.inc(v.tensor_scalar(wq32[:], wq32[:], MAGIC, MAGIC,
                                      op0=ALU.add, op1=ALU.subtract), "V")
            self.sec("vector", vr)
            def sq2(s, ec=ec):
                b.w(s, "V")
                b.inc(s.activation(weTq[:], wq32[:], AF.Identity, scale=inv_c[:]), "S")
            self.sec("scalar", sq2)
            for h in range(2):
                def tt2(t, ec=ec, h=h):
                    b.w(t, "S")
                    for j in range(4):
                        g = h * 4 + j
                        b.inc(t.transpose(p_twb[:, j * 128:(j + 1) * 128],
                                          weTq[:, g * 128:(g + 1) * 128],
                                          self.ident_b[:]), "P")
                self.sec("tensor", tt2)
                def se2(s, ec=ec, h=h):
                    b.w(s, "P")
                    for j in range(4):
                        g = h * 4 + j
                        b.inc(s.copy(W_inq[:, g * 4096 + ec * 128:g * 4096 + (ec + 1) * 128],
                                     p_twb[:, j * 128:(j + 1) * 128]), "S")
                self.sec("scalar", se2)

        xf = self.sb("xfB", [128, 4096], F32)
        zq = self.sb("zqB", [128, 4096], BF16)
        hb = self.sb("hbB", [128, 512], BF16)
        mr0 = self.sb("mr0B", [1, 512], F32)
        rr0 = self.sb("rr0B", [1, 512], F32)
        am_t = self.sb("am_tB", [128, 1], F32)
        p_bc1 = self.ps("p_bc1B", [128, 512], F32)
        p_bc2 = self.ps("p_bc2B", [128, 512], F32)
        p_h = [self.ps("p_hB0", [128, 512], F32), self.ps("p_hB1", [128, 512], F32)]

        for ch in range(NCH):
            t0 = ch * T
            def ld(sync, t0=t0):
                b.w(sync, "V", "S", "ST")
                for g in range(8):
                    b.inc(sync.dma_start(out=xf[:, g * 512:(g + 1) * 512],
                                         in_=b.xh1[g * 128:(g + 1) * 128, t0:t0 + T]),
                          "L", 16)
                b.inc(sync.dma_start(out=mr0[:], in_=b.sth[0:1, t0:t0 + T]), "L", 16)
                b.inc(sync.dma_start(out=rr0[:], in_=b.sth[1:2, t0:t0 + T]), "L", 16)
            self.sec("sync", ld)
            def tb(t, t0=t0):
                b.w(t, "L", "V")
                b.inc(t.matmul(p_bc1[:], self.ones1f[:], mr0[:], start=True, stop=True), "P")
                b.inc(t.matmul(p_bc2[:], self.ones1f[:], rr0[:], start=True, stop=True), "P")
            self.sec("tensor", tb)
            def sb_(s):
                b.w(s, "P")
                b.inc(s.activation(wq32[:, 0:512], p_bc2[:], AF.Identity,
                                   scale=sx_col[:]), "S")
            self.sec("scalar", sb_)
            def vz1(v):
                b.w(v, "S")
                for g in range(8):
                    gsl = slice(g * 512, (g + 1) * 512)
                    v.tensor_tensor(xf[:, gsl], xf[:, gsl], p_bc1[:], op=ALU.subtract)
                    v.tensor_tensor(xf[:, gsl], xf[:, gsl], wq32[:, 0:512], op=ALU.mult)
                    b.inc(v.tensor_scalar(xf[:, gsl], xf[:, gsl], MAGIC, MAGIC,
                                          op0=ALU.add, op1=ALU.subtract), "V")
            self.sec("vector", vz1)
            def sz1(s):
                b.w(s, "V")
                for g in range(8):
                    gsl = slice(g * 512, (g + 1) * 512)
                    b.inc(s.activation(zq[:, gsl], xf[:, gsl], AF.Identity,
                                       scale=isx_col[:]), "S")
            self.sec("scalar", sz1)
            for ec in range(32):
                pp = p_h[ec % 2]
                def tm(t, ec=ec, pp=pp):
                    b.w(t, "S")
                    if ec >= 2:
                        b.w(t, "ST")
                    for g in range(8):
                        b.inc(t.matmul(pp[:], W_inq[:, g * 4096 + ec * 128:
                                                    g * 4096 + (ec + 1) * 128],
                                       zq[:, g * 512:(g + 1) * 512],
                                       start=(g == 0), stop=(g == 7)), "P")
                self.sec("tensor", tm)
                def sg(s, ec=ec, pp=pp):
                    b.w(s, "P", "ST")
                    b.inc(s.activation(hb[:], pp[:], AF.Gelu,
                                       bias=bin_col[:, ec:ec + 1]), "S")
                self.sec("scalar", sg)
                def va(v, ec=ec):
                    b.w(v, "S")
                    b.inc(v.reduce_max(am_t[:], hb[:], axis=AX.X,
                                       apply_absolute_value=True), "V")
                    b.w(v, "V")
                    b.inc(v.tensor_tensor(amax2[:], amax2[:], am_t[:], op=ALU.max), "V")
                self.sec("vector", va)
                def st(sync, ec=ec, t0=t0):
                    b.w(sync, "V")
                    b.inc(sync.dma_start(out=b.hh[ec * 128:(ec + 1) * 128, t0:t0 + T],
                                         in_=hb[:]), "ST", 16)
                self.sec("sync", st)
            if self.dbg and ch == 0:
                def dmpB(sync):
                    b.w(sync, "V", "S", "P")
                    for nm, tens_ap, shp, dt in [
                            ("sx_col", sx_col[:], [128, 1], F32),
                            ("isx_col", isx_col[:], [128, 1], F32),
                            ("zq", zq[:], [128, 4096], BF16),
                            ("xfq", xf[:], [128, 4096], F32),
                            ("hb31", hb[:], [128, 512], BF16),
                            ("winq", W_inq[:, 0:4096], [128, 4096], BF16),
                            ("amax2c", amax2[:], [128, 1], F32)]:
                        ap = self.nc.dram_tensor("dbgB_" + nm, shp, dt,
                                                 kind="ExternalOutput").ap()
                        b.inc(sync.dma_start(out=ap, in_=tens_ap), "ST", 16)
                self.sec("sync", dmpB)
                for e in ["vector", "scalar", "tensor", "gpsimd"]:
                    def bar(eng):
                        b.w(eng, "ST")
                    self.sec(e, bar)

        self.allreduce_max_scalar(amax2, self.cc_in2, self.cc_out2,
                                  sh_col, ish_col, "2")
        self.free("p_bc1B", "p_bc2B", "p_hB0", "p_hB1")
        self.free("xfB", "zqB", "hbB", "mr0B", "rr0B", "am_tB")
        self.free("W_inq", "weT", "weTq", "p_twB", "p_twbB")

        hq = self.sb("hqB", [128, 32 * 512], BF16)
        u1 = self.sb("u1B", [128, 16 * 512], F32)
        xf1 = self.sb("xf1B", [128, 4096], F32)
        x2 = self.sb("x2B", [128, 4096], F32)
        x2b = self.sb("x2bB", [128, 4096], BF16)
        x2sq = self.sb("x2sqB", [128, 4096], BF16)
        m2t = self.sb("m2tB", [128, 512], F32)
        r2t = self.sb("r2tB", [128, 512], F32)
        s2c = self.sb("s2cB", [128, 512], F32)
        p_o = [self.ps("p_oB0", [128, 512], F32), self.ps("p_oB1", [128, 512], F32)]
        p_s1 = self.ps("p_s1B", [128, 512], F32)
        p_s2 = self.ps("p_s2B", [128, 512], F32)
        hh_r = b.hh.rearrange("(eb p) s -> p eb s", p=128)

        for ch in range(NCH):
            t0 = ch * T
            def ld(sync, t0=t0):
                b.w(sync, "V", "S", "P", "ST")
                b.inc(sync.dma_start(out=hq.rearrange("p (eb t) -> p eb t", t=T),
                                     in_=hh_r[:, :, t0:t0 + T]), "L", 16)
                for g in range(8):
                    b.inc(sync.dma_start(out=xf1[:, g * 512:(g + 1) * 512],
                                         in_=b.xh1[g * 128:(g + 1) * 128, t0:t0 + T]),
                          "L", 16)
            self.sec("sync", ld)
            for hf in range(2):
                def vq(v, hf=hf):
                    b.w(v, "L") if hf == 0 else None
                    b.w(v, "S")
                    b.inc(v.tensor_scalar(u1[:], hq[:, hf * 8192:(hf + 1) * 8192],
                                          sh_col[:], MAGIC, op0=ALU.mult, op1=ALU.add),
                          "V")
                self.sec("vector", vq)
                def sq(s, hf=hf):
                    b.w(s, "V")
                    pass
                self.sec("scalar", sq)
                def vq2(v, hf=hf):
                    b.inc(v.tensor_scalar(hq[:, hf * 8192:(hf + 1) * 8192], u1[:],
                                          MAGIC, ish_col[:], op0=ALU.subtract,
                                          op1=ALU.mult), "V")
                self.sec("vector", vq2)
            for fc in range(8):
                pp = p_o[fc % 2]
                def tm(t, fc=fc, pp=pp):
                    b.w(t, "V")
                    if fc >= 2:
                        b.w(t, "S")
                    for et in range(32):
                        b.inc(t.matmul(pp[:], W_outq[:, et * 1024 + fc * 128:
                                                     et * 1024 + (fc + 1) * 128],
                                       hq[:, et * 512:(et + 1) * 512],
                                       start=(et == 0), stop=(et == 31)), "P")
                self.sec("tensor", tm)
                def so(s, fc=fc, pp=pp):
                    b.w(s, "P", "ST")
                    b.inc(s.activation(x2[:, fc * 512:(fc + 1) * 512], pp[:],
                                       AF.Identity, bias=bout_col[:, fc:fc + 1]), "S")
                self.sec("scalar", so)
                def vo(v, fc=fc):
                    b.w(v, "S")
                    b.inc(v.tensor_tensor(x2[:, fc * 512:(fc + 1) * 512],
                                          x2[:, fc * 512:(fc + 1) * 512],
                                          xf1[:, fc * 512:(fc + 1) * 512],
                                          op=ALU.add), "V")
                self.sec("vector", vo)
            def sst(s):
                b.w(s, "V", "ST")
                for g in range(8):
                    gsl = slice(g * 512, (g + 1) * 512)
                    s.copy(x2b[:, gsl], x2[:, gsl])
                    b.inc(s.activation(x2sq[:, gsl], x2[:, gsl], AF.Square), "S")
            self.sec("scalar", sst)
            def tst(t):
                b.w(t, "S")
                for g in range(8):
                    gsl = slice(g * 512, (g + 1) * 512)
                    b.inc(t.matmul(p_s1[:], self.ones128b[:], x2b[:, gsl],
                                   start=(g == 0), stop=(g == 7)), "P")
                    b.inc(t.matmul(p_s2[:], self.ones128b[:], x2sq[:, gsl],
                                   start=(g == 0), stop=(g == 7)), "P")
            self.sec("tensor", tst)
            def sm2(s):
                b.w(s, "P", "ST")
                s.mul(m2t[:], p_s1[:], self.c_invD)
                b.inc(s.mul(s2c[:], p_s2[:], self.c_invD), "S")
            self.sec("scalar", sm2)
            def vv2(v):
                b.w(v, "S")
                v.tensor_tensor(r2t[:], m2t[:], m2t[:], op=ALU.mult)
                b.inc(v.tensor_tensor(s2c[:], s2c[:], r2t[:], op=ALU.subtract), "V")
            self.sec("vector", vv2)
            def sr2(s):
                b.w(s, "V", "ST")
                b.inc(self.rstd(s, r2t, s2c), "S")
            self.sec("scalar", sr2)
            def st2(sync, t0=t0):
                b.w(sync, "V", "S")
                for g in range(8):
                    gsl = slice(g * 512, (g + 1) * 512)
                    b.inc(sync.dma_start(out=b.xh2[g * 128:(g + 1) * 128, t0:t0 + T],
                                         in_=x2[:, gsl]), "ST", 16)
                b.inc(sync.dma_start(out=b.sth[2:3, t0:t0 + T], in_=m2t[0:1, :]), "ST", 16)
                b.inc(sync.dma_start(out=b.sth[3:4, t0:t0 + T], in_=r2t[0:1, :]), "ST", 16)
            self.sec("sync", st2)
            if self.dbg and ch == 0:
                def dmpB2(sync):
                    b.w(sync, "V", "S", "P")
                    for nm, tens_ap, shp, dt in [
                            ("sh_col", sh_col[:], [128, 1], F32),
                            ("hq0", hq[:], [128, 16384], BF16),
                            ("x2", x2[:], [128, 4096], F32),
                            ("m2t", m2t[:], [128, 512], F32),
                            ("woutq", W_outq[:, 0:4096], [128, 4096], BF16),
                            ("woutqf", W_outq[:], [128, 32768], BF16),
                            ("xf1", xf1[:], [128, 4096], F32),
                            ("r2t", r2t[:], [128, 512], F32)]:
                        ap = self.nc.dram_tensor("dbgB2_" + nm, shp, dt,
                                                 kind="ExternalOutput").ap()
                        b.inc(sync.dma_start(out=ap, in_=tens_ap), "ST", 16)
                self.sec("sync", dmpB2)
                for e in ["vector", "scalar", "tensor", "gpsimd"]:
                    def bar(eng):
                        b.w(eng, "ST")
                    self.sec(e, bar)

        self.free("p_oB0", "p_oB1", "p_s1B", "p_s2B")
        self.free("hqB", "u1B", "xf1B", "x2B", "x2bB", "x2sqB", "m2tB", "r2tB", "s2cB")
        self.free("W_outq", "wsrcB", "wq32", "am_cB", "scl_cB", "inv_cB")
        self.free("sx_col", "isx_col", "sh_col", "ish_col", "amax2", "bin_col",
                  "bout_col")


    def phase_C(self):
        b = self
        self.probe_xh2("postB")
        cmax = self.sb("cmaxC", [128, 8], F32)
        ncmax = self.sb("ncmaxC", [128, 8], F32)
        csum = self.sb("csumC", [128, 8], F32)
        rcs = self.sb("rcsC", [128, 8], F32)
        bo_col = self.sb("bo_colC", [128, 8], F32)
        gam_col = self.sb("gam_colC", [128, 8], F32)
        bet_col = self.sb("bet_colC", [128, 8], F32)
        red1 = self.sb("red1C", [128, 1], F32)
        ctxb = self.sb("ctxbC", [128, 8 * 1024], BF16)

        def vz(v):
            b.w(v, "ST", "S")
            v.memset(cmax[:], -1e30)
            b.inc(v.memset(csum[:], 0.0), "V")
        self.sec("vector", vz)
        def ldc(sync):
            b.w(sync, "ST", "V", "S", "P")
            for j in range(8):
                b.inc(sync.dma_start(out=bo_col[:, j:j + 1], in_=b.di["bo"][j:j + 1, :]), "L", 16)
                b.inc(sync.dma_start(out=gam_col[:, j:j + 1], in_=b.di["gam"][j:j + 1, :]), "L", 16)
                b.inc(sync.dma_start(out=bet_col[:, j:j + 1], in_=b.di["bet"][j:j + 1, :]), "L", 16)
        self.sec("sync", ldc)

        Wqb = self.sb("WqbC", [128, 8 * 1024], BF16)
        Wkb = self.sb("WkbC", [128, 8 * 1024], BF16)
        xf = self.sb("xfC", [128, 4096], F32)
        z2b = self.sb("z2bC", [128, 4096], BF16)
        kt = self.sb("ktC", [128, 512], F32)
        qtm = self.sb("qtmC", [128, 1024], F32)
        qnb = self.sb("qnbC", [128, 1024], BF16)
        qfm = self.sb("qfmC", [128, 4096], BF16)
        mrow = self.sb("mrowC", [1, 512], F32)
        rrow = self.sb("rrowC", [1, 512], F32)
        nmx = self.sb("nmxC", [128, 1], F32)
        ssum = self.sb("ssumC", [128, 1], F32)
        p_bc1 = self.ps("p_bc1C", [128, 512], F32)
        p_bc2 = self.ps("p_bc2C", [128, 512], F32)
        p_k = self.ps("p_kC", [128, 512], F32)
        p_q = [self.ps("p_qC0", [128, 512], F32), self.ps("p_qC1", [128, 512], F32)]
        p_trb = self.ps("p_trbC", [128, 512], BF16)

        def ldw(g):
            b.w(g, "V", "S", "P", "ST")
            for i in range(8):
                b.inc(g.dma_start(out=Wqb[:, i * 1024:(i + 1) * 1024],
                                  in_=b.di["Wq"][i * 128:(i + 1) * 128, :]), "G", 16)
                b.inc(g.dma_start(out=Wkb[:, i * 1024:(i + 1) * 1024],
                                  in_=b.di["Wk"][i * 128:(i + 1) * 128, :]), "G", 16)
        self.sec("gpsimd", ldw)

        for ch in range(NCH):
            t0 = ch * T
            def ld(sync, t0=t0):
                b.w(sync, "V", "S", "P", "ST")
                for g in range(8):
                    b.inc(sync.dma_start(out=xf[:, g * 512:(g + 1) * 512],
                                         in_=b.xh2[g * 128:(g + 1) * 128, t0:t0 + T]), "L", 16)
                b.inc(sync.dma_start(out=mrow[:], in_=b.sth[2:3, t0:t0 + T]), "L", 16)
                b.inc(sync.dma_start(out=rrow[:], in_=b.sth[3:4, t0:t0 + T]), "L", 16)
            self.sec("sync", ld)
            def tb(t):
                b.w(t, "L", "S", "G")
                b.inc(t.matmul(p_bc1[:], self.ones1f[:], mrow[:], start=True, stop=True), "P")
                b.inc(t.matmul(p_bc2[:], self.ones1f[:], rrow[:], start=True, stop=True), "P")
            self.sec("tensor", tb)
            def vz2(v):
                b.w(v, "P")
                for g in range(8):
                    gsl = slice(g * 512, (g + 1) * 512)
                    v.tensor_tensor(xf[:, gsl], xf[:, gsl], p_bc1[:], op=ALU.subtract)
                    b.inc(v.tensor_tensor(xf[:, gsl], xf[:, gsl], p_bc2[:], op=ALU.mult), "V")
            self.sec("vector", vz2)
            def sz2(s):
                b.w(s, "V")
                for g in range(8):
                    gsl = slice(g * 512, (g + 1) * 512)
                    b.inc(s.copy(z2b[:, gsl], xf[:, gsl]), "S")
            self.sec("scalar", sz2)
            for ec in range(8):
                def tk(t, ec=ec):
                    b.w(t, "S")
                    if ec == 0:
                        b.w(t, "ST")
                    for g in range(8):
                        b.inc(t.matmul(p_k[:], Wkb[:, g * 1024 + ec * 128:g * 1024 + (ec + 1) * 128],
                                       z2b[:, g * 512:(g + 1) * 512],
                                       start=(g == 0), stop=(g == 7)), "P")
                self.sec("tensor", tk)
                def sk(s, ec=ec):
                    b.w(s, "P", "ST", "V")
                    b.inc(s.copy(kt[:], p_k[:]), "S")
                self.sec("scalar", sk)
                def vk(v, ec=ec):
                    b.w(v, "S")
                    b.inc(v.reduce_max(red1[:], kt[:], axis=AX.X), "V")
                    b.w(v, "V")
                    b.inc(v.tensor_tensor(cmax[:, ec:ec + 1], cmax[:, ec:ec + 1],
                                          red1[:], op=ALU.max), "V")
                self.sec("vector", vk)
                def stk(sync, ec=ec, t0=t0):
                    b.w(sync, "V")
                    b.inc(sync.dma_start(out=b.kh[ec * 128:(ec + 1) * 128, t0:t0 + T],
                                         in_=kt[:]), "ST", 16)
                self.sec("sync", stk)
            for tt in range(4):
                for eh in range(2):
                    def tq(t, tt=tt, eh=eh):
                        b.w(t, "S", "V")
                        for g in range(8):
                            b.inc(t.matmul(p_q[eh][:],
                                           z2b[:, g * 512 + tt * 128: g * 512 + (tt + 1) * 128],
                                           Wqb[:, g * 1024 + eh * 512: g * 1024 + (eh + 1) * 512],
                                           start=(g == 0), stop=(g == 7)), "P")
                    self.sec("tensor", tq)
                    def sq(s, tt=tt, eh=eh):
                        b.w(s, "P")
                        b.inc(s.copy(qtm[:, eh * 512:(eh + 1) * 512], p_q[eh][:]), "S")
                    self.sec("scalar", sq)
                def vqm(v, tt=tt):
                    b.w(v, "S")
                    b.inc(v.reduce_max(nmx[:], qtm[:], axis=AX.X, negate=True), "V")
                    b.w(v, "V")
                self.sec("vector", vqm)
                def sqe(s, tt=tt):
                    b.w(s, "V")
                    b.inc(s.activation(qtm[:], qtm[:], AF.Exp, bias=nmx[:]), "S")
                self.sec("scalar", sqe)
                def vqs(v, tt=tt):
                    b.w(v, "S")
                    b.inc(v.reduce_sum(ssum[:], qtm[:], axis=AX.X), "V")
                    b.w(v, "V")
                    b.inc(v.reciprocal(ssum[:], ssum[:]), "V")
                    b.w(v, "V")
                    b.inc(v.tensor_scalar(ssum[:], ssum[:], 0.03125, None, op0=ALU.mult), "V")
                self.sec("vector", vqs)
                def sqn(s, tt=tt):
                    b.w(s, "V")
                    b.inc(s.activation(qnb[:], qtm[:], AF.Identity, scale=ssum[:]), "S")
                self.sec("scalar", sqn)
                for eh in range(2):
                    def tqt(t, tt=tt, eh=eh):
                        b.w(t, "S")
                        for j in range(4):
                            g = eh * 4 + j
                            b.inc(t.transpose(p_trb[:, j * 128:(j + 1) * 128],
                                              qnb[:, g * 128:(g + 1) * 128],
                                              self.ident_b[:]), "P")
                    self.sec("tensor", tqt)
                    def sqt(s, tt=tt, eh=eh):
                        b.w(s, "P")
                        for j in range(4):
                            g = eh * 4 + j
                            b.inc(s.copy(qfm[:, g * 512 + tt * 128: g * 512 + (tt + 1) * 128],
                                         p_trb[:, j * 128:(j + 1) * 128]), "S")
                    self.sec("scalar", sqt)
            def stq(sync, t0=t0):
                b.w(sync, "S")
                for g in range(8):
                    b.inc(sync.dma_start(out=b.qh[g * 128:(g + 1) * 128, t0:t0 + T],
                                         in_=qfm[:, g * 512:(g + 1) * 512]), "ST", 16)
            self.sec("sync", stq)
            if ch == 0:
                self.dump([("C_z2b", z2b[:], [128, 4096], BF16),
                           ("C_kt7", kt[:], [128, 512], F32),
                           ("C_qfm", qfm[:], [128, 4096], BF16),
                           ("C_qtm3", qtm[:], [128, 1024], F32)])

        self.probe_xh2("postC1")
        self.free("p_bc1C", "p_bc2C", "p_kC", "p_qC0", "p_qC1", "p_trbC")
        self.free("WqbC", "WkbC", "xfC", "z2bC", "ktC", "qtmC", "qnbC", "qfmC",
                  "mrowC", "rrowC", "nmxC", "ssumC")

        def vnc(v):
            b.w(v, "S", "ST")
            b.inc(v.tensor_scalar(ncmax[:], cmax[:], -1.0, None, op0=ALU.mult), "V")
        self.sec("vector", vnc)

        Wvb = self.sb("WvbC", [128, 8 * 1024], BF16)
        xfr = self.sb("xfrC", [128, 8 * 512], F32)
        z2br = self.sb("z2brC", [128, 8 * 1024], BF16)
        kfr = self.sb("kfrC", [128, 8 * 512], F32)
        kxb = self.sb("kxbC", [128, 8 * 1024], BF16)
        kxT = self.sb("kxTC", [128, 8 * 1024], BF16)
        vtm = self.sb("vtmC", [128, 8 * 1024], BF16)
        ctxf = self.sb("ctxfC", [128, 8 * 1024], F32)
        mrow2 = self.sb("mrow2C", [1, 1024], F32)
        rrow2 = self.sb("rrow2C", [1, 1024], F32)
        p_bc1 = self.ps("p_bc1C2", [128, 512], F32)
        p_bc2 = self.ps("p_bc2C2", [128, 512], F32)
        p_v = [self.ps("p_vC0", [128, 512], F32), self.ps("p_vC1", [128, 512], F32)]
        p_cx = [self.ps("p_cxC0", [128, 512], F32), self.ps("p_cxC1", [128, 512], F32)]
        p_trb2 = self.ps("p_trbC2", [128, 512], BF16)

        def ldwv(g):
            b.w(g, "V", "S", "P", "ST")
            for i in range(8):
                b.inc(g.dma_start(out=Wvb[:, i * 1024:(i + 1) * 1024],
                                  in_=b.di["Wv"][i * 128:(i + 1) * 128, :]), "G", 16)
        self.sec("gpsimd", ldwv)
        def vzc(v):
            b.w(v, "ST", "S", "P")
            b.inc(v.memset(ctxf[:], 0.0), "V")
        self.sec("vector", vzc)

        RT = 1024
        for rn in range(S // RT):
            r0_ = rn * RT
            def ldr(sync, r0_=r0_):
                b.w(sync, "V", "S", "P", "ST")
                b.inc(sync.dma_start(out=mrow2[:], in_=b.sth[2:3, r0_:r0_ + RT]), "L", 16)
                b.inc(sync.dma_start(out=rrow2[:], in_=b.sth[3:4, r0_:r0_ + RT]), "L", 16)
            self.sec("sync", ldr)
            for hh in range(2):
                h0 = r0_ + hh * 512
                def ldh(sync, h0=h0, hh=hh):
                    if hh == 1:
                        b.w(sync, "V", "S")
                    for g in range(8):
                        b.inc(sync.dma_start(out=xfr[:, g * 512:(g + 1) * 512],
                                             in_=b.xh2[g * 128:(g + 1) * 128, h0:h0 + 512]),
                              "L", 16)
                        b.inc(sync.dma_start(out=kfr[:, g * 512:(g + 1) * 512],
                                             in_=b.kh[g * 128:(g + 1) * 128, h0:h0 + 512]),
                              "L", 16)
                self.sec("sync", ldh)
                def tb(t, hh=hh):
                    b.w(t, "L", "V", "G")
                    b.inc(t.matmul(p_bc1[:], self.ones1f[:], mrow2[:, hh * 512:(hh + 1) * 512],
                                   start=True, stop=True), "P")
                    b.inc(t.matmul(p_bc2[:], self.ones1f[:], rrow2[:, hh * 512:(hh + 1) * 512],
                                   start=True, stop=True), "P")
                self.sec("tensor", tb)
                def vzr(v, hh=hh):
                    b.w(v, "P")
                    for g in range(8):
                        sl = slice(g * 512, (g + 1) * 512)
                        v.tensor_tensor(xfr[:, sl], xfr[:, sl], p_bc1[:], op=ALU.subtract)
                        b.inc(v.tensor_tensor(xfr[:, sl], xfr[:, sl], p_bc2[:], op=ALU.mult), "V")
                self.sec("vector", vzr)
                def szr(s, hh=hh):
                    b.w(s, "V")
                    for g in range(8):
                        b.inc(s.copy(z2br[:, g * 1024 + hh * 512:g * 1024 + (hh + 1) * 512],
                                     xfr[:, g * 512:(g + 1) * 512]), "S")
                self.sec("scalar", szr)
                def skx(s, hh=hh):
                    for g in range(8):
                        b.inc(s.activation(kxb[:, g * 1024 + hh * 512:g * 1024 + (hh + 1) * 512],
                                           kfr[:, g * 512:(g + 1) * 512], AF.Exp,
                                           bias=ncmax[:, g:g + 1]), "S")
                self.sec("scalar", skx)
            def vcs(v):
                b.w(v, "S")
                for g in range(8):
                    sl = slice(g * 1024, (g + 1) * 1024)
                    b.inc(v.reduce_sum(red1[:], kxb[:, sl], axis=AX.X), "V")
                    b.w(v, "V")
                    b.inc(v.tensor_tensor(csum[:, g:g + 1], csum[:, g:g + 1],
                                          red1[:], op=ALU.add), "V")
            self.sec("vector", vcs)
            for tb8 in range(8):
                def tkt(t, tb8=tb8):
                    b.w(t, "S")
                    if tb8 == 0:
                        b.w(t, "V")
                    for g in range(4):
                        b.inc(t.transpose(p_trb2[:, g * 128:(g + 1) * 128],
                                          kxb[:, g * 1024 + tb8 * 128: g * 1024 + (tb8 + 1) * 128],
                                          self.ident_b[:]), "P")
                self.sec("tensor", tkt)
                def skt(s, tb8=tb8):
                    b.w(s, "P")
                    for g in range(4):
                        b.inc(s.copy(kxT[:, tb8 * 1024 + g * 128: tb8 * 1024 + (g + 1) * 128],
                                     p_trb2[:, g * 128:(g + 1) * 128]), "S")
                self.sec("scalar", skt)
                def tkt2(t, tb8=tb8):
                    b.w(t, "S")
                    for g in range(4, 8):
                        b.inc(t.transpose(p_trb2[:, (g - 4) * 128:(g - 3) * 128],
                                          kxb[:, g * 1024 + tb8 * 128: g * 1024 + (tb8 + 1) * 128],
                                          self.ident_b[:]), "P")
                self.sec("tensor", tkt2)
                def skt2(s, tb8=tb8):
                    b.w(s, "P")
                    for g in range(4, 8):
                        b.inc(s.copy(kxT[:, tb8 * 1024 + g * 128: tb8 * 1024 + (g + 1) * 128],
                                     p_trb2[:, (g - 4) * 128:(g - 3) * 128]), "S")
                self.sec("scalar", skt2)
            for tb8 in range(8):
                for eh in range(2):
                    def tv(t, tb8=tb8, eh=eh):
                        b.w(t, "S")
                        for g in range(8):
                            b.inc(t.matmul(p_v[eh][:],
                                           z2br[:, g * 1024 + tb8 * 128: g * 1024 + (tb8 + 1) * 128],
                                           Wvb[:, g * 1024 + eh * 512: g * 1024 + (eh + 1) * 512],
                                           start=(g == 0), stop=(g == 7)), "P")
                    self.sec("tensor", tv)
                    def sv(s, tb8=tb8, eh=eh):
                        b.w(s, "P")
                        b.inc(s.copy(vtm[:, tb8 * 1024 + eh * 512: tb8 * 1024 + (eh + 1) * 512],
                                     p_v[eh][:]), "S")
                    self.sec("scalar", sv)
            for g in range(8):
                for eh in range(2):
                    def tcx(t, g=g, eh=eh):
                        b.w(t, "S", "V")
                        for tb8 in range(8):
                            b.inc(t.matmul(p_cx[eh][:],
                                           kxT[:, tb8 * 1024 + g * 128: tb8 * 1024 + (g + 1) * 128],
                                           vtm[:, tb8 * 1024 + eh * 512: tb8 * 1024 + (eh + 1) * 512],
                                           start=(tb8 == 0), stop=(tb8 == 7)), "P")
                    self.sec("tensor", tcx)
                    def vcx(v, g=g, eh=eh):
                        b.w(v, "P")
                        sl = slice(g * 1024 + eh * 512, g * 1024 + (eh + 1) * 512)
                        b.inc(v.tensor_tensor(ctxf[:, sl], ctxf[:, sl], p_cx[eh][:],
                                              op=ALU.add), "V")
                    self.sec("vector", vcx)
        def vrc(v):
            b.w(v, "V")
            b.inc(v.reciprocal(rcs[:], csum[:]), "V")
            b.w(v, "V")
        self.sec("vector", vrc)
        def snc(s):
            b.w(s, "V")
            for g in range(8):
                sl = slice(g * 1024, (g + 1) * 1024)
                b.inc(s.activation(ctxb[:, sl], ctxf[:, sl], AF.Identity,
                                   scale=rcs[:, g:g + 1]), "S")
        self.sec("scalar", snc)
        def ldfresh(sync):
            b.w(sync, "V", "S", "P")
            for g in range(8):
                b.inc(sync.dma_start(out=xfr[:, g * 512:(g + 1) * 512],
                                     in_=b.xh2[g * 128:(g + 1) * 128, 7680:8192]), "L", 16)
        if self.dbg:
            self.sec("sync", ldfresh)
        self.dump([("C_ctxb", ctxb[:, 0:4096], [128, 4096], BF16),
                   ("C_z2br", z2br[:], [128, 8192], BF16),
                   ("C_kfr", kfr[:], [128, 4096], F32),
                   ("C_x2re", xfr[:], [128, 4096], F32),
                   ("C_csum", csum[:], [128, 8], F32),
                   ("C_cmax", cmax[:], [128, 8], F32),
                   ("C_kxb0", kxb[:], [128, 8192], BF16),
                   ("C_kxT0", kxT[:], [128, 8192], BF16),
                   ("C_vtm0", vtm[:], [128, 8192], BF16)])
        self.free("p_bc1C2", "p_bc2C2", "p_vC0", "p_vC1", "p_cxC0", "p_cxC1", "p_trbC2")
        self.free("WvbC", "xfrC", "z2brC", "kfrC", "kxbC", "kxTC", "vtmC", "ctxfC",
                  "mrow2C", "rrow2C")

        Wob = self.sb("WobC", [128, 8 * 1024], BF16)
        xf3 = self.sb("xf3C", [128, 4096], F32)
        z2f = self.sb("z2fC", [128, 4096], F32)
        qf3 = self.sb("qf3C", [128, 4096], BF16)
        attnb = self.sb("attnbC", [128, 4096], BF16)
        u2 = self.sb("u2C", [128, 4096], F32)
        u2b = self.sb("u2bC", [128, 4096], BF16)
        u2sq = self.sb("u2sqC", [128, 4096], BF16)
        x3b = self.sb("x3bC", [128, 4096], BF16)
        x3sq = self.sb("x3sqC", [128, 4096], BF16)
        m3t = self.sb("m3tC", [128, 512], F32)
        r3t = self.sb("r3tC", [128, 512], F32)
        s3c = self.sb("s3cC", [128, 512], F32)
        mrow3 = self.sb("mrow3C", [1, 512], F32)
        rrow3 = self.sb("rrow3C", [1, 512], F32)
        p_bc1 = self.ps("p_bc1C3", [128, 512], F32)
        p_bc2 = self.ps("p_bc2C3", [128, 512], F32)
        p_a = [self.ps("p_aC0", [128, 512], F32), self.ps("p_aC1", [128, 512], F32)]
        p_s1 = self.ps("p_s1C", [128, 512], F32)
        p_s2 = self.ps("p_s2C", [128, 512], F32)

        def ldwo(g):
            b.w(g, "V", "S", "P", "ST")
            for i in range(8):
                b.inc(g.dma_start(out=Wob[:, i * 1024:(i + 1) * 1024],
                                  in_=b.di["Wo"][i * 128:(i + 1) * 128, :]), "G", 16)
        self.sec("gpsimd", ldwo)

        for ch in range(NCH):
            t0 = ch * T
            def ld(sync, t0=t0):
                b.w(sync, "V", "S", "P", "ST")
                for g in range(8):
                    b.inc(sync.dma_start(out=xf3[:, g * 512:(g + 1) * 512],
                                         in_=b.xh2[g * 128:(g + 1) * 128, t0:t0 + T]), "L", 16)
                    b.inc(sync.dma_start(out=qf3[:, g * 512:(g + 1) * 512],
                                         in_=b.qh[g * 128:(g + 1) * 128, t0:t0 + T]), "L", 16)
                b.inc(sync.dma_start(out=mrow3[:], in_=b.sth[2:3, t0:t0 + T]), "L", 16)
                b.inc(sync.dma_start(out=rrow3[:], in_=b.sth[3:4, t0:t0 + T]), "L", 16)
            self.sec("sync", ld)
            def tb(t):
                b.w(t, "L", "V", "G")
                b.inc(t.matmul(p_bc1[:], self.ones1f[:], mrow3[:], start=True, stop=True), "P")
                b.inc(t.matmul(p_bc2[:], self.ones1f[:], rrow3[:], start=True, stop=True), "P")
            self.sec("tensor", tb)
            def vz3(v):
                b.w(v, "P")
                for g in range(8):
                    gsl = slice(g * 512, (g + 1) * 512)
                    v.tensor_tensor(z2f[:, gsl], xf3[:, gsl], p_bc1[:], op=ALU.subtract)
                    b.inc(v.tensor_tensor(z2f[:, gsl], z2f[:, gsl], p_bc2[:], op=ALU.mult), "V")
            self.sec("vector", vz3)
            for ec in range(8):
                pp = p_a[ec % 2]
                def ta(t, ec=ec, pp=pp):
                    b.w(t, "V") if ec == 0 else b.w(t, "S")
                    for g in range(8):
                        b.inc(t.matmul(pp[:], ctxb[:, g * 1024 + ec * 128: g * 1024 + (ec + 1) * 128],
                                       qf3[:, g * 512:(g + 1) * 512],
                                       start=(g == 0), stop=(g == 7)), "P")
                self.sec("tensor", ta)
                def sa(s, ec=ec, pp=pp):
                    b.w(s, "P")
                    b.inc(s.copy(attnb[:, ec * 512:(ec + 1) * 512], pp[:]), "S")
                self.sec("scalar", sa)
            for fc in range(8):
                pp = p_a[fc % 2]
                def to(t, fc=fc, pp=pp):
                    b.w(t, "S")
                    for g in range(8):
                        b.inc(t.matmul(pp[:], Wob[:, g * 1024 + fc * 128: g * 1024 + (fc + 1) * 128],
                                       attnb[:, g * 512:(g + 1) * 512],
                                       start=(g == 0), stop=(g == 7)), "P")
                self.sec("tensor", to)
                def so(s, fc=fc, pp=pp):
                    b.w(s, "P")
                    b.inc(s.activation(u2[:, fc * 512:(fc + 1) * 512], pp[:],
                                       AF.Identity, bias=bo_col[:, fc:fc + 1]), "S")
                self.sec("scalar", so)
                def vo(v, fc=fc):
                    b.w(v, "S")
                    b.inc(v.tensor_tensor(u2[:, fc * 512:(fc + 1) * 512],
                                          u2[:, fc * 512:(fc + 1) * 512],
                                          z2f[:, fc * 512:(fc + 1) * 512], op=ALU.add), "V")
                self.sec("vector", vo)
            def sst(s):
                b.w(s, "V")
                for g in range(8):
                    gsl = slice(g * 512, (g + 1) * 512)
                    s.copy(u2b[:, gsl], u2[:, gsl])
                    b.inc(s.activation(u2sq[:, gsl], u2[:, gsl], AF.Square), "S")
            self.sec("scalar", sst)
            def tst(t):
                b.w(t, "S")
                for g in range(8):
                    gsl = slice(g * 512, (g + 1) * 512)
                    b.inc(t.matmul(p_s1[:], self.ones128b[:], u2b[:, gsl],
                                   start=(g == 0), stop=(g == 7)), "P")
                    b.inc(t.matmul(p_s2[:], self.ones128b[:], u2sq[:, gsl],
                                   start=(g == 0), stop=(g == 7)), "P")
            self.sec("tensor", tst)
            def smu(s):
                b.w(s, "P")
                s.mul(m3t[:], p_s1[:], self.c_invD)
                b.inc(s.mul(s3c[:], p_s2[:], self.c_invD), "S")
            self.sec("scalar", smu)
            def vvu(v):
                b.w(v, "S")
                v.tensor_tensor(r3t[:], m3t[:], m3t[:], op=ALU.mult)
                b.inc(v.tensor_tensor(s3c[:], s3c[:], r3t[:], op=ALU.subtract), "V")
            self.sec("vector", vvu)
            def sru(s):
                b.w(s, "V")
                b.inc(self.rstd(s, r3t, s3c), "S")
            self.sec("scalar", sru)
            for g in range(8):
                def vap(v, g=g):
                    if g == 0:
                        b.w(v, "S")
                    gsl = slice(g * 512, (g + 1) * 512)
                    v.tensor_tensor(u2[:, gsl], u2[:, gsl], m3t[:], op=ALU.subtract)
                    b.inc(v.tensor_tensor(u2[:, gsl], u2[:, gsl], r3t[:], op=ALU.mult), "V")
                self.sec("vector", vap)
            def sgb(s):
                b.w(s, "V", "ST")
                for g in range(8):
                    gsl = slice(g * 512, (g + 1) * 512)
                    b.inc(s.activation(u2[:, gsl], u2[:, gsl], AF.Identity,
                                       scale=gam_col[:, g:g + 1], bias=bet_col[:, g:g + 1]), "S")
            self.sec("scalar", sgb)
            def vx3(v):
                b.w(v, "S")
                for g in range(8):
                    gsl = slice(g * 512, (g + 1) * 512)
                    b.inc(v.tensor_tensor(u2[:, gsl], u2[:, gsl], xf3[:, gsl], op=ALU.add), "V")
            self.sec("vector", vx3)
            def sst3(s):
                b.w(s, "V")
                for g in range(8):
                    gsl = slice(g * 512, (g + 1) * 512)
                    s.copy(x3b[:, gsl], u2[:, gsl])
                    b.inc(s.activation(x3sq[:, gsl], u2[:, gsl], AF.Square), "S")
            self.sec("scalar", sst3)
            def tst3(t):
                b.w(t, "S")
                for g in range(8):
                    gsl = slice(g * 512, (g + 1) * 512)
                    b.inc(t.matmul(p_s1[:], self.ones128b[:], x3b[:, gsl],
                                   start=(g == 0), stop=(g == 7)), "P")
                    b.inc(t.matmul(p_s2[:], self.ones128b[:], x3sq[:, gsl],
                                   start=(g == 0), stop=(g == 7)), "P")
            self.sec("tensor", tst3)
            def smu3(s):
                b.w(s, "P", "ST")
                s.mul(m3t[:], p_s1[:], self.c_invD)
                b.inc(s.mul(s3c[:], p_s2[:], self.c_invD), "S")
            self.sec("scalar", smu3)
            def vvu3(v):
                b.w(v, "S")
                v.tensor_tensor(r3t[:], m3t[:], m3t[:], op=ALU.mult)
                b.inc(v.tensor_tensor(s3c[:], s3c[:], r3t[:], op=ALU.subtract), "V")
            self.sec("vector", vvu3)
            def sru3(s):
                b.w(s, "V", "ST")
                b.inc(self.rstd(s, r3t, s3c), "S")
            self.sec("scalar", sru3)
            def st3(sync, t0=t0):
                b.w(sync, "V", "S")
                for g in range(8):
                    gsl = slice(g * 512, (g + 1) * 512)
                    b.inc(sync.dma_start(out=b.xh3[g * 128:(g + 1) * 128, t0:t0 + T],
                                         in_=u2[:, gsl]), "ST", 16)
                b.inc(sync.dma_start(out=b.sth[4:5, t0:t0 + T], in_=m3t[0:1, :]), "ST", 16)
                b.inc(sync.dma_start(out=b.sth[5:6, t0:t0 + T], in_=r3t[0:1, :]), "ST", 16)
            self.sec("sync", st3)
            if ch == 0:
                self.dump([("C_attnb", attnb[:], [128, 4096], BF16),
                           ("C_x3", u2[:], [128, 4096], F32)])

        self.free("p_bc1C3", "p_bc2C3", "p_aC0", "p_aC1", "p_s1C", "p_s2C")
        self.free("WobC", "xf3C", "z2fC", "qf3C", "attnbC", "u2C", "u2bC", "u2sqC",
                  "x3bC", "x3sqC", "m3tC", "r3tC", "s3cC", "mrow3C", "rrow3C")
        self.free("cmaxC", "ncmaxC", "csumC", "rcsC", "bo_colC", "gam_colC",
                  "bet_colC", "red1C", "ctxbC")


    def phase_D(self):
        b = self
        wrcol = self.sb("wrcolD", [128, 64], F32)
        x3c = self.sb("x3cD", [128, 8], F32)
        x3sqc = self.sb("x3sqcD", [128, 8], F32)
        of128 = self.sb("of128D", [128, 1], F32)
        on8 = self.sb("on8D", [8, 1], F32)
        s8a = self.sb("s8aD", [8, 1], F32)
        s8b = self.sb("s8bD", [8, 1], F32)
        s8ab = self.sb("s8abD", [8, 2], F32)
        tot2 = self.sb("tot2D", [1, 2], F32)
        dbg4 = self.sb("dbg4D", [1, 8], F32)
        srow = self.sb("srowD", [1, 8], F32)
        brrow = self.sb("brrowD", [1, 8], F32)
        mr2 = self.sb("mr2D", [1, 2], F32)
        mrcol = self.sb("mrcolD", [128, 2], F32)
        m1s = self.sb("m1sD", [1, 2], F32)
        ind0 = self.sb("ind0D", [1, 8], F32)
        ind1 = self.sb("ind1D", [1, 8], F32)
        ind0c = self.sb("ind0cD", [8, 1], F32)
        ind1c = self.sb("ind1cD", [8, 1], F32)
        indc2 = self.sb("indc2D", [8, 2], F32)
        ind2 = self.sb("ind2D", [8, 64], F32)
        rep16 = self.sb("rep16D", [8, 128], F32)
        p16 = self.sb("p16D", [16, 128], F32)
        r16 = self.sb("r16D", [16, 64], F32)
        m16 = self.sb("m16D", [128, 64], F32)
        selb = self.sb("selbD", [128, 64], BF16)
        pi16 = self.sb("pi16D", [16, 128], F32)
        pi8 = self.sb("pi8D", [8, 128], F32)
        pi16b = self.sb("pi16bD", [16, 64], F32)
        p_sm = self.ps("p_smD", [128, 512], F32)
        p_sm2 = self.ps("p_sm2D", [128, 512], F32)

        def ld0(sync):
            b.w(sync, "V", "S", "P", "ST")
            for g in range(8):
                b.inc(sync.dma_start(out=wrcol[:, g * 8:(g + 1) * 8],
                                     in_=b.di["Wr"][g * 128:(g + 1) * 128, :]), "L", 16)
                b.inc(sync.dma_start(out=x3c[:, g:g + 1],
                                     in_=b.xh3[g * 128:(g + 1) * 128, 0:1],
                                     allow_slow_non_contiguous=True), "L", 16)
            b.inc(sync.dma_start(out=brrow[:], in_=b.di["br"]), "L", 16)
        self.sec("sync", ld0)
        def gio(g):
            b.w(g, "L", "ST", "V", "S", "P")
            g.iota(p16[:], pattern=[[0, 8], [1, 16]], base=0, channel_multiplier=0,
                   allow_small_or_imprecise_dtypes=True)
            g.iota(rep16[:], pattern=[[1, 8], [0, 16]], base=0, channel_multiplier=0,
                   allow_small_or_imprecise_dtypes=True)
            g.iota(r16[:], pattern=[[0, 2], [1, 32]], base=0, channel_multiplier=0,
                   allow_small_or_imprecise_dtypes=True)
            g.iota(pi16[:], pattern=[[0, 128]], base=0, channel_multiplier=1,
                   allow_small_or_imprecise_dtypes=True)
            g.iota(pi8[:], pattern=[[0, 128]], base=0, channel_multiplier=1,
                   allow_small_or_imprecise_dtypes=True)
            b.inc(g.iota(pi16b[:], pattern=[[0, 64]], base=0, channel_multiplier=1,
                         allow_small_or_imprecise_dtypes=True), "G")
        self.sec("gpsimd", gio)
        def vio(v):
            b.w(v, "G")
            v.memset(of128[:], 1.0)
            v.memset(on8[:], 1.0)
            v.memset(ind2[:], 0.0)
            v.tensor_tensor(p16[:], p16[:], pi16[:], op=ALU.is_equal)
            v.tensor_tensor(rep16[:], rep16[:], pi8[:], op=ALU.is_equal)
            b.inc(v.tensor_tensor(r16[:], r16[:], pi16b[:], op=ALU.is_equal), "V")
        self.sec("vector", vio)
        def tm16(t):
            b.w(t, "V")
            b.inc(t.matmul(p_sm[:, 0:64], p16[:], r16[:], start=True, stop=True), "P")
        self.sec("tensor", tm16)
        def vm16(v):
            b.w(v, "P")
            b.inc(v.tensor_copy(m16[:], p_sm[:, 0:64]), "V")
        self.sec("vector", vm16)
        def ssq(s):
            b.w(s, "L")
            b.inc(s.square(x3sqc[:], x3c[:]), "S")
        self.sec("scalar", ssq)
        def tp1(t):
            b.w(t, "S", "V")
            b.inc(t.matmul(p_sm[0:8, 0:1], x3c[:], of128[:], start=True, stop=True), "P")
            b.inc(t.matmul(p_sm2[0:8, 0:1], x3sqc[:], of128[:], start=True, stop=True), "P")
        self.sec("tensor", tp1)
        def vp1(v):
            b.w(v, "P")
            v.tensor_copy(s8ab[:, 0:1], p_sm[0:8, 0:1])
            b.inc(v.tensor_copy(s8ab[:, 1:2], p_sm2[0:8, 0:1]), "V")
        self.sec("vector", vp1)
        def tp2(t):
            b.w(t, "V")
            b.inc(t.matmul(p_sm[0:1, 0:2], on8[:], s8ab[:], start=True, stop=True), "P")
        self.sec("tensor", tp2)
        def vp2(v):
            b.w(v, "P")
            b.inc(v.tensor_copy(tot2[:], p_sm[0:1, 0:2]), "V")
            b.w(v, "V")
            v.tensor_scalar(mr2[:, 0:1], tot2[:, 0:1], self.c_invD[0:1, 0:1], None,
                            op0=ALU.mult)
            b.inc(v.tensor_scalar(mr2[:, 1:2], tot2[:, 1:2], self.c_invD[0:1, 0:1],
                                  None, op0=ALU.mult), "V")
            b.w(v, "V")
            b.inc(v.tensor_tensor(tot2[:, 0:1], mr2[:, 0:1], mr2[:, 0:1],
                                  op=ALU.mult), "V")
            b.w(v, "V")
            b.inc(v.tensor_tensor(mr2[:, 1:2], mr2[:, 1:2], tot2[:, 0:1],
                                  op=ALU.subtract), "V")
        self.sec("vector", vp2)
        def vdbg(v):
            b.w(v, "V")
            v.tensor_copy(dbg4[:, 0:2], tot2[:])
            b.inc(v.tensor_copy(dbg4[:, 2:4], mr2[:]), "V")
        self.sec("vector", vdbg)
        def sr(s):
            b.w(s, "V")
            s.activation(mr2[:, 1:2], mr2[:, 1:2], AF.Ln, bias=self.c_eps[0:1, 0:1])
            b.inc(s.activation(mr2[:, 1:2], mr2[:, 1:2], AF.Exp,
                               scale=self.c_mhalf[0:1, 0:1]), "S")
        self.sec("scalar", sr)
        def tbc(t):
            b.w(t, "S")
            b.inc(t.matmul(p_sm[:, 0:2], self.ones1f[:], mr2[:], start=True, stop=True), "P")
        self.sec("tensor", tbc)
        def vz3c(v):
            b.w(v, "P")
            b.inc(v.tensor_copy(mrcol[:], p_sm[:, 0:2]), "V")
            b.w(v, "V")
            v.tensor_scalar(x3c[:], x3c[:], mrcol[:, 0:1], None, op0=ALU.subtract)
            b.inc(v.tensor_scalar(x3c[:], x3c[:], mrcol[:, 1:2], None, op0=ALU.mult), "V")
        self.sec("vector", vz3c)
        def tsc(t):
            b.w(t, "V")
            for g in range(8):
                b.inc(t.matmul(p_sm[0:8, 0:1], wrcol[:, g * 8:(g + 1) * 8],
                               x3c[:, g:g + 1], start=(g == 0), stop=(g == 7)), "P")
        self.sec("tensor", tsc)
        def vsc(v):
            b.w(v, "P")
            b.inc(v.tensor_copy(s8a[:], p_sm[0:8, 0:1]), "V")
        self.sec("vector", vsc)
        def trw(t):
            b.w(t, "V")
            b.inc(t.matmul(p_sm[0:1, 0:8], s8a[:], self.ident_f[0:8, 0:8],
                           start=True, stop=True), "P")
        self.sec("tensor", trw)
        def vtop(v):
            b.w(v, "P")
            b.inc(v.tensor_copy(srow[:], p_sm[0:1, 0:8]), "V")
            b.w(v, "V")
            b.inc(v.tensor_tensor(srow[:], srow[:], brrow[:], op=ALU.add), "V")
            b.w(v, "V")
            b.inc(v.reduce_max(m1s[:, 0:1], srow[:], axis=AX.X), "V")
            b.w(v, "V")
            b.inc(v.tensor_single_scalar(ind0[:], srow[:], m1s[:, 0:1], op=ALU.is_ge), "V")
            b.w(v, "V")
            b.inc(v.scalar_tensor_tensor(ind1[:], ind0[:], -1e30, srow[:],
                                         op0=ALU.mult, op1=ALU.add), "V")
            b.w(v, "V")
            b.inc(v.reduce_max(m1s[:, 1:2], ind1[:], axis=AX.X), "V")
            b.w(v, "V")
            b.inc(v.tensor_single_scalar(ind1[:], ind1[:], m1s[:, 1:2], op=ALU.is_ge), "V")
            b.w(v, "V")
        self.sec("vector", vtop)
        def tic(t):
            b.w(t, "V")
            b.inc(t.matmul(p_sm[0:8, 0:1], ind0[:], self.ones1f[0:1, 0:1],
                           start=True, stop=True), "P")
            b.inc(t.matmul(p_sm2[0:8, 0:1], ind1[:], self.ones1f[0:1, 0:1],
                           start=True, stop=True), "P")
        self.sec("tensor", tic)
        def vic(v):
            b.w(v, "P")
            v.tensor_copy(ind0c[:], p_sm[0:8, 0:1])
            b.inc(v.tensor_copy(ind1c[:], p_sm2[0:8, 0:1]), "V")
            b.w(v, "V")
            v.tensor_copy(indc2[:, 0:1], ind0c[:])
            v.tensor_copy(indc2[:, 1:2], ind1c[:])
            for r in range(16):
                v.tensor_copy(ind2[:, r:r + 1], ind0c[:])
                v.tensor_copy(ind2[:, 32 + r:32 + r + 1], ind1c[:])
            b.inc(v.engine_nop(), "V")
        self.sec("vector", vic)
        def tir(t):
            b.w(t, "V")
            b.inc(t.matmul(p_sm[:, 0:64], rep16[:], ind2[:], start=True, stop=True), "P")
        self.sec("tensor", tir)
        def vsel(v):
            b.w(v, "P")
            b.inc(v.tensor_tensor(m16[:], m16[:], p_sm[:, 0:64], op=ALU.mult), "V")
            b.w(v, "V")
            b.inc(v.tensor_copy(selb[:], m16[:]), "V")
        self.sec("vector", vsel)

        W1g = self.sb("W1gD", [128, 2 * 8192], BF16)
        W2g = self.sb("W2gD", [128, 2 * 8192], BF16)
        b1g = self.sb("b1gD", [128, 16], F32)
        b2g = self.sb("b2gD", [128, 16], F32)
        b2h = self.sb("b2hD", [128, 8], F32)
        wslab = self.sb("wslabD", [128, 1024], F32)
        wtmp = self.sb("wtmpD", [64, 1024], BF16)
        wslabb = self.sb("wslabbD", [128, 1024], BF16)
        bsl = self.sb("bslD", [8, 128], F32)
        bsl2 = self.sb("bsl2D", [2, 128], F32)
        p_gw = self.ps("p_gwD", [64, 512], F32)
        p_gw2 = self.ps("p_gw2D", [64, 512], F32)

        for wi, (wname, wreg) in enumerate([("W1", W1g), ("W2", W2g)]):
            for sl in range(64):
                r0_ = sl * 16
                def lws(sync, wname=wname, r0_=r0_):
                    b.w(sync, "S", "P")
                    b.inc(sync.dma_start(out=wslab[:],
                                         in_=b.di[wname][:, r0_:r0_ + 16, :]), "L", 16)
                self.sec("sync", lws)
                def scast(s, wname=wname):
                    b.w(s, "L", "P")
                    b.inc(s.copy(wslabb[:], wslab[:]), "S")
                self.sec("scalar", scast)
                def tg(t, wname=wname):
                    b.w(t, "S")
                    b.inc(t.matmul(p_gw[:], selb[:], wslabb[:, 0:512],
                                   start=True, stop=True), "P")
                    b.inc(t.matmul(p_gw2[:], selb[:], wslabb[:, 512:1024],
                                   start=True, stop=True), "P")
                self.sec("tensor", tg)
                def sg(s, r0_=r0_):
                    b.w(s, "P", "ST")
                    for j in range(2):
                        s.copy(wtmp[j * 32:j * 32 + 16, 0:512], p_gw[j * 32:j * 32 + 16, :])
                        b.inc(s.copy(wtmp[j * 32:j * 32 + 16, 512:1024],
                                     p_gw2[j * 32:j * 32 + 16, :]), "S")
                self.sec("scalar", sg)
                def stw(sync, wreg=wreg, r0_=r0_):
                    b.w(sync, "S")
                    g = r0_ // 128
                    pr = r0_ % 128
                    for j in range(2):
                        base = j * 8192 + g * 1024
                        b.inc(sync.dma_start(out=wreg[pr:pr + 16, base:base + 1024],
                                             in_=wtmp[j * 32:j * 32 + 16, :]), "ST", 16)
                self.sec("sync", stw)
        for bi, (bname, breg) in enumerate([("b1", b1g), ("b2", b2g)]):
            for g in range(8):
                def lbs(sync, bname=bname, g=g):
                    b.w(sync, "V", "P")
                    b.inc(sync.dma_start(out=bsl[:], in_=b.di[bname][:, g, :]), "L", 16)
                self.sec("sync", lbs)
                def tb1(t, g=g, p_sm=p_sm):
                    b.w(t, "L", "V")
                    b.inc(t.matmul(p_gw[0:2, 0:128], indc2[:], bsl[:],
                                   start=True, stop=True), "P")
                self.sec("tensor", tb1)
                def vb1(v, g=g):
                    b.w(v, "P")
                    b.inc(v.tensor_copy(bsl2[:], p_gw[0:2, 0:128]), "V")
                self.sec("vector", vb1)
                def tb2(t, g=g, p_sm=p_sm):
                    b.w(t, "V")
                    b.inc(t.matmul(p_sm2[:, 0:2], bsl2[:], self.ident_f[0:2, 0:2],
                                   start=True, stop=True), "P")
                self.sec("tensor", tb2)
                def vb2(v, breg=breg, g=g, p_sm=p_sm):
                    b.w(v, "P")
                    b.inc(v.tensor_copy(breg[:, g * 2:g * 2 + 2], p_sm2[:, 0:2]), "V")
                self.sec("vector", vb2)
        def vb2h(v):
            b.w(v, "V")
            for g in range(8):
                v.tensor_tensor(b2h[:, g:g + 1], b2g[:, g * 2:g * 2 + 1],
                                b2g[:, g * 2 + 1:g * 2 + 2], op=ALU.add)
            b.inc(v.engine_nop(), "V")
            b.w(v, "V")
            b.inc(v.tensor_scalar(b2h[:], b2h[:], 0.5, None, op0=ALU.mult), "V")
        self.sec("vector", vb2h)
        self.free("p_gwD", "p_gw2D", "wslabD", "wtmpD", "wslabbD", "bslD", "bsl2D")

        xf4 = self.sb("xf4D", [128, 4096], F32)
        z3b = self.sb("z3bD", [128, 4096], BF16)
        hjb = self.sb("hjbD", [128, 2 * 4096], BF16)
        x4 = self.sb("x4D", [128, 4096], F32)
        mrow4 = self.sb("mrow4D", [1, 512], F32)
        rrow4 = self.sb("rrow4D", [1, 512], F32)
        p_bc1 = self.ps("p_bc1D", [128, 512], F32)
        p_bc2 = self.ps("p_bc2D", [128, 512], F32)
        p_h = [self.ps("p_hD0", [128, 512], F32), self.ps("p_hD1", [128, 512], F32)]

        for ch in range(NCH):
            t0 = ch * T
            def ld(sync, t0=t0):
                b.w(sync, "V", "S", "P", "ST")
                for g in range(8):
                    b.inc(sync.dma_start(out=xf4[:, g * 512:(g + 1) * 512],
                                         in_=b.xh3[g * 128:(g + 1) * 128, t0:t0 + T]), "L", 16)
                b.inc(sync.dma_start(out=mrow4[:], in_=b.sth[4:5, t0:t0 + T]), "L", 16)
                b.inc(sync.dma_start(out=rrow4[:], in_=b.sth[5:6, t0:t0 + T]), "L", 16)
            self.sec("sync", ld)
            def tb(t):
                b.w(t, "L", "V", "S")
                b.inc(t.matmul(p_bc1[:], self.ones1f[:], mrow4[:], start=True, stop=True), "P")
                b.inc(t.matmul(p_bc2[:], self.ones1f[:], rrow4[:], start=True, stop=True), "P")
            self.sec("tensor", tb)
            def vz4(v):
                b.w(v, "P")
                for g in range(8):
                    gsl = slice(g * 512, (g + 1) * 512)
                    v.tensor_tensor(x4[:, gsl], xf4[:, gsl], p_bc1[:], op=ALU.subtract)
                    b.inc(v.tensor_tensor(x4[:, gsl], x4[:, gsl], p_bc2[:], op=ALU.mult), "V")
            self.sec("vector", vz4)
            def sz4(s):
                b.w(s, "V")
                for g in range(8):
                    gsl = slice(g * 512, (g + 1) * 512)
                    b.inc(s.copy(z3b[:, gsl], x4[:, gsl]), "S")
            self.sec("scalar", sz4)
            for j in range(2):
                for ec in range(8):
                    pp = p_h[ec % 2]
                    def th(t, j=j, ec=ec, pp=pp):
                        b.w(t, "S")
                        for g in range(8):
                            b.inc(t.matmul(pp[:],
                                           W1g[:, j * 8192 + g * 1024 + ec * 128:
                                               j * 8192 + g * 1024 + (ec + 1) * 128],
                                           z3b[:, g * 512:(g + 1) * 512],
                                           start=(g == 0), stop=(g == 7)), "P")
                    self.sec("tensor", th)
                    def shh(s, j=j, ec=ec, pp=pp):
                        b.w(s, "P")
                        b.inc(s.activation(hjb[:, j * 4096 + ec * 512:j * 4096 + (ec + 1) * 512],
                                           pp[:], AF.Gelu, bias=b1g[:, ec * 2 + j:ec * 2 + j + 1]),
                              "S")
                    self.sec("scalar", shh)
            for fc in range(8):
                pp = p_h[fc % 2]
                def to(t, fc=fc, pp=pp):
                    b.w(t, "S")
                    for j in range(2):
                        for g in range(8):
                            b.inc(t.matmul(pp[:],
                                           W2g[:, j * 8192 + g * 1024 + fc * 128:
                                               j * 8192 + g * 1024 + (fc + 1) * 128],
                                           hjb[:, j * 4096 + g * 512:j * 4096 + (g + 1) * 512],
                                           start=(j == 0 and g == 0),
                                           stop=(j == 1 and g == 7)), "P")
                self.sec("tensor", to)
                def so(s, fc=fc, pp=pp):
                    b.w(s, "P", "ST")
                    b.inc(s.activation(x4[:, fc * 512:(fc + 1) * 512], pp[:], AF.Identity,
                                       scale=self.c_half, bias=b2h[:, fc:fc + 1]), "S")
                self.sec("scalar", so)
                def vo(v, fc=fc):
                    b.w(v, "S")
                    b.inc(v.tensor_tensor(x4[:, fc * 512:(fc + 1) * 512],
                                          x4[:, fc * 512:(fc + 1) * 512],
                                          xf4[:, fc * 512:(fc + 1) * 512], op=ALU.add), "V")
                self.sec("vector", vo)
            def st4(sync, t0=t0):
                b.w(sync, "V")
                for g in range(8):
                    b.inc(sync.dma_start(out=b.xh1[g * 128:(g + 1) * 128, t0:t0 + T],
                                         in_=x4[:, g * 512:(g + 1) * 512]), "ST", 16)
            self.sec("sync", st4)
            if self.dbg and ch == 0:
                self.dump([("D_srow", srow[:], [1, 8], F32),
                           ("D_z3c", x3c[:], [128, 8], F32),
                           ("D_s8b", s8ab[:, 1:2], [8, 1], F32),
                           ("D_tot2", tot2[:], [1, 2], F32),
                           ("D_dbg4", dbg4[:], [1, 8], F32),
                           ("D_on8", on8[:], [8, 1], F32),
                           ("D_x3sqc", x3sqc[:], [128, 8], F32),
                           ("D_mr2", mr2[:], [1, 2], F32),
                           ("D_wrcol", wrcol[:], [128, 64], F32),
                           ("D_ind0", ind0[:], [1, 8], F32),
                           ("D_ind1", ind1[:], [1, 8], F32),
                           ("D_selb", selb[:], [128, 64], BF16),
                           ("D_b1g", b1g[:], [128, 16], F32),
                           ("D_w1g", W1g[:, 0:4096], [128, 4096], BF16),
                           ("D_hjb", hjb[:], [128, 8192], BF16),
                           ("D_x4", x4[:], [128, 4096], F32)])

        self.free("p_bc1D", "p_bc2D", "p_hD0", "p_hD1")
        self.free("xf4D", "z3bD", "hjbD", "x4D", "mrow4D", "rrow4D")
        self.free("W1gD", "W2gD", "b1gD", "b2gD", "b2hD")
        self.free("wrcolD", "x3cD", "x3sqcD", "of128D", "on8D", "s8aD", "s8bD", "s8abD",
                  "tot2D", "dbg4D", "srowD", "brrowD", "mr2D", "mrcolD", "m1sD", "ind0D",
                  "ind1D", "ind0cD", "ind1cD", "indc2D", "ind2D", "rep16D", "p16D",
                  "r16D", "m16D", "selbD", "pi16D", "pi8D", "pi16bD", "p_smD",
                  "p_sm2D")

    def epilogue(self, src):
        b = self
        efm = self.sb("efm", [128, 4096], F32)
        etm = self.sb("etm", [128, 1024], F32)
        p_ep = self.ps("p_ep", [128, 512], F32)
        for ch in range(NCH):
            t0 = ch * T
            def ld(sync, t0=t0):
                b.w(sync, "P", "ST")
                for g in range(8):
                    b.inc(sync.dma_start(out=efm[:, g * 512:(g + 1) * 512],
                                         in_=src[g * 128:(g + 1) * 128, t0:t0 + T]), "L", 16)
            self.sec("sync", ld)
            for tt in range(4):
                for half in range(2):
                    def ttr(t, tt=tt, half=half):
                        b.w(t, "L", "S", "ST")
                        for j in range(4):
                            g = half * 4 + j
                            b.inc(t.transpose(
                                p_ep[:, j * 128:(j + 1) * 128],
                                efm[:, g * 512 + tt * 128: g * 512 + tt * 128 + 128],
                                self.ident_f[:]), "P")
                    self.sec("tensor", ttr)
                    def sev(s, half=half):
                        b.w(s, "P", "ST")
                        b.inc(s.copy(etm[:, half * 512:(half + 1) * 512], p_ep[:]), "S")
                    self.sec("scalar", sev)
                def stt(sync, tt=tt, t0=t0):
                    b.w(sync, "S")
                    b.inc(sync.dma_start(out=b.out[t0 + tt * 128: t0 + (tt + 1) * 128, :],
                                         in_=etm[:]), "ST", 16)
                self.sec("sync", stt)
        def fin_(sync):
            b.w(sync, "ST")
        self.sec("sync", fin_)
        self.free("efm", "etm", "p_ep")

    def build(self):
        self.prep_common()
        self.phase_A()
        src = self.xh1
        if self.nphase >= 2:
            self.phase_B()
            src = self.xh2
        if self.nphase >= 3:
            self.phase_C()
            src = self.xh3
        if self.nphase >= 4:
            self.phase_D()
            src = self.xh1
        self.epilogue(src)
        return self.fin()


def _host_prep(inputs):
    f = np.ascontiguousarray
    w = {
        "A_log": inputs["L0_attn_A_log"].reshape(1, 16),
        "xpw": inputs["L0_attn_x_proj_w"],
        "dtw": inputs["L0_attn_dt_proj_w"],
        "dtb": inputs["L0_attn_dt_proj_b"].reshape(8, 128),
        "Dv": inputs["L0_attn_D"].reshape(8, 128),
        "Win": inputs["L0_ffn_W_in"],
        "bin": inputs["L0_ffn_b_in"].reshape(32, 128),
        "Wout": inputs["L0_ffn_W_out"],
        "bout": inputs["L0_ffn_b_out"].reshape(8, 128),
        "Wq": inputs["L1_attn_Wq"], "Wk": inputs["L1_attn_Wk"],
        "Wv": inputs["L1_attn_Wv"], "Wo": inputs["L1_attn_Wo"],
        "bo": inputs["L1_attn_bo"].reshape(8, 128),
        "gam": inputs["L1_attn_gamma"].reshape(8, 128),
        "bet": inputs["L1_attn_beta"].reshape(8, 128),
        "Wr": inputs["L1_ffn_Wr"], "br": inputs["L1_ffn_br"].reshape(1, 8),
        "W1": inputs["L1_ffn_W1"], "b1": inputs["L1_ffn_b1"].reshape(NE, 8, 128),
        "W2": inputs["L1_ffn_W2"], "b2": inputs["L1_ffn_b2"].reshape(NE, 8, 128),
    }
    w = {k: f(np.asarray(v, dtype=np.float32)) for k, v in w.items()}
    u = np.asarray(inputs["L0_attn_u"], dtype=np.float32)
    return [dict(w, u=f(u[c])) for c in range(8)]


def kernel(**inputs):
    kb = KB(nphase=_NPHASE)
    nc = kb.build()
    in_maps = _host_prep(inputs)
    res = run_bass_kernel_spmd(nc, in_maps, core_ids=list(range(8)))
    out = np.stack([np.asarray(r["out"]) for r in res.results], axis=0)
    return out


# version 03 (final)
# speedup vs baseline: 1.2671x; 1.2671x over previous
import numpy as np
import ml_dtypes
import concourse.bass as bass
import concourse.mybir as mybir
from concourse.bass_utils import run_bass_kernel_spmd

F32 = mybir.dt.float32
BF16 = mybir.dt.bfloat16
AF = mybir.ActivationFunctionType
ALU = mybir.AluOpType
AX = mybir.AxisListType

MAGIC = 12582912.0
EPS = 1e-5
S = 8192
D = 1024
T = 512
NCH = S // T
N16 = 16
DFF = 4096
NE = 8

_NPHASE = 4


class KB:
    def __init__(self, nphase=_NPHASE, dbg=False):
        self.dbg = dbg
        self.nphase = nphase
        nc = bass.Bass("TRN2", target_bir_lowering=False,
                       detect_race_conditions=False)
        self.nc = nc
        self.c = {}
        self.sems = {}
        self._cms = {}
        self._alloc_seq = []

        di = {}
        def inp(name, shape, dt=F32):
            di[name] = nc.dram_tensor(name, list(shape), dt, kind="ExternalInput").ap()
        inp("u", (S, D))
        inp("A_log", (1, 16)); inp("xpw", (96, D)); inp("dtw", (D, 64))
        inp("dtb", (8, 128)); inp("Dv", (8, 128))
        inp("Win", (D, DFF)); inp("bin", (32, 128))
        inp("Wout", (DFF, D)); inp("bout", (8, 128))
        inp("Wq", (D, D)); inp("Wk", (D, D)); inp("Wv", (D, D)); inp("Wo", (D, D))
        inp("bo", (8, 128)); inp("gam", (8, 128)); inp("bet", (8, 128))
        inp("Wr", (D, 8)); inp("br", (1, 8))
        inp("W1", (NE, D, D)); inp("b1", (NE, 8, 128))
        inp("W2", (NE, D, D)); inp("b2", (NE, 8, 128))
        self.di = di
        self.out = nc.dram_tensor("out", [S, D], F32, kind="ExternalOutput").ap()

        self.xh1 = nc.dram_tensor("xh1", [D, S], F32).ap()
        self.xh2 = nc.dram_tensor("xh2", [D, S], F32).ap()
        self.xh3 = nc.dram_tensor("xh3", [D, S], F32).ap()
        self.hh = nc.dram_tensor("hh", [DFF, S], BF16).ap()
        self.kh = nc.dram_tensor("kh", [D, S], F32).ap()
        self.qh = nc.dram_tensor("qh", [D, S], BF16).ap()
        self.sth = nc.dram_tensor("sth", [8, S], F32).ap()
        self.cc_in = nc.dram_tensor("cc_in", [1, 1], F32)
        self.cc_out = nc.dram_tensor("cc_out", [1, 1], F32)
        self.cc_in2 = nc.dram_tensor("cc_in2", [1, 1], F32)
        self.cc_out2 = nc.dram_tensor("cc_out2", [1, 1], F32)

        for s in ["L", "G", "P", "V", "S", "ST", "CC"]:
            self.sems[s] = nc.semaphore(s).__enter__()
            self.c[s] = 0
        self.blk = nc.Block().__enter__()

    def inc(self, instr, sem, n=1):
        instr.then_inc(self.sems[sem], n)
        self.c[sem] += n
        return instr

    def w(self, eng, *sems):
        for s in sems:
            if self.c[s] > 0:
                eng.wait_ge(self.sems[s], self.c[s])

    def wv(self, eng, sem, val):
        if val > 0:
            eng.wait_ge(self.sems[sem], val)

    def sb(self, name, shape, dt=F32):
        t = self.nc.sbuf_tensor(name, list(shape), dt)
        self._cms[name] = t
        self._alloc_seq.append(name)
        return t.__enter__()

    def ps(self, name, shape, dt=F32):
        t = self.nc.psum_tensor(name, list(shape), dt)
        self._cms[name] = t
        self._alloc_seq.append(name)
        return t.__enter__()

    def free(self, *names):
        order = {n: i for i, n in enumerate(self._alloc_seq)}
        for n in sorted(names, key=lambda n: -order[n]):
            cm = self._cms.pop(n, None)
            if cm is not None:
                cm.__exit__(None, None, None)

    def sec(self, eng, fn):
        getattr(self.blk, eng)(fn)

    def fin(self):
        self.blk.__exit__(None, None, None)
        return self.nc

    def dump(self, items):
        if not self.dbg:
            return
        b = self
        def dmp(sync):
            b.w(sync, "V", "S", "P", "G")
            for nm, ap_, shp, dt in items:
                o = self.nc.dram_tensor("dbg_" + nm, list(shp), dt,
                                        kind="ExternalOutput").ap()
                b.inc(sync.dma_start(out=o, in_=ap_), "ST", 16)
        self.sec("sync", dmp)
        for e in ["vector", "scalar", "tensor", "gpsimd"]:
            def bar(eng):
                b.w(eng, "ST")
            self.sec(e, bar)

    def probe_xh2(self, tag):
        if not self.dbg:
            return
        b = self
        tmp = self.sb("pxh2_" + tag, [128, 4096], F32)
        def ldp(sync):
            b.w(sync, "V", "S", "P", "ST", "G")
            for g in range(8):
                b.inc(sync.dma_start(out=tmp[:, g * 512:(g + 1) * 512],
                                     in_=b.xh2[g * 128:(g + 1) * 128, 7680:8192]), "L", 16)
        self.sec("sync", ldp)
        self.dump([("PX_" + tag, tmp[:], [128, 4096], F32)])
        self.free("pxh2_" + tag)

    def rstd(self, s, rt, vt):
        s.activation(rt[:], vt[:], AF.Ln, bias=self.c_eps)
        return s.activation(rt[:], rt[:], AF.Exp, scale=self.c_mhalf)

    def prep_common(self):
        b = self
        self.ident_f = self.sb("ident_f", [128, 128], F32)
        self.ident_b = self.sb("ident_b", [128, 128], BF16)
        self.ones1b = self.sb("ones1b", [1, 128], BF16)
        self.ones1f = self.sb("ones1f", [1, 128], F32)
        self.ones128b = self.sb("ones128b", [128, 128], BF16)
        self.iota_p = self.sb("iota_p", [128, 128], F32)
        self.A_rep = self.sb("A_rep", [128, 16], F32)
        self.Dcol = self.sb("Dcol", [128, 8], F32)
        self.dtbcol = self.sb("dtbcol", [128, 8], F32)
        self.sel16 = self.sb("sel16", [16, 16 * 128], BF16)
        self.WxpT = self.sb("WxpT", [128, 8 * 96], BF16)
        self.WdtT = self.sb("WdtT", [64, 8 * 128], BF16)
        self.tmp_w = self.sb("tmp_w", [128, 1024], F32)
        self.arow = self.sb("arow", [1, 16], F32)
        cst = self.sb("cst", [128, 8], F32)
        self.c_eps = cst[:, 0:1]
        self.c_mhalf = cst[:, 1:2]
        self.c_invD = cst[:, 2:3]
        self.c_half = cst[:, 3:4]
        self.c_invDff = cst[:, 4:5]
        p_pr = self.ps("p_prep", [128, 512], F32)
        p_a = self.ps("p_a", [128, 16], F32)

        s16a = self.sb("s16a", [16, 16 * 128], F32)
        s16b = self.sb("s16b", [16, 16 * 128], F32)

        def g1(g):
            g.iota(self.iota_p[:], pattern=[[0, 128]], base=0, channel_multiplier=1,
                   allow_small_or_imprecise_dtypes=True)
            g.iota(self.ident_f[:], pattern=[[1, 128]], base=0, channel_multiplier=0,
                   allow_small_or_imprecise_dtypes=True)
            g.iota(s16a[:], pattern=[[1, 16], [0, 128]], base=0, channel_multiplier=0,
                   allow_small_or_imprecise_dtypes=True)
            b.inc(g.iota(s16b[:], pattern=[[0, 16], [0, 128]], base=0, channel_multiplier=1,
                         allow_small_or_imprecise_dtypes=True), "G")
        self.sec("gpsimd", g1)

        def s1(sync):
            b.inc(sync.dma_start(out=self.arow[:], in_=b.di["A_log"]), "L", 16)
            for g in range(8):
                b.inc(sync.dma_start(out=self.Dcol[:, g:g + 1],
                                     in_=b.di["Dv"][g:g + 1, :]), "L", 16)
                b.inc(sync.dma_start(out=self.dtbcol[:, g:g + 1],
                                     in_=b.di["dtb"][g:g + 1, :]), "L", 16)
            b.inc(sync.dma_start(out=self.tmp_w[0:96, :], in_=b.di["xpw"]), "L", 16)
        self.sec("sync", s1)

        def v1(v):
            b.w(v, "G", "L")
            v.tensor_tensor(s16a[:], s16a[:], s16b[:], op=ALU.is_equal)
            v.tensor_copy(self.sel16[:], s16a[:])
            v.memset(cst[:, 0:1], EPS)
            v.memset(cst[:, 1:2], -0.5)
            v.memset(cst[:, 2:3], 1.0 / D)
            v.memset(cst[:, 3:4], 0.5)
            v.memset(cst[:, 4:5], 1.0 / DFF)
            v.tensor_tensor(self.ident_f[:], self.iota_p[:], self.ident_f[:], op=ALU.is_equal)
            v.tensor_copy(self.ident_b[:], self.ident_f[:])
            v.memset(self.ones1b[:], 1.0)
            v.memset(self.ones1f[:], 1.0)
            b.inc(v.memset(self.ones128b[:], 1.0), "V")
        self.sec("vector", v1)

        def s2(s):
            b.w(s, "L", "V")
            b.inc(s.activation(self.arow[:], self.arow[:], AF.Exp), "S")
        self.sec("scalar", s2)
        def v2(v):
            b.w(v, "S")
            b.inc(v.tensor_scalar(self.arow[:], self.arow[:], -1.0, None, op0=ALU.mult), "V")
        self.sec("vector", v2)
        def t2(t):
            b.w(t, "V")
            b.inc(t.matmul(p_a[:], self.ones1f[:], self.arow[:], start=True, stop=True), "P")
        self.sec("tensor", t2)
        def v3(v):
            b.w(v, "P")
            b.inc(v.tensor_copy(self.A_rep[:], p_a[:]), "V")
        self.sec("vector", v3)

        for g in range(8):
            def tf(t, g=g):
                b.w(t, "V", "L", "S")
                b.inc(t.transpose(p_pr[:, 0:96], self.tmp_w[0:96, g * 128:(g + 1) * 128],
                                  self.ident_f[0:96, 0:96]), "P")
            self.sec("tensor", tf)
            def sf(s, g=g):
                b.w(s, "P")
                b.inc(s.copy(self.WxpT[:, g * 96:(g + 1) * 96], p_pr[:, 0:96]), "S")
            self.sec("scalar", sf)
        for g in range(8):
            def lf(sync, g=g):
                b.w(sync, "P")
                b.inc(sync.dma_start(out=self.tmp_w[:, 0:64],
                                     in_=b.di["dtw"][g * 128:(g + 1) * 128, :]), "L", 16)
            self.sec("sync", lf)
            def tf(t, g=g):
                b.w(t, "L", "S")
                b.inc(t.transpose(p_pr[0:64, 0:128], self.tmp_w[:, 0:64], self.ident_f[:]), "P")
            self.sec("tensor", tf)
            def sf(s, g=g):
                b.w(s, "P")
                b.inc(s.copy(self.WdtT[:, g * 128:(g + 1) * 128], p_pr[0:64, 0:128]), "S")
            self.sec("scalar", sf)
        self.free("p_prep", "p_a", "s16a", "s16b")

    def phase_A(self):
        b = self
        amax1 = self.sb("amax1", [128, 1], F32)
        u_tm = self.sb("u_tm", [128, 4096], F32)
        z0b_tm = self.sb("z0b_tm", [128, 4096], BF16)
        z0f = self.sb("z0f", [128, 4096], F32)
        z0bf = self.sb("z0bf", [128, 4096], BF16)
        r0 = self.sb("r0", [128, 4096], F32)
        delta = self.sb("delta", [128, 4096], F32)
        du = self.sb("du", [128, 4096], F32)
        hblk = self.sb("hblk", [128, 8 * 512], F32)
        yfm = self.sb("yfm", [128, 4096], F32)
        dA_t = self.sb("dA_t", [128, 2 * 512], F32)
        dBu_t = self.sb("dBu_t", [128, 2 * 512], F32)
        ytmp = self.sb("ytmp", [128, 2 * 512], F32)
        state = self.sb("state", [128, 128], F32)
        dtr_b = self.sb("dtr_b", [64, 512], BF16)
        B_b = self.sb("B_b", [16, 512], BF16)
        C_b = self.sb("C_b", [16, 512], BF16)
        mst = self.sb("mst", [128, 4], F32)
        vst = self.sb("vst", [128, 4], F32)
        rst = self.sb("rst", [128, 4], F32)
        nmr = self.sb("nmr", [128, 4], F32)
        sc4 = self.sb("sc4", [128, 4], F32)
        bn6 = self.sb("bn6", [128, 12], F32)
        bn2 = self.sb("bn2", [128, 2], F32)
        m1t = self.sb("m1t", [128, 512], F32)
        r1t = self.sb("r1t", [128, 512], F32)
        sc512 = self.sb("sc512", [128, 512], F32)
        z1t = self.sb("z1t", [128, 512], F32)

        p_trb = self.ps("pA_trb", [128, 512], BF16)
        p_misc = self.ps("pA_misc", [128, 512], F32)
        p_xp = self.ps("pA_xp", [80, 512], F32)
        p_xpc = self.ps("pA_xpc", [16, 512], F32)
        p_bb = self.ps("pA_bb", [128, 512], F32)
        p_cb = self.ps("pA_cb", [128, 512], F32)
        p_s1 = self.ps("pA_s1", [128, 512], F32)
        p_s2 = self.ps("pA_s2", [128, 512], F32)

        def vinit(v):
            v.memset(state[:], 0.0)
            b.inc(v.memset(amax1[:], 0.0), "V")
        self.sec("vector", vinit)

        for ch in range(NCH):
            t0 = ch * T

            def ld(sync, t0=t0):
                b.w(sync, "V", "P")
                for i in range(4):
                    b.inc(sync.dma_start(
                        out=u_tm[:, i * 1024:(i + 1) * 1024],
                        in_=b.di["u"][t0 + i * 128: t0 + (i + 1) * 128, :]), "L", 16)
            self.sec("sync", ld)

            def vln(v):
                b.w(v, "L")
                for i in range(4):
                    v.bn_stats(bn6[:, 0:6], u_tm[:, i * 1024:i * 1024 + 512])
                    b.inc(v.bn_stats(bn6[:, 6:12], u_tm[:, i * 1024 + 512:(i + 1) * 1024]), "V")
                    b.w(v, "V")
                    b.inc(v.bn_aggr(bn2[:], bn6[:]), "V")
                    b.w(v, "V")
                    v.tensor_copy(mst[:, i:i + 1], bn2[:, 0:1])
                    b.inc(v.tensor_copy(vst[:, i:i + 1], bn2[:, 1:2]), "V")
            self.sec("vector", vln)

            def sln(s):
                b.w(s, "V")
                b.inc(self.rstd(s, rst, vst), "S")
            self.sec("scalar", sln)

            def vln2(v):
                b.w(v, "S")
                b.inc(v.scalar_tensor_tensor(nmr[:], mst[:], -1.0, rst[:],
                                             op0=ALU.mult, op1=ALU.mult), "V")
            self.sec("vector", vln2)

            def sz0(s):
                b.w(s, "V")
                for i in range(4):
                    sl = slice(i * 1024, (i + 1) * 1024)
                    b.inc(s.activation(z0b_tm[:, sl], u_tm[:, sl], AF.Identity,
                                       bias=nmr[:, i:i + 1], scale=rst[:, i:i + 1]), "S")
            self.sec("scalar", sz0)

            for g in range(8):
                def ttr(t, g=g):
                    b.w(t, "S", "V")
                    for i in range(4):
                        b.inc(t.transpose(
                            p_trb[:, i * 128:(i + 1) * 128],
                            z0b_tm[:, i * 1024 + g * 128: i * 1024 + (g + 1) * 128],
                            self.ident_b[:]), "P")
                self.sec("tensor", ttr)
                def sev(s, g=g):
                    b.w(s, "P")
                    s.copy(z0f[:, g * 512:(g + 1) * 512], p_trb[:])
                    b.inc(s.copy(z0bf[:, g * 512:(g + 1) * 512], p_trb[:]), "S")
                self.sec("scalar", sev)
                def ttr2(t, g=g):
                    b.w(t, "S")
                    for i in range(4):
                        b.inc(t.transpose(
                            p_misc[:, i * 128:(i + 1) * 128],
                            u_tm[:, i * 1024 + g * 128: i * 1024 + (g + 1) * 128],
                            self.ident_f[:]), "P")
                self.sec("tensor", ttr2)
                def vr0(v, g=g):
                    b.w(v, "P", "S", "ST")
                    b.inc(v.scalar_tensor_tensor(r0[:, g * 512:(g + 1) * 512],
                                                 z0f[:, g * 512:(g + 1) * 512],
                                                 self.Dcol[:, g:g + 1], p_misc[:],
                                                 op0=ALU.mult, op1=ALU.add), "V")
                self.sec("vector", vr0)

            def txp(t):
                b.w(t, "S", "V")
                for g in range(8):
                    b.inc(t.matmul(p_xp[:], self.WxpT[:, g * 96:g * 96 + 80],
                                   z0bf[:, g * 512:(g + 1) * 512],
                                   start=(g == 0), stop=(g == 7)), "P")
                for g in range(8):
                    b.inc(t.matmul(p_xpc[:], self.WxpT[:, g * 96 + 80:(g + 1) * 96],
                                   z0bf[:, g * 512:(g + 1) * 512],
                                   start=(g == 0), stop=(g == 7)), "P")
            self.sec("tensor", txp)
            def sxp(s):
                b.w(s, "P")
                b.inc(s.copy(dtr_b[:], p_xp[0:64, :]), "S")
            self.sec("scalar", sxp)
            def vxp(v):
                b.w(v, "P")
                v.tensor_copy(B_b[:], p_xp[64:80, :])
                b.inc(v.tensor_copy(C_b[:], p_xpc[0:16, :]), "V")
            self.sec("vector", vxp)

            for g in range(8):
                def tdl(t, g=g):
                    b.w(t, "S", "V")
                    b.inc(t.matmul(p_misc[:], self.WdtT[:, g * 128:(g + 1) * 128],
                                   dtr_b[:], start=True, stop=True), "P")
                self.sec("tensor", tdl)
                def sdl(s, g=g):
                    b.w(s, "P")
                    sl = slice(g * 512, (g + 1) * 512)
                    s.activation(du[:, sl], p_misc[:], AF.Abs, bias=self.dtbcol[:, g:g + 1])
                    s.activation(du[:, sl], du[:, sl], AF.Exp, scale=-1.0)
                    s.activation(du[:, sl], du[:, sl], AF.Ln, bias=1.0)
                    b.inc(s.activation(delta[:, sl], p_misc[:], AF.Relu,
                                       bias=self.dtbcol[:, g:g + 1]), "S")
                self.sec("scalar", sdl)
                def vdl(v, g=g):
                    b.w(v, "S")
                    sl = slice(g * 512, (g + 1) * 512)
                    v.tensor_tensor(delta[:, sl], delta[:, sl], du[:, sl], op=ALU.add)
                    b.inc(v.tensor_tensor(du[:, sl], delta[:, sl], z0f[:, sl],
                                          op=ALU.mult), "V")
                self.sec("vector", vdl)

            Vbase = self.c["V"]
            Sfull = self.c["S"]
            vdone = []
            for n in range(N16):
                def tbc(t, n=n):
                    if n == 0:
                        b.w(t, "V", "S")
                    else:
                        b.wv(t, "V", vdone[n * 8 - 1])
                    b.inc(t.matmul(p_bb[:], self.sel16[:, n * 128:(n + 1) * 128],
                                   B_b[:], start=True, stop=True), "P")
                    b.inc(t.matmul(p_cb[:], self.sel16[:, n * 128:(n + 1) * 128],
                                   C_b[:], start=True, stop=True), "P")
                self.sec("tensor", tbc)
                Pn = self.c["P"]
                for g in range(8):
                    k = n * 8 + g
                    par = k % 2
                    def sda(s, n=n, g=g, k=k, par=par):
                        if k >= 2:
                            b.wv(s, "V", vdone[k - 2])
                        else:
                            b.wv(s, "V", Vbase)
                        b.inc(s.activation(dA_t[:, par * 512:(par + 1) * 512],
                                           delta[:, g * 512:(g + 1) * 512],
                                           AF.Exp, scale=self.A_rep[:, n:n + 1]), "S")
                    self.sec("scalar", sda)
                    Sk = self.c["S"]
                    def vsc(v, n=n, g=g, k=k, par=par, Sk=Sk, Pn=Pn):
                        b.wv(v, "S", Sk)
                        b.wv(v, "P", Pn)
                        gsl = slice(g * 512, (g + 1) * 512)
                        v.tensor_tensor(dBu_t[:, par * 512:(par + 1) * 512],
                                        du[:, gsl], p_bb[:], op=ALU.mult)
                        v.tensor_tensor_scan(hblk[:, gsl],
                                             dA_t[:, par * 512:(par + 1) * 512],
                                             dBu_t[:, par * 512:(par + 1) * 512],
                                             state[:, g * 16 + n:g * 16 + n + 1],
                                             op0=ALU.mult, op1=ALU.add)
                        v.tensor_tensor(ytmp[:, par * 512:(par + 1) * 512],
                                        hblk[:, gsl], p_cb[:], op=ALU.mult)
                        if n == 0:
                            b.inc(v.tensor_copy(yfm[:, gsl],
                                                ytmp[:, par * 512:(par + 1) * 512]), "V")
                        else:
                            b.inc(v.tensor_tensor(yfm[:, gsl], yfm[:, gsl],
                                                  ytmp[:, par * 512:(par + 1) * 512],
                                                  op=ALU.add), "V")
                    self.sec("vector", vsc)
                    vdone.append(self.c["V"])
                def vst_up(v, n=n):
                    b.inc(v.tensor_copy(state[0:128, n:n + 113:16],
                                        hblk[0:128, 511:4096:512]), "V")
                self.sec("vector", vst_up)

            for g in range(8):
                def vx1(v, g=g):
                    gsl = slice(g * 512, (g + 1) * 512)
                    b.inc(v.tensor_tensor(r0[:, gsl], r0[:, gsl], yfm[:, gsl],
                                          op=ALU.add), "V")
                self.sec("vector", vx1)
            def sst(s):
                b.w(s, "V")
                for g in range(8):
                    gsl = slice(g * 512, (g + 1) * 512)
                    s.copy(z0bf[:, gsl], r0[:, gsl])
                    b.inc(s.activation(z0b_tm[:, gsl], r0[:, gsl], AF.Square), "S")
            self.sec("scalar", sst)
            def tst(t):
                b.w(t, "S")
                for g in range(8):
                    gsl = slice(g * 512, (g + 1) * 512)
                    b.inc(t.matmul(p_s1[:], self.ones128b[:], z0bf[:, gsl],
                                   start=(g == 0), stop=(g == 7)), "P")
                    b.inc(t.matmul(p_s2[:], self.ones128b[:], z0b_tm[:, gsl],
                                   start=(g == 0), stop=(g == 7)), "P")
            self.sec("tensor", tst)
            def sm1(s):
                b.w(s, "P", "V", "ST")
                s.mul(m1t[:], p_s1[:], self.c_invD)
                b.inc(s.mul(sc512[:], p_s2[:], self.c_invD), "S")
            self.sec("scalar", sm1)
            def vv1(v):
                b.w(v, "S")
                v.tensor_tensor(z1t[:], m1t[:], m1t[:], op=ALU.mult)
                b.inc(v.tensor_tensor(sc512[:], sc512[:], z1t[:], op=ALU.subtract), "V")
            self.sec("vector", vv1)
            def sr1(s):
                b.w(s, "V", "ST")
                b.inc(self.rstd(s, r1t, sc512), "S")
            self.sec("scalar", sr1)
            for g in range(8):
                def vam(v, g=g):
                    if g == 0:
                        b.w(v, "S")
                    gsl = slice(g * 512, (g + 1) * 512)
                    v.tensor_tensor(z1t[:], r0[:, gsl], m1t[:], op=ALU.subtract)
                    v.tensor_tensor(z1t[:], z1t[:], r1t[:], op=ALU.mult)
                    b.inc(v.reduce_max(sc4[:, 0:1], z1t[:], axis=AX.X,
                                       apply_absolute_value=True), "V")
                    b.w(v, "V")
                    b.inc(v.tensor_tensor(amax1[:], amax1[:], sc4[:, 0:1], op=ALU.max), "V")
                self.sec("vector", vam)
            def st1(sync, t0=t0):
                b.w(sync, "V")
                for g in range(8):
                    gsl = slice(g * 512, (g + 1) * 512)
                    b.inc(sync.dma_start(out=b.xh1[g * 128:(g + 1) * 128, t0:t0 + T],
                                         in_=r0[:, gsl]), "ST", 16)
                b.inc(sync.dma_start(out=b.sth[0:1, t0:t0 + T], in_=m1t[0:1, :]), "ST", 16)
                b.inc(sync.dma_start(out=b.sth[1:2, t0:t0 + T], in_=r1t[0:1, :]), "ST", 16)
            self.sec("sync", st1)

            if self.dbg and ch == 0:
                dumps = [("z0f", z0f, F32), ("delta", delta, F32), ("du", du, F32),
                         ("yfm", yfm, F32), ("x1", r0, F32), ("hblk", hblk, F32),
                         ("u_tm", u_tm, F32)]
                small = [("z0b_tm", z0b_tm, [128, 4096]), ("rst", rst, [128, 4]),
                         ("nmr", nmr, [128, 4]), ("z0bf", z0bf, [128, 4096]),
                         ("A_rep", self.A_rep, [128, 16]), ("B_b", B_b, [16, 512]),
                         ("C_b", C_b, [16, 512]), ("dtr_b", dtr_b, [64, 512]),
                         ("m1t", m1t, [128, 512]), ("r1t", r1t, [128, 512]),
                         ("state", state, [128, 128])]
                def dmp(sync):
                    b.w(sync, "V", "S", "P")
                    for nm, tens, dt in dumps:
                        ap = self.nc.dram_tensor("dbg_" + nm, [128, 4096], dt,
                                                 kind="ExternalOutput").ap()
                        b.inc(sync.dma_start(out=ap, in_=tens[:]), "ST", 16)
                    for nm, tens, shp in small:
                        ap = self.nc.dram_tensor("dbg_" + nm, list(shp),
                                                 BF16 if tens.dtype == BF16 else F32,
                                                 kind="ExternalOutput").ap()
                        b.inc(sync.dma_start(out=ap, in_=tens[:]), "ST", 16)
                self.sec("sync", dmp)
                for e in ["vector", "scalar", "tensor", "gpsimd"]:
                    def bar(eng):
                        b.w(eng, "ST")
                    self.sec(e, bar)

        self._amax1 = amax1
        self.free("u_tm", "z0b_tm", "z0f", "z0bf", "r0", "delta", "du", "hblk",
                  "yfm", "dA_t", "dBu_t", "ytmp", "state", "dtr_b", "B_b", "C_b",
                  "mst", "vst", "rst", "nmr", "sc4", "bn6", "bn2",
                  "m1t", "r1t", "sc512", "z1t",
                  "pA_trb", "pA_misc", "pA_xp", "pA_xpc", "pA_bb", "pA_cb", "pA_s1", "pA_s2")


    def allreduce_max_scalar(self, amax_col, cin, cout, sxs_col, inv_col, tag):
        b = self
        p_sc = self.ps("p_sc" + tag, [128, 128], F32)
        sc_row = self.sb("sc_row" + tag, [1, 128], F32)
        sc_one = self.sb("sc_one" + tag, [1, 2], F32)

        sc_sq = self.sb("sc_sq" + tag, [128, 128], F32)
        def s0(s):
            b.w(s, "V", "P", "ST")
            b.inc(s.activation(sc_sq[:], self.ident_f[:], AF.Identity,
                               bias=amax_col[:], scale=0.0), "S")
        self.sec("scalar", s0)
        def t1(t):
            b.w(t, "S")
            b.inc(t.transpose(p_sc[:], sc_sq[:], self.ident_f[:]), "P")
        self.sec("tensor", t1)
        def v1(v):
            b.w(v, "P")
            b.inc(v.reduce_max(sc_one[:, 0:1], p_sc[0:1, 0:128], axis=AX.X), "V")
            b.w(v, "V")
        self.sec("vector", v1)
        def g1(g):
            b.w(g, "V")
            b.inc(g.dma_start(out=cin.ap(), in_=sc_one[:, 0:1]), "CC", 16)
            b.w(g, "CC")
            b.inc(g.collective_compute(
                "AllReduce", ALU.max, replica_groups=[list(range(8))],
                ins=[cin.ap()], outs=[cout.ap()]), "CC", 1)
            b.w(g, "CC")
            b.inc(g.dma_start(out=sc_one[:, 1:2], in_=cout.ap()), "CC", 16)
        self.sec("gpsimd", g1)
        def v2(v):
            b.w(v, "CC")
            b.inc(v.reciprocal(sc_one[:, 0:1], sc_one[:, 1:2]), "V")
            b.w(v, "V")
            b.inc(v.tensor_scalar(sc_one[:, 0:1], sc_one[:, 0:1], 32767.0, None,
                                  op0=ALU.mult), "V")
            b.w(v, "V")
            b.inc(v.reciprocal(sc_one[:, 1:2], sc_one[:, 0:1]), "V")
            b.w(v, "V")
        self.sec("vector", v2)
        def t2(t):
            b.w(t, "V")
            b.inc(t.matmul(p_sc[:, 0:2], self.ones1f[:], sc_one[:], start=True,
                           stop=True), "P")
        self.sec("tensor", t2)
        def v3(v):
            b.w(v, "P")
            v.tensor_copy(sxs_col[:], p_sc[:, 0:1])
            b.inc(v.tensor_copy(inv_col[:], p_sc[:, 1:2]), "V")
        self.sec("vector", v3)
        self.free("p_sc" + tag, "sc_row" + tag, "sc_one" + tag, "sc_sq" + tag)

    def phase_B(self):
        b = self
        sx_col = self.sb("sx_col", [128, 1], F32)
        isx_col = self.sb("isx_col", [128, 1], F32)
        sh_col = self.sb("sh_col", [128, 1], F32)
        ish_col = self.sb("ish_col", [128, 1], F32)
        amax2 = self.sb("amax2", [128, 1], F32)
        bin_col = self.sb("bin_col", [128, 32], F32)
        bout_col = self.sb("bout_col", [128, 8], F32)

        self.allreduce_max_scalar(self._amax1, self.cc_in, self.cc_out,
                                  sx_col, isx_col, "1")

        def vz(v):
            b.w(v, "ST", "S")
            b.inc(v.memset(amax2[:], 0.0), "V")
        self.sec("vector", vz)
        def ldb(sync):
            b.w(sync, "ST", "V", "S", "P")
            for j in range(32):
                b.inc(sync.dma_start(out=bin_col[:, j:j + 1],
                                     in_=b.di["bin"][j:j + 1, :]), "L", 16)
            for j in range(8):
                b.inc(sync.dma_start(out=bout_col[:, j:j + 1],
                                     in_=b.di["bout"][j:j + 1, :]), "L", 16)
        self.sec("sync", ldb)

        W_outq = self.sb("W_outq", [128, 32 * 1024], BF16)
        wsrc = self.sb("wsrcB", [128, 1024], F32)
        wq32 = self.sb("wq32", [128, 1024], F32)
        am_c = self.sb("am_cB", [128, 1], F32)
        scl_c = self.sb("scl_cB", [128, 1], F32)
        inv_c = self.sb("inv_cB", [128, 1], F32)
        for et in range(32):
            def lw(sync, et=et):
                b.w(sync, "S", "V")
                if et == 0:
                    b.w(sync, "ST", "P")
                b.inc(sync.dma_start(out=wsrc[:],
                                     in_=b.di["Wout"][et * 128:(et + 1) * 128, :]), "L", 16)
            self.sec("sync", lw)
            def vq(v, et=et):
                b.w(v, "L")
                b.inc(v.reduce_max(am_c[:], wsrc[:], axis=AX.X,
                                   apply_absolute_value=True), "V")
                b.w(v, "V")
                b.inc(v.reciprocal(scl_c[:], am_c[:]), "V")
                b.w(v, "V")
                b.inc(v.tensor_scalar(scl_c[:], scl_c[:], 127.0, None, op0=ALU.mult), "V")
                b.w(v, "V")
                b.inc(v.reciprocal(inv_c[:], scl_c[:]), "V")
                b.w(v, "V")
            self.sec("vector", vq)
            def sq(s, et=et):
                b.w(s, "V")
                b.inc(s.activation(wq32[:], wsrc[:], AF.Identity, scale=scl_c[:]), "S")
            self.sec("scalar", sq)
            def vr(v, et=et):
                b.w(v, "S")
                b.inc(v.tensor_scalar(wq32[:], wq32[:], MAGIC, MAGIC,
                                      op0=ALU.add, op1=ALU.subtract), "V")
            self.sec("vector", vr)
            def sq2(s, et=et):
                b.w(s, "V")
                b.inc(s.activation(W_outq[:, et * 1024:(et + 1) * 1024], wq32[:],
                                   AF.Identity, scale=inv_c[:]), "S")
            self.sec("scalar", sq2)

        W_inq = self.sb("W_inq", [128, 8 * 4096], BF16)
        weT = self.sb("weT", [128, 1024], F32)
        weTq = self.sb("weTq", [128, 1024], BF16)
        p_tw = self.ps("p_twB", [128, 512], F32)
        p_twb = self.ps("p_twbB", [128, 512], BF16)
        for ec in range(32):
            def lw(sync, ec=ec):
                b.w(sync, "P")
                if ec == 0:
                    b.w(sync, "S", "V")
                for g in range(8):
                    b.inc(sync.dma_start(
                        out=wsrc[:, g * 128:(g + 1) * 128],
                        in_=b.di["Win"][g * 128:(g + 1) * 128, ec * 128:(ec + 1) * 128]),
                        "L", 16)
            self.sec("sync", lw)
            for h in range(2):
                def tt(t, ec=ec, h=h):
                    b.w(t, "L", "S")
                    for j in range(4):
                        g = h * 4 + j
                        b.inc(t.transpose(p_tw[:, j * 128:(j + 1) * 128],
                                          wsrc[:, g * 128:(g + 1) * 128],
                                          self.ident_f[:]), "P")
                self.sec("tensor", tt)
                def se(s, ec=ec, h=h):
                    b.w(s, "P")
                    b.inc(s.copy(weT[:, h * 512:(h + 1) * 512], p_tw[:]), "S")
                self.sec("scalar", se)
            def vq(v, ec=ec):
                b.w(v, "S")
                b.inc(v.reduce_max(am_c[:], weT[:], axis=AX.X,
                                   apply_absolute_value=True), "V")
                b.w(v, "V")
                b.inc(v.reciprocal(scl_c[:], am_c[:]), "V")
                b.w(v, "V")
                b.inc(v.tensor_scalar(scl_c[:], scl_c[:], 127.0, None, op0=ALU.mult), "V")
                b.w(v, "V")
                b.inc(v.reciprocal(inv_c[:], scl_c[:]), "V")
                b.w(v, "V")
            self.sec("vector", vq)
            def sq(s, ec=ec):
                b.w(s, "V")
                b.inc(s.activation(wq32[:], weT[:], AF.Identity, scale=scl_c[:]), "S")
            self.sec("scalar", sq)
            def vr(v, ec=ec):
                b.w(v, "S")
                b.inc(v.tensor_scalar(wq32[:], wq32[:], MAGIC, MAGIC,
                                      op0=ALU.add, op1=ALU.subtract), "V")
            self.sec("vector", vr)
            def sq2(s, ec=ec):
                b.w(s, "V")
                b.inc(s.activation(weTq[:], wq32[:], AF.Identity, scale=inv_c[:]), "S")
            self.sec("scalar", sq2)
            for h in range(2):
                def tt2(t, ec=ec, h=h):
                    b.w(t, "S")
                    for j in range(4):
                        g = h * 4 + j
                        b.inc(t.transpose(p_twb[:, j * 128:(j + 1) * 128],
                                          weTq[:, g * 128:(g + 1) * 128],
                                          self.ident_b[:]), "P")
                self.sec("tensor", tt2)
                def se2(s, ec=ec, h=h):
                    b.w(s, "P")
                    for j in range(4):
                        g = h * 4 + j
                        b.inc(s.copy(W_inq[:, g * 4096 + ec * 128:g * 4096 + (ec + 1) * 128],
                                     p_twb[:, j * 128:(j + 1) * 128]), "S")
                self.sec("scalar", se2)

        xf = self.sb("xfB", [128, 4096], F32)
        zq = self.sb("zqB", [128, 4096], BF16)
        hb = self.sb("hbB", [128, 1024], BF16)
        mr0 = self.sb("mr0B", [1, 512], F32)
        rr0 = self.sb("rr0B", [1, 512], F32)
        am_t = self.sb("am_tB", [128, 1], F32)
        p_bc1 = self.ps("p_bc1B", [128, 512], F32)
        p_bc2 = self.ps("p_bc2B", [128, 512], F32)
        p_h = [self.ps("p_hB0", [128, 512], F32), self.ps("p_hB1", [128, 512], F32)]

        pipeB1 = {"Ssz1": 0, "Vvz1": 0, "Ptb": 0, "sg": [], "st": []}
        for ch in range(NCH):
            t0 = ch * T
            def ld(sync, t0=t0, ch=ch):
                if ch == 0:
                    b.w(sync, "V", "S", "ST")
                else:
                    b.wv(sync, "S", pipeB1["Ssz1"])
                    b.wv(sync, "V", pipeB1["Vvz1"])
                    b.wv(sync, "P", pipeB1["Ptb"])
                for g in range(8):
                    b.inc(sync.dma_start(out=xf[:, g * 512:(g + 1) * 512],
                                         in_=b.xh1[g * 128:(g + 1) * 128, t0:t0 + T]),
                          "L", 16)
                b.inc(sync.dma_start(out=mr0[:], in_=b.sth[0:1, t0:t0 + T]), "L", 16)
                b.inc(sync.dma_start(out=rr0[:], in_=b.sth[1:2, t0:t0 + T]), "L", 16)
            self.sec("sync", ld)
            def tb(t, t0=t0, ch=ch):
                b.w(t, "L")
                if ch > 0:
                    b.wv(t, "V", pipeB1["Vvz1"])
                    b.wv(t, "S", pipeB1["Ssz1"])
                b.inc(t.matmul(p_bc1[:], self.ones1f[:], mr0[:], start=True, stop=True), "P")
                b.inc(t.matmul(p_bc2[:], self.ones1f[:], rr0[:], start=True, stop=True), "P")
            self.sec("tensor", tb)
            pipeB1["Ptb"] = self.c["P"]
            def sb_(s, ch=ch):
                b.wv(s, "P", pipeB1["Ptb"])
                b.inc(s.activation(wq32[:, 0:512], p_bc2[:], AF.Identity,
                                   scale=sx_col[:]), "S")
            self.sec("scalar", sb_)
            Ssb = self.c["S"]
            def vz1(v, ch=ch):
                b.wv(v, "S", Ssb)
                if ch > 0:
                    pass
                for g in range(8):
                    gsl = slice(g * 512, (g + 1) * 512)
                    v.tensor_tensor(xf[:, gsl], xf[:, gsl], p_bc1[:], op=ALU.subtract)
                    v.tensor_tensor(xf[:, gsl], xf[:, gsl], wq32[:, 0:512], op=ALU.mult)
                    b.inc(v.tensor_scalar(xf[:, gsl], xf[:, gsl], MAGIC, MAGIC,
                                          op0=ALU.add, op1=ALU.subtract), "V")
            self.sec("vector", vz1)
            pipeB1["Vvz1"] = self.c["V"]
            Vvz1 = self.c["V"]
            def sz1(s, ch=ch):
                b.wv(s, "V", Vvz1)
                if ch > 0:
                    b.w(s, "P")
                for g in range(8):
                    gsl = slice(g * 512, (g + 1) * 512)
                    b.inc(s.activation(zq[:, gsl], xf[:, gsl], AF.Identity,
                                       scale=isx_col[:]), "S")
            self.sec("scalar", sz1)
            pipeB1["Ssz1"] = self.c["S"]
            Ssz1 = self.c["S"]
            for ec in range(32):
                pp = p_h[ec % 2]
                def tm(t, ec=ec, pp=pp, Ssz1=Ssz1):
                    b.wv(t, "S", Ssz1)
                    if len(pipeB1["sg"]) >= 2:
                        b.wv(t, "S", pipeB1["sg"][-2])
                    for g in range(8):
                        b.inc(t.matmul(pp[:], W_inq[:, g * 4096 + ec * 128:
                                                    g * 4096 + (ec + 1) * 128],
                                       zq[:, g * 512:(g + 1) * 512],
                                       start=(g == 0), stop=(g == 7)), "P")
                self.sec("tensor", tm)
                Ptm = self.c["P"]
                def sg(s, ec=ec, pp=pp, Ptm=Ptm):
                    b.wv(s, "P", Ptm)
                    if len(pipeB1["st"]) >= 2:
                        b.wv(s, "ST", pipeB1["st"][-2])
                    b.inc(s.activation(hb[:, (ec % 2) * 512:(ec % 2 + 1) * 512], pp[:],
                                       AF.Gelu, bias=bin_col[:, ec:ec + 1]), "S")
                self.sec("scalar", sg)
                Ssg = self.c["S"]
                pipeB1["sg"].append(Ssg)
                def va(v, ec=ec, Ssg=Ssg):
                    b.wv(v, "S", Ssg)
                    b.inc(v.reduce_max(am_t[:], hb[:, (ec % 2) * 512:(ec % 2 + 1) * 512],
                                       axis=AX.X, apply_absolute_value=True), "V")
                    b.w(v, "V")
                    b.inc(v.tensor_tensor(amax2[:], amax2[:], am_t[:], op=ALU.max), "V")
                self.sec("vector", va)
                Vva = self.c["V"]
                def st(sync, ec=ec, t0=t0, Vva=Vva):
                    b.wv(sync, "V", Vva)
                    b.inc(sync.dma_start(out=b.hh[ec * 128:(ec + 1) * 128, t0:t0 + T],
                                         in_=hb[:, (ec % 2) * 512:(ec % 2 + 1) * 512]),
                          "ST", 16)
                self.sec("sync", st)
                pipeB1["st"].append(self.c["ST"])
            if self.dbg and ch == 0:
                def dmpB(sync):
                    b.w(sync, "V", "S", "P")
                    for nm, tens_ap, shp, dt in [
                            ("sx_col", sx_col[:], [128, 1], F32),
                            ("isx_col", isx_col[:], [128, 1], F32),
                            ("zq", zq[:], [128, 4096], BF16),
                            ("xfq", xf[:], [128, 4096], F32),
                            ("hb31", hb[:], [128, 512], BF16),
                            ("winq", W_inq[:, 0:4096], [128, 4096], BF16),
                            ("amax2c", amax2[:], [128, 1], F32)]:
                        ap = self.nc.dram_tensor("dbgB_" + nm, shp, dt,
                                                 kind="ExternalOutput").ap()
                        b.inc(sync.dma_start(out=ap, in_=tens_ap), "ST", 16)
                self.sec("sync", dmpB)
                for e in ["vector", "scalar", "tensor", "gpsimd"]:
                    def bar(eng):
                        b.w(eng, "ST")
                    self.sec(e, bar)

        self.allreduce_max_scalar(amax2, self.cc_in2, self.cc_out2,
                                  sh_col, ish_col, "2")
        self.free("p_bc1B", "p_bc2B", "p_hB0", "p_hB1")
        self.free("xfB", "zqB", "hbB", "mr0B", "rr0B", "am_tB")
        self.free("W_inq", "weT", "weTq", "p_twB", "p_twbB")

        hq = self.sb("hqB", [128, 32 * 512], BF16)
        u1 = self.sb("u1B", [128, 16 * 512], F32)
        xf1 = self.sb("xf1B", [128, 4096], F32)
        x2 = self.sb("x2B", [128, 4096], F32)
        x2b = self.sb("x2bB", [128, 4096], BF16)
        x2sq = self.sb("x2sqB", [128, 4096], BF16)
        m2t = self.sb("m2tB", [128, 512], F32)
        r2t = self.sb("r2tB", [128, 512], F32)
        s2c = self.sb("s2cB", [128, 512], F32)
        p_o = [self.ps("p_oB0", [128, 512], F32), self.ps("p_oB1", [128, 512], F32)]
        p_s1 = self.ps("p_s1B", [128, 512], F32)
        p_s2 = self.ps("p_s2B", [128, 512], F32)
        hh_r = b.hh.rearrange("(eb p) s -> p eb s", p=128)

        pipeB2 = {"Pto": 0, "Vvo": 0, "so": [], "st2": 0}
        for ch in range(NCH):
            t0 = ch * T
            def ld(sync, t0=t0, ch=ch):
                if ch == 0:
                    b.w(sync, "V", "S", "P", "ST")
                else:
                    b.wv(sync, "P", pipeB2["Pto"])
                    b.wv(sync, "V", pipeB2["Vvo"])
                b.inc(sync.dma_start(out=hq.rearrange("p (eb t) -> p eb t", t=T),
                                     in_=hh_r[:, :, t0:t0 + T]), "L", 16)
                for g in range(8):
                    b.inc(sync.dma_start(out=xf1[:, g * 512:(g + 1) * 512],
                                         in_=b.xh1[g * 128:(g + 1) * 128, t0:t0 + T]),
                          "L", 16)
            self.sec("sync", ld)
            for hf in range(2):
                def vq(v, hf=hf):
                    if hf == 0:
                        b.w(v, "L")
                    b.inc(v.tensor_scalar(u1[:], hq[:, hf * 8192:(hf + 1) * 8192],
                                          sh_col[:], MAGIC, op0=ALU.mult, op1=ALU.add),
                          "V")
                    b.inc(v.tensor_scalar(hq[:, hf * 8192:(hf + 1) * 8192], u1[:],
                                          MAGIC, ish_col[:], op0=ALU.subtract,
                                          op1=ALU.mult), "V")
                self.sec("vector", vq)
            Vvq = self.c["V"]
            for fc in range(8):
                pp = p_o[fc % 2]
                def tm(t, fc=fc, pp=pp, Vvq=Vvq):
                    b.wv(t, "V", Vvq)
                    if len(pipeB2["so"]) >= 2:
                        b.wv(t, "S", pipeB2["so"][-2])
                    for et in range(32):
                        b.inc(t.matmul(pp[:], W_outq[:, et * 1024 + fc * 128:
                                                     et * 1024 + (fc + 1) * 128],
                                       hq[:, et * 512:(et + 1) * 512],
                                       start=(et == 0), stop=(et == 31)), "P")
                self.sec("tensor", tm)
                Ptm = self.c["P"]
                def so(s, fc=fc, pp=pp, Ptm=Ptm):
                    b.wv(s, "P", Ptm)
                    b.wv(s, "ST", pipeB2["st2"])
                    b.inc(s.activation(x2[:, fc * 512:(fc + 1) * 512], pp[:],
                                       AF.Identity, bias=bout_col[:, fc:fc + 1]), "S")
                self.sec("scalar", so)
                Sso = self.c["S"]
                pipeB2["so"].append(Sso)
                def vo(v, fc=fc, Sso=Sso):
                    b.wv(v, "S", Sso)
                    b.inc(v.tensor_tensor(x2[:, fc * 512:(fc + 1) * 512],
                                          x2[:, fc * 512:(fc + 1) * 512],
                                          xf1[:, fc * 512:(fc + 1) * 512],
                                          op=ALU.add), "V")
                self.sec("vector", vo)
            def sst(s):
                b.w(s, "V", "ST")
                for g in range(8):
                    gsl = slice(g * 512, (g + 1) * 512)
                    s.copy(x2b[:, gsl], x2[:, gsl])
                    b.inc(s.activation(x2sq[:, gsl], x2[:, gsl], AF.Square), "S")
            self.sec("scalar", sst)
            def tst(t):
                b.w(t, "S")
                for g in range(8):
                    gsl = slice(g * 512, (g + 1) * 512)
                    b.inc(t.matmul(p_s1[:], self.ones128b[:], x2b[:, gsl],
                                   start=(g == 0), stop=(g == 7)), "P")
                    b.inc(t.matmul(p_s2[:], self.ones128b[:], x2sq[:, gsl],
                                   start=(g == 0), stop=(g == 7)), "P")
            self.sec("tensor", tst)
            def sm2(s):
                b.w(s, "P", "ST")
                s.mul(m2t[:], p_s1[:], self.c_invD)
                b.inc(s.mul(s2c[:], p_s2[:], self.c_invD), "S")
            self.sec("scalar", sm2)
            def vv2(v):
                b.w(v, "S")
                v.tensor_tensor(r2t[:], m2t[:], m2t[:], op=ALU.mult)
                b.inc(v.tensor_tensor(s2c[:], s2c[:], r2t[:], op=ALU.subtract), "V")
            self.sec("vector", vv2)
            def sr2(s):
                b.w(s, "V", "ST")
                b.inc(self.rstd(s, r2t, s2c), "S")
            self.sec("scalar", sr2)
            def st2(sync, t0=t0):
                b.w(sync, "V", "S")
                for g in range(8):
                    gsl = slice(g * 512, (g + 1) * 512)
                    b.inc(sync.dma_start(out=b.xh2[g * 128:(g + 1) * 128, t0:t0 + T],
                                         in_=x2[:, gsl]), "ST", 16)
                b.inc(sync.dma_start(out=b.sth[2:3, t0:t0 + T], in_=m2t[0:1, :]), "ST", 16)
                b.inc(sync.dma_start(out=b.sth[3:4, t0:t0 + T], in_=r2t[0:1, :]), "ST", 16)
            self.sec("sync", st2)
            pipeB2["Pto"] = self.c["P"]
            pipeB2["Vvo"] = self.c["V"]
            pipeB2["st2"] = self.c["ST"]
            if self.dbg and ch == 0:
                def dmpB2(sync):
                    b.w(sync, "V", "S", "P")
                    for nm, tens_ap, shp, dt in [
                            ("sh_col", sh_col[:], [128, 1], F32),
                            ("hq0", hq[:], [128, 16384], BF16),
                            ("x2", x2[:], [128, 4096], F32),
                            ("m2t", m2t[:], [128, 512], F32),
                            ("woutq", W_outq[:, 0:4096], [128, 4096], BF16),
                            ("woutqf", W_outq[:], [128, 32768], BF16),
                            ("xf1", xf1[:], [128, 4096], F32),
                            ("r2t", r2t[:], [128, 512], F32)]:
                        ap = self.nc.dram_tensor("dbgB2_" + nm, shp, dt,
                                                 kind="ExternalOutput").ap()
                        b.inc(sync.dma_start(out=ap, in_=tens_ap), "ST", 16)
                self.sec("sync", dmpB2)
                for e in ["vector", "scalar", "tensor", "gpsimd"]:
                    def bar(eng):
                        b.w(eng, "ST")
                    self.sec(e, bar)

        self.free("p_oB0", "p_oB1", "p_s1B", "p_s2B")
        self.free("hqB", "u1B", "xf1B", "x2B", "x2bB", "x2sqB", "m2tB", "r2tB", "s2cB")
        self.free("W_outq", "wsrcB", "wq32", "am_cB", "scl_cB", "inv_cB")
        self.free("sx_col", "isx_col", "sh_col", "ish_col", "amax2", "bin_col",
                  "bout_col")


    def phase_C(self):
        b = self
        self.probe_xh2("postB")
        cmax = self.sb("cmaxC", [128, 8], F32)
        ncmax = self.sb("ncmaxC", [128, 8], F32)
        csum = self.sb("csumC", [128, 8], F32)
        rcs = self.sb("rcsC", [128, 8], F32)
        bo_col = self.sb("bo_colC", [128, 8], F32)
        gam_col = self.sb("gam_colC", [128, 8], F32)
        bet_col = self.sb("bet_colC", [128, 8], F32)
        red1 = self.sb("red1C", [128, 1], F32)
        ctxb = self.sb("ctxbC", [128, 8 * 1024], BF16)

        def vz(v):
            b.w(v, "ST", "S")
            v.memset(cmax[:], -1e30)
            b.inc(v.memset(csum[:], 0.0), "V")
        self.sec("vector", vz)
        def ldc(sync):
            b.w(sync, "ST", "V", "S", "P")
            for j in range(8):
                b.inc(sync.dma_start(out=bo_col[:, j:j + 1], in_=b.di["bo"][j:j + 1, :]), "L", 16)
                b.inc(sync.dma_start(out=gam_col[:, j:j + 1], in_=b.di["gam"][j:j + 1, :]), "L", 16)
                b.inc(sync.dma_start(out=bet_col[:, j:j + 1], in_=b.di["bet"][j:j + 1, :]), "L", 16)
        self.sec("sync", ldc)

        Wqb = self.sb("WqbC", [128, 8 * 1024], BF16)
        Wkb = self.sb("WkbC", [128, 8 * 1024], BF16)
        xf = self.sb("xfC", [128, 4096], F32)
        z2b = self.sb("z2bC", [128, 4096], BF16)
        kt = self.sb("ktC", [128, 1024], F32)
        qtm = self.sb("qtmC", [128, 1024], F32)
        qnb = self.sb("qnbC", [128, 1024], BF16)
        qfm = self.sb("qfmC", [128, 4096], BF16)
        mrow = self.sb("mrowC", [1, 512], F32)
        rrow = self.sb("rrowC", [1, 512], F32)
        nmx = self.sb("nmxC", [128, 1], F32)
        ssum = self.sb("ssumC", [128, 1], F32)
        p_bc1 = self.ps("p_bc1C", [128, 512], F32)
        p_bc2 = self.ps("p_bc2C", [128, 512], F32)
        p_k = [self.ps("p_kC0", [128, 512], F32), self.ps("p_kC1", [128, 512], F32)]
        p_q = [self.ps("p_qC0", [128, 512], F32), self.ps("p_qC1", [128, 512], F32)]
        p_trb = self.ps("p_trbC", [128, 512], BF16)

        def ldw(g):
            b.w(g, "V", "S", "P", "ST")
            for i in range(8):
                b.inc(g.dma_start(out=Wqb[:, i * 1024:(i + 1) * 1024],
                                  in_=b.di["Wq"][i * 128:(i + 1) * 128, :]), "G", 16)
                b.inc(g.dma_start(out=Wkb[:, i * 1024:(i + 1) * 1024],
                                  in_=b.di["Wk"][i * 128:(i + 1) * 128, :]), "G", 16)
        self.sec("gpsimd", ldw)

        pipeC1 = {"Ssz2": 0, "Vvz2": 0, "Ptb": 0, "sk": [], "stk": [], "vk": [],
                  "sq": {0: [], 1: []}, "Pend": 0, "Vend": 0, "Send": 0}
        for ch in range(NCH):
            t0 = ch * T
            def ld(sync, t0=t0, ch=ch):
                if ch == 0:
                    b.w(sync, "V", "S", "P", "ST")
                else:
                    b.wv(sync, "S", pipeC1["Send"])
                    b.wv(sync, "P", pipeC1["Ptb"])
                for g in range(8):
                    b.inc(sync.dma_start(out=xf[:, g * 512:(g + 1) * 512],
                                         in_=b.xh2[g * 128:(g + 1) * 128, t0:t0 + T]), "L", 16)
                b.inc(sync.dma_start(out=mrow[:], in_=b.sth[2:3, t0:t0 + T]), "L", 16)
                b.inc(sync.dma_start(out=rrow[:], in_=b.sth[3:4, t0:t0 + T]), "L", 16)
            self.sec("sync", ld)
            def tb(t, ch=ch):
                b.w(t, "L")
                if ch == 0:
                    b.w(t, "S", "G")
                else:
                    b.wv(t, "V", pipeC1["Vvz2"])
                b.inc(t.matmul(p_bc1[:], self.ones1f[:], mrow[:], start=True, stop=True), "P")
                b.inc(t.matmul(p_bc2[:], self.ones1f[:], rrow[:], start=True, stop=True), "P")
            self.sec("tensor", tb)
            pipeC1["Ptb"] = self.c["P"]
            Ptb = self.c["P"]
            def vz2(v, Ptb=Ptb):
                b.wv(v, "P", Ptb)
                for g in range(8):
                    gsl = slice(g * 512, (g + 1) * 512)
                    v.tensor_tensor(xf[:, gsl], xf[:, gsl], p_bc1[:], op=ALU.subtract)
                    b.inc(v.tensor_tensor(xf[:, gsl], xf[:, gsl], p_bc2[:], op=ALU.mult), "V")
            self.sec("vector", vz2)
            pipeC1["Vvz2"] = self.c["V"]
            Vvz2 = self.c["V"]
            def sz2(s, Vvz2=Vvz2, ch=ch):
                b.wv(s, "V", Vvz2)
                if ch > 0:
                    b.wv(s, "P", pipeC1["Pend"])
                for g in range(8):
                    gsl = slice(g * 512, (g + 1) * 512)
                    b.inc(s.copy(z2b[:, gsl], xf[:, gsl]), "S")
            self.sec("scalar", sz2)
            pipeC1["Ssz2"] = self.c["S"]
            Ssz2 = self.c["S"]
            for ec in range(8):
                def tk(t, ec=ec, Ssz2=Ssz2):
                    b.wv(t, "S", Ssz2)
                    if len(pipeC1["sk"]) >= 2:
                        b.wv(t, "S", pipeC1["sk"][-2])
                    for g in range(8):
                        b.inc(t.matmul(p_k[ec % 2][:],
                                       Wkb[:, g * 1024 + ec * 128:g * 1024 + (ec + 1) * 128],
                                       z2b[:, g * 512:(g + 1) * 512],
                                       start=(g == 0), stop=(g == 7)), "P")
                self.sec("tensor", tk)
                Ptk = self.c["P"]
                def sk(s, ec=ec, Ptk=Ptk):
                    b.wv(s, "P", Ptk)
                    if len(pipeC1["stk"]) >= 2:
                        b.wv(s, "ST", pipeC1["stk"][-2])
                    if len(pipeC1["vk"]) >= 2:
                        b.wv(s, "V", pipeC1["vk"][-2])
                    b.inc(s.copy(kt[:, (ec % 2) * 512:(ec % 2 + 1) * 512], p_k[ec % 2][:]), "S")
                self.sec("scalar", sk)
                Ssk = self.c["S"]
                pipeC1["sk"].append(Ssk)
                def vk(v, ec=ec, Ssk=Ssk):
                    b.wv(v, "S", Ssk)
                    b.inc(v.reduce_max(red1[:], kt[:, (ec % 2) * 512:(ec % 2 + 1) * 512],
                                       axis=AX.X), "V")
                    b.w(v, "V")
                    b.inc(v.tensor_tensor(cmax[:, ec:ec + 1], cmax[:, ec:ec + 1],
                                          red1[:], op=ALU.max), "V")
                self.sec("vector", vk)
                pipeC1["vk"].append(self.c["V"])
                Vvk = self.c["V"]
                def stk(sync, ec=ec, t0=t0, Vvk=Vvk):
                    b.wv(sync, "V", Vvk)
                    b.inc(sync.dma_start(out=b.kh[ec * 128:(ec + 1) * 128, t0:t0 + T],
                                         in_=kt[:, (ec % 2) * 512:(ec % 2 + 1) * 512]),
                          "ST", 16)
                self.sec("sync", stk)
                pipeC1["stk"].append(self.c["ST"])
            for tt in range(4):
                for eh in range(2):
                    def tq(t, tt=tt, eh=eh, Ssz2=Ssz2):
                        b.wv(t, "S", Ssz2)
                        if len(pipeC1["sq"][eh]) >= 1:
                            b.wv(t, "S", pipeC1["sq"][eh][-1])
                        for g in range(8):
                            b.inc(t.matmul(p_q[eh][:],
                                           z2b[:, g * 512 + tt * 128: g * 512 + (tt + 1) * 128],
                                           Wqb[:, g * 1024 + eh * 512: g * 1024 + (eh + 1) * 512],
                                           start=(g == 0), stop=(g == 7)), "P")
                    self.sec("tensor", tq)
                    Ptq = self.c["P"]
                    def sq(s, tt=tt, eh=eh, Ptq=Ptq):
                        b.wv(s, "P", Ptq)
                        b.inc(s.copy(qtm[:, eh * 512:(eh + 1) * 512], p_q[eh][:]), "S")
                    self.sec("scalar", sq)
                    pipeC1["sq"][eh].append(self.c["S"])
                def vqm(v, tt=tt):
                    b.w(v, "S")
                    b.inc(v.reduce_max(nmx[:], qtm[:], axis=AX.X, negate=True), "V")
                    b.w(v, "V")
                self.sec("vector", vqm)
                def sqe(s, tt=tt):
                    b.w(s, "V")
                    b.inc(s.activation(qtm[:], qtm[:], AF.Exp, bias=nmx[:]), "S")
                self.sec("scalar", sqe)
                def vqs(v, tt=tt):
                    b.w(v, "S")
                    b.inc(v.reduce_sum(ssum[:], qtm[:], axis=AX.X), "V")
                    b.w(v, "V")
                    b.inc(v.reciprocal(ssum[:], ssum[:]), "V")
                    b.w(v, "V")
                    b.inc(v.tensor_scalar(ssum[:], ssum[:], 0.03125, None, op0=ALU.mult), "V")
                self.sec("vector", vqs)
                def sqn(s, tt=tt):
                    b.w(s, "V")
                    b.inc(s.activation(qnb[:], qtm[:], AF.Identity, scale=ssum[:]), "S")
                self.sec("scalar", sqn)
                for eh in range(2):
                    def tqt(t, tt=tt, eh=eh):
                        b.w(t, "S")
                        for j in range(4):
                            g = eh * 4 + j
                            b.inc(t.transpose(p_trb[:, j * 128:(j + 1) * 128],
                                              qnb[:, g * 128:(g + 1) * 128],
                                              self.ident_b[:]), "P")
                    self.sec("tensor", tqt)
                    def sqt(s, tt=tt, eh=eh):
                        b.w(s, "P")
                        for j in range(4):
                            g = eh * 4 + j
                            b.inc(s.copy(qfm[:, g * 512 + tt * 128: g * 512 + (tt + 1) * 128],
                                         p_trb[:, j * 128:(j + 1) * 128]), "S")
                    self.sec("scalar", sqt)
            def stq(sync, t0=t0):
                b.w(sync, "S")
                for g in range(8):
                    b.inc(sync.dma_start(out=b.qh[g * 128:(g + 1) * 128, t0:t0 + T],
                                         in_=qfm[:, g * 512:(g + 1) * 512]), "ST", 16)
            self.sec("sync", stq)
            pipeC1["Pend"] = self.c["P"]
            pipeC1["Send"] = self.c["S"]
            if ch == 0:
                self.dump([("C_z2b", z2b[:], [128, 4096], BF16),
                           ("C_kt7", kt[:, 512:1024], [128, 512], F32),
                           ("C_qfm", qfm[:], [128, 4096], BF16),
                           ("C_qtm3", qtm[:], [128, 1024], F32)])

        self.probe_xh2("postC1")
        self.free("p_bc1C", "p_bc2C", "p_kC0", "p_kC1", "p_qC0", "p_qC1", "p_trbC")
        self.free("WqbC", "WkbC", "xfC", "z2bC", "ktC", "qtmC", "qnbC", "qfmC",
                  "mrowC", "rrowC", "nmxC", "ssumC")

        def vnc(v):
            b.w(v, "S", "ST")
            b.inc(v.tensor_scalar(ncmax[:], cmax[:], -1.0, None, op0=ALU.mult), "V")
        self.sec("vector", vnc)

        Wvb = self.sb("WvbC", [128, 8 * 1024], BF16)
        xfr = self.sb("xfrC", [128, 8 * 512], F32)
        z2br = self.sb("z2brC", [128, 8 * 1024], BF16)
        kfr = self.sb("kfrC", [128, 8 * 512], F32)
        kxb = self.sb("kxbC", [128, 8 * 1024], BF16)
        kxT = self.sb("kxTC", [128, 8 * 1024], BF16)
        vtm = self.sb("vtmC", [128, 8 * 1024], BF16)
        ctxf = self.sb("ctxfC", [128, 8 * 1024], F32)
        mrow2 = self.sb("mrow2C", [1, 1024], F32)
        rrow2 = self.sb("rrow2C", [1, 1024], F32)
        p_bc1 = self.ps("p_bc1C2", [128, 512], F32)
        p_bc2 = self.ps("p_bc2C2", [128, 512], F32)
        p_v = [self.ps("p_vC0", [128, 512], F32), self.ps("p_vC1", [128, 512], F32)]
        p_cx = [self.ps("p_cxC0", [128, 512], F32), self.ps("p_cxC1", [128, 512], F32)]
        p_trb2 = self.ps("p_trbC2", [128, 512], BF16)

        def ldwv(g):
            b.w(g, "V", "S", "P", "ST")
            for i in range(8):
                b.inc(g.dma_start(out=Wvb[:, i * 1024:(i + 1) * 1024],
                                  in_=b.di["Wv"][i * 128:(i + 1) * 128, :]), "G", 16)
        self.sec("gpsimd", ldwv)
        def vzc(v):
            b.w(v, "ST", "S", "P")
            b.inc(v.memset(ctxf[:], 0.0), "V")
        self.sec("vector", vzc)

        RT = 1024
        for rn in range(S // RT):
            r0_ = rn * RT
            def ldr(sync, r0_=r0_):
                b.w(sync, "V", "S", "P", "ST")
                b.inc(sync.dma_start(out=mrow2[:], in_=b.sth[2:3, r0_:r0_ + RT]), "L", 16)
                b.inc(sync.dma_start(out=rrow2[:], in_=b.sth[3:4, r0_:r0_ + RT]), "L", 16)
            self.sec("sync", ldr)
            for hh in range(2):
                h0 = r0_ + hh * 512
                def ldh(sync, h0=h0, hh=hh):
                    if hh == 1:
                        b.w(sync, "V", "S")
                    for g in range(8):
                        b.inc(sync.dma_start(out=xfr[:, g * 512:(g + 1) * 512],
                                             in_=b.xh2[g * 128:(g + 1) * 128, h0:h0 + 512]),
                              "L", 16)
                        b.inc(sync.dma_start(out=kfr[:, g * 512:(g + 1) * 512],
                                             in_=b.kh[g * 128:(g + 1) * 128, h0:h0 + 512]),
                              "L", 16)
                self.sec("sync", ldh)
                def tb(t, hh=hh):
                    b.w(t, "L", "V", "G")
                    b.inc(t.matmul(p_bc1[:], self.ones1f[:], mrow2[:, hh * 512:(hh + 1) * 512],
                                   start=True, stop=True), "P")
                    b.inc(t.matmul(p_bc2[:], self.ones1f[:], rrow2[:, hh * 512:(hh + 1) * 512],
                                   start=True, stop=True), "P")
                self.sec("tensor", tb)
                def vzr(v, hh=hh):
                    b.w(v, "P")
                    for g in range(8):
                        sl = slice(g * 512, (g + 1) * 512)
                        v.tensor_tensor(xfr[:, sl], xfr[:, sl], p_bc1[:], op=ALU.subtract)
                        b.inc(v.tensor_tensor(xfr[:, sl], xfr[:, sl], p_bc2[:], op=ALU.mult), "V")
                self.sec("vector", vzr)
                def szr(s, hh=hh):
                    b.w(s, "V")
                    for g in range(8):
                        b.inc(s.copy(z2br[:, g * 1024 + hh * 512:g * 1024 + (hh + 1) * 512],
                                     xfr[:, g * 512:(g + 1) * 512]), "S")
                self.sec("scalar", szr)
                def skx(s, hh=hh):
                    for g in range(8):
                        b.inc(s.activation(kxb[:, g * 1024 + hh * 512:g * 1024 + (hh + 1) * 512],
                                           kfr[:, g * 512:(g + 1) * 512], AF.Exp,
                                           bias=ncmax[:, g:g + 1]), "S")
                self.sec("scalar", skx)
            def vcs(v):
                b.w(v, "S")
                for g in range(8):
                    sl = slice(g * 1024, (g + 1) * 1024)
                    b.inc(v.reduce_sum(red1[:], kxb[:, sl], axis=AX.X), "V")
                    b.w(v, "V")
                    b.inc(v.tensor_tensor(csum[:, g:g + 1], csum[:, g:g + 1],
                                          red1[:], op=ALU.add), "V")
            self.sec("vector", vcs)
            for tb8 in range(8):
                def tkt(t, tb8=tb8):
                    b.w(t, "S")
                    if tb8 == 0:
                        b.w(t, "V")
                    for g in range(4):
                        b.inc(t.transpose(p_trb2[:, g * 128:(g + 1) * 128],
                                          kxb[:, g * 1024 + tb8 * 128: g * 1024 + (tb8 + 1) * 128],
                                          self.ident_b[:]), "P")
                self.sec("tensor", tkt)
                def skt(s, tb8=tb8):
                    b.w(s, "P")
                    for g in range(4):
                        b.inc(s.copy(kxT[:, tb8 * 1024 + g * 128: tb8 * 1024 + (g + 1) * 128],
                                     p_trb2[:, g * 128:(g + 1) * 128]), "S")
                self.sec("scalar", skt)
                def tkt2(t, tb8=tb8):
                    b.w(t, "S")
                    for g in range(4, 8):
                        b.inc(t.transpose(p_trb2[:, (g - 4) * 128:(g - 3) * 128],
                                          kxb[:, g * 1024 + tb8 * 128: g * 1024 + (tb8 + 1) * 128],
                                          self.ident_b[:]), "P")
                self.sec("tensor", tkt2)
                def skt2(s, tb8=tb8):
                    b.w(s, "P")
                    for g in range(4, 8):
                        b.inc(s.copy(kxT[:, tb8 * 1024 + g * 128: tb8 * 1024 + (g + 1) * 128],
                                     p_trb2[:, (g - 4) * 128:(g - 3) * 128]), "S")
                self.sec("scalar", skt2)
            Szv = self.c["S"]
            svl = {0: [], 1: []}
            for tb8 in range(8):
                for eh in range(2):
                    def tv(t, tb8=tb8, eh=eh, Szv=Szv):
                        b.wv(t, "S", Szv)
                        if len(svl[eh]) >= 1:
                            b.wv(t, "S", svl[eh][-1])
                        for g in range(8):
                            b.inc(t.matmul(p_v[eh][:],
                                           z2br[:, g * 1024 + tb8 * 128: g * 1024 + (tb8 + 1) * 128],
                                           Wvb[:, g * 1024 + eh * 512: g * 1024 + (eh + 1) * 512],
                                           start=(g == 0), stop=(g == 7)), "P")
                    self.sec("tensor", tv)
                    Ptv = self.c["P"]
                    def sv(s, tb8=tb8, eh=eh, Ptv=Ptv):
                        b.wv(s, "P", Ptv)
                        b.inc(s.copy(vtm[:, tb8 * 1024 + eh * 512: tb8 * 1024 + (eh + 1) * 512],
                                     p_v[eh][:]), "S")
                    self.sec("scalar", sv)
                    svl[eh].append(self.c["S"])
            Sall = self.c["S"]
            vcxl = {0: [], 1: []}
            for g in range(8):
                for eh in range(2):
                    def tcx(t, g=g, eh=eh, Sall=Sall):
                        b.wv(t, "S", Sall)
                        if len(vcxl[eh]) >= 1:
                            b.wv(t, "V", vcxl[eh][-1])
                        for tb8 in range(8):
                            b.inc(t.matmul(p_cx[eh][:],
                                           kxT[:, tb8 * 1024 + g * 128: tb8 * 1024 + (g + 1) * 128],
                                           vtm[:, tb8 * 1024 + eh * 512: tb8 * 1024 + (eh + 1) * 512],
                                           start=(tb8 == 0), stop=(tb8 == 7)), "P")
                    self.sec("tensor", tcx)
                    Ptcx = self.c["P"]
                    def vcx(v, g=g, eh=eh, Ptcx=Ptcx):
                        b.wv(v, "P", Ptcx)
                        sl = slice(g * 1024 + eh * 512, g * 1024 + (eh + 1) * 512)
                        b.inc(v.tensor_tensor(ctxf[:, sl], ctxf[:, sl], p_cx[eh][:],
                                              op=ALU.add), "V")
                    self.sec("vector", vcx)
                    vcxl[eh].append(self.c["V"])
        def vrc(v):
            b.w(v, "V")
            b.inc(v.reciprocal(rcs[:], csum[:]), "V")
            b.w(v, "V")
        self.sec("vector", vrc)
        def snc(s):
            b.w(s, "V")
            for g in range(8):
                sl = slice(g * 1024, (g + 1) * 1024)
                b.inc(s.activation(ctxb[:, sl], ctxf[:, sl], AF.Identity,
                                   scale=rcs[:, g:g + 1]), "S")
        self.sec("scalar", snc)
        def ldfresh(sync):
            b.w(sync, "V", "S", "P")
            for g in range(8):
                b.inc(sync.dma_start(out=xfr[:, g * 512:(g + 1) * 512],
                                     in_=b.xh2[g * 128:(g + 1) * 128, 7680:8192]), "L", 16)
        if self.dbg:
            self.sec("sync", ldfresh)
        self.dump([("C_ctxb", ctxb[:, 0:4096], [128, 4096], BF16),
                   ("C_z2br", z2br[:], [128, 8192], BF16),
                   ("C_kfr", kfr[:], [128, 4096], F32),
                   ("C_x2re", xfr[:], [128, 4096], F32),
                   ("C_csum", csum[:], [128, 8], F32),
                   ("C_cmax", cmax[:], [128, 8], F32),
                   ("C_kxb0", kxb[:], [128, 8192], BF16),
                   ("C_kxT0", kxT[:], [128, 8192], BF16),
                   ("C_vtm0", vtm[:], [128, 8192], BF16)])
        self.free("p_bc1C2", "p_bc2C2", "p_vC0", "p_vC1", "p_cxC0", "p_cxC1", "p_trbC2")
        self.free("WvbC", "xfrC", "z2brC", "kfrC", "kxbC", "kxTC", "vtmC", "ctxfC",
                  "mrow2C", "rrow2C")

        Wob = self.sb("WobC", [128, 8 * 1024], BF16)
        xf3 = self.sb("xf3C", [128, 4096], F32)
        z2f = self.sb("z2fC", [128, 4096], F32)
        qf3 = self.sb("qf3C", [128, 4096], BF16)
        attnb = self.sb("attnbC", [128, 4096], BF16)
        u2 = self.sb("u2C", [128, 4096], F32)
        u2b = self.sb("u2bC", [128, 4096], BF16)
        u2sq = self.sb("u2sqC", [128, 4096], BF16)
        x3b = self.sb("x3bC", [128, 4096], BF16)
        x3sq = self.sb("x3sqC", [128, 4096], BF16)
        m3t = self.sb("m3tC", [128, 512], F32)
        r3t = self.sb("r3tC", [128, 512], F32)
        s3c = self.sb("s3cC", [128, 512], F32)
        mrow3 = self.sb("mrow3C", [1, 512], F32)
        rrow3 = self.sb("rrow3C", [1, 512], F32)
        p_bc1 = self.ps("p_bc1C3", [128, 512], F32)
        p_bc2 = self.ps("p_bc2C3", [128, 512], F32)
        p_a = [self.ps("p_aC0", [128, 512], F32), self.ps("p_aC1", [128, 512], F32)]
        p_s1 = self.ps("p_s1C", [128, 512], F32)
        p_s2 = self.ps("p_s2C", [128, 512], F32)

        def ldwo(g):
            b.w(g, "V", "S", "P", "ST")
            for i in range(8):
                b.inc(g.dma_start(out=Wob[:, i * 1024:(i + 1) * 1024],
                                  in_=b.di["Wo"][i * 128:(i + 1) * 128, :]), "G", 16)
        self.sec("gpsimd", ldwo)

        pipeC3 = {"Pend": 0, "Vvx3": 0, "Send": 0, "sa": [], "so": [], "st3": 0}
        for ch in range(NCH):
            t0 = ch * T
            def ld(sync, t0=t0, ch=ch):
                if ch == 0:
                    b.w(sync, "V", "S", "P", "ST")
                else:
                    b.wv(sync, "P", pipeC3["Pend"])
                    b.wv(sync, "V", pipeC3["Vvx3"])
                    b.wv(sync, "S", pipeC3["Send"])
                for g in range(8):
                    b.inc(sync.dma_start(out=xf3[:, g * 512:(g + 1) * 512],
                                         in_=b.xh2[g * 128:(g + 1) * 128, t0:t0 + T]), "L", 16)
                    b.inc(sync.dma_start(out=qf3[:, g * 512:(g + 1) * 512],
                                         in_=b.qh[g * 128:(g + 1) * 128, t0:t0 + T]), "L", 16)
                b.inc(sync.dma_start(out=mrow3[:], in_=b.sth[2:3, t0:t0 + T]), "L", 16)
                b.inc(sync.dma_start(out=rrow3[:], in_=b.sth[3:4, t0:t0 + T]), "L", 16)
            self.sec("sync", ld)
            def tb(t, ch=ch):
                b.w(t, "L")
                if ch == 0:
                    b.w(t, "V", "G")
                b.inc(t.matmul(p_bc1[:], self.ones1f[:], mrow3[:], start=True, stop=True), "P")
                b.inc(t.matmul(p_bc2[:], self.ones1f[:], rrow3[:], start=True, stop=True), "P")
            self.sec("tensor", tb)
            Ptb3 = self.c["P"]
            def vz3(v, Ptb3=Ptb3):
                b.wv(v, "P", Ptb3)
                for g in range(8):
                    gsl = slice(g * 512, (g + 1) * 512)
                    v.tensor_tensor(z2f[:, gsl], xf3[:, gsl], p_bc1[:], op=ALU.subtract)
                    b.inc(v.tensor_tensor(z2f[:, gsl], z2f[:, gsl], p_bc2[:], op=ALU.mult), "V")
            self.sec("vector", vz3)
            for ec in range(8):
                pp = p_a[ec % 2]
                def ta(t, ec=ec, pp=pp):
                    b.w(t, "L")
                    if len(pipeC3["sa"]) >= 2:
                        b.wv(t, "S", pipeC3["sa"][-2])
                    for g in range(8):
                        b.inc(t.matmul(pp[:], ctxb[:, g * 1024 + ec * 128: g * 1024 + (ec + 1) * 128],
                                       qf3[:, g * 512:(g + 1) * 512],
                                       start=(g == 0), stop=(g == 7)), "P")
                self.sec("tensor", ta)
                Pta = self.c["P"]
                def sa(s, ec=ec, pp=pp, Pta=Pta):
                    b.wv(s, "P", Pta)
                    b.inc(s.copy(attnb[:, ec * 512:(ec + 1) * 512], pp[:]), "S")
                self.sec("scalar", sa)
                pipeC3["sa"].append(self.c["S"])
            Ssa7 = self.c["S"]
            for fc in range(8):
                pp = p_a[fc % 2]
                def to(t, fc=fc, pp=pp, Ssa7=Ssa7):
                    b.wv(t, "S", Ssa7)
                    if len(pipeC3["so"]) >= 2:
                        b.wv(t, "S", pipeC3["so"][-2])
                    for g in range(8):
                        b.inc(t.matmul(pp[:], Wob[:, g * 1024 + fc * 128: g * 1024 + (fc + 1) * 128],
                                       attnb[:, g * 512:(g + 1) * 512],
                                       start=(g == 0), stop=(g == 7)), "P")
                self.sec("tensor", to)
                Pto = self.c["P"]
                def so(s, fc=fc, pp=pp, Pto=Pto):
                    b.wv(s, "P", Pto)
                    b.wv(s, "ST", pipeC3["st3"])
                    b.inc(s.activation(u2[:, fc * 512:(fc + 1) * 512], pp[:],
                                       AF.Identity, bias=bo_col[:, fc:fc + 1]), "S")
                self.sec("scalar", so)
                Sso3 = self.c["S"]
                pipeC3["so"].append(Sso3)
                def vo(v, fc=fc, Sso3=Sso3):
                    b.wv(v, "S", Sso3)
                    b.inc(v.tensor_tensor(u2[:, fc * 512:(fc + 1) * 512],
                                          u2[:, fc * 512:(fc + 1) * 512],
                                          z2f[:, fc * 512:(fc + 1) * 512], op=ALU.add), "V")
                self.sec("vector", vo)
            def sst(s):
                b.w(s, "V")
                for g in range(8):
                    gsl = slice(g * 512, (g + 1) * 512)
                    s.copy(u2b[:, gsl], u2[:, gsl])
                    b.inc(s.activation(u2sq[:, gsl], u2[:, gsl], AF.Square), "S")
            self.sec("scalar", sst)
            def tst(t):
                b.w(t, "S")
                for g in range(8):
                    gsl = slice(g * 512, (g + 1) * 512)
                    b.inc(t.matmul(p_s1[:], self.ones128b[:], u2b[:, gsl],
                                   start=(g == 0), stop=(g == 7)), "P")
                    b.inc(t.matmul(p_s2[:], self.ones128b[:], u2sq[:, gsl],
                                   start=(g == 0), stop=(g == 7)), "P")
            self.sec("tensor", tst)
            def smu(s):
                b.w(s, "P")
                s.mul(m3t[:], p_s1[:], self.c_invD)
                b.inc(s.mul(s3c[:], p_s2[:], self.c_invD), "S")
            self.sec("scalar", smu)
            def vvu(v):
                b.w(v, "S")
                v.tensor_tensor(r3t[:], m3t[:], m3t[:], op=ALU.mult)
                b.inc(v.tensor_tensor(s3c[:], s3c[:], r3t[:], op=ALU.subtract), "V")
            self.sec("vector", vvu)
            def sru(s):
                b.w(s, "V")
                b.inc(self.rstd(s, r3t, s3c), "S")
            self.sec("scalar", sru)
            for g in range(8):
                def vap(v, g=g):
                    if g == 0:
                        b.w(v, "S")
                    gsl = slice(g * 512, (g + 1) * 512)
                    v.tensor_tensor(u2[:, gsl], u2[:, gsl], m3t[:], op=ALU.subtract)
                    b.inc(v.tensor_tensor(u2[:, gsl], u2[:, gsl], r3t[:], op=ALU.mult), "V")
                self.sec("vector", vap)
            def sgb(s):
                b.w(s, "V", "ST")
                for g in range(8):
                    gsl = slice(g * 512, (g + 1) * 512)
                    b.inc(s.activation(u2[:, gsl], u2[:, gsl], AF.Identity,
                                       scale=gam_col[:, g:g + 1], bias=bet_col[:, g:g + 1]), "S")
            self.sec("scalar", sgb)
            def vx3(v):
                b.w(v, "S")
                for g in range(8):
                    gsl = slice(g * 512, (g + 1) * 512)
                    b.inc(v.tensor_tensor(u2[:, gsl], u2[:, gsl], xf3[:, gsl], op=ALU.add), "V")
            self.sec("vector", vx3)
            def sst3(s):
                b.w(s, "V")
                for g in range(8):
                    gsl = slice(g * 512, (g + 1) * 512)
                    s.copy(x3b[:, gsl], u2[:, gsl])
                    b.inc(s.activation(x3sq[:, gsl], u2[:, gsl], AF.Square), "S")
            self.sec("scalar", sst3)
            def tst3(t):
                b.w(t, "S")
                for g in range(8):
                    gsl = slice(g * 512, (g + 1) * 512)
                    b.inc(t.matmul(p_s1[:], self.ones128b[:], x3b[:, gsl],
                                   start=(g == 0), stop=(g == 7)), "P")
                    b.inc(t.matmul(p_s2[:], self.ones128b[:], x3sq[:, gsl],
                                   start=(g == 0), stop=(g == 7)), "P")
            self.sec("tensor", tst3)
            def smu3(s):
                b.w(s, "P", "ST")
                s.mul(m3t[:], p_s1[:], self.c_invD)
                b.inc(s.mul(s3c[:], p_s2[:], self.c_invD), "S")
            self.sec("scalar", smu3)
            def vvu3(v):
                b.w(v, "S")
                v.tensor_tensor(r3t[:], m3t[:], m3t[:], op=ALU.mult)
                b.inc(v.tensor_tensor(s3c[:], s3c[:], r3t[:], op=ALU.subtract), "V")
            self.sec("vector", vvu3)
            def sru3(s):
                b.w(s, "V", "ST")
                b.inc(self.rstd(s, r3t, s3c), "S")
            self.sec("scalar", sru3)
            def st3(sync, t0=t0):
                b.w(sync, "V", "S")
                for g in range(8):
                    gsl = slice(g * 512, (g + 1) * 512)
                    b.inc(sync.dma_start(out=b.xh3[g * 128:(g + 1) * 128, t0:t0 + T],
                                         in_=u2[:, gsl]), "ST", 16)
                b.inc(sync.dma_start(out=b.sth[4:5, t0:t0 + T], in_=m3t[0:1, :]), "ST", 16)
                b.inc(sync.dma_start(out=b.sth[5:6, t0:t0 + T], in_=r3t[0:1, :]), "ST", 16)
            self.sec("sync", st3)
            pipeC3["Pend"] = self.c["P"]
            pipeC3["Vvx3"] = self.c["V"]
            pipeC3["Send"] = self.c["S"]
            pipeC3["st3"] = self.c["ST"]
            if ch == 0:
                self.dump([("C_attnb", attnb[:], [128, 4096], BF16),
                           ("C_x3", u2[:], [128, 4096], F32)])

        self.free("p_bc1C3", "p_bc2C3", "p_aC0", "p_aC1", "p_s1C", "p_s2C")
        self.free("WobC", "xf3C", "z2fC", "qf3C", "attnbC", "u2C", "u2bC", "u2sqC",
                  "x3bC", "x3sqC", "m3tC", "r3tC", "s3cC", "mrow3C", "rrow3C")
        self.free("cmaxC", "ncmaxC", "csumC", "rcsC", "bo_colC", "gam_colC",
                  "bet_colC", "red1C", "ctxbC")


    def phase_D(self):
        b = self
        wrcol = self.sb("wrcolD", [128, 64], F32)
        x3c = self.sb("x3cD", [128, 8], F32)
        x3sqc = self.sb("x3sqcD", [128, 8], F32)
        of128 = self.sb("of128D", [128, 1], F32)
        on8 = self.sb("on8D", [8, 1], F32)
        s8a = self.sb("s8aD", [8, 1], F32)
        s8b = self.sb("s8bD", [8, 1], F32)
        s8ab = self.sb("s8abD", [8, 2], F32)
        tot2 = self.sb("tot2D", [1, 2], F32)
        dbg4 = self.sb("dbg4D", [1, 8], F32)
        srow = self.sb("srowD", [1, 8], F32)
        brrow = self.sb("brrowD", [1, 8], F32)
        mr2 = self.sb("mr2D", [1, 2], F32)
        mrcol = self.sb("mrcolD", [128, 2], F32)
        m1s = self.sb("m1sD", [1, 2], F32)
        ind0 = self.sb("ind0D", [1, 8], F32)
        ind1 = self.sb("ind1D", [1, 8], F32)
        ind0c = self.sb("ind0cD", [8, 1], F32)
        ind1c = self.sb("ind1cD", [8, 1], F32)
        indc2 = self.sb("indc2D", [8, 2], F32)
        ind2 = self.sb("ind2D", [8, 64], F32)
        rep16 = self.sb("rep16D", [8, 128], F32)
        p16 = self.sb("p16D", [16, 128], F32)
        r16 = self.sb("r16D", [16, 64], F32)
        m16 = self.sb("m16D", [128, 64], F32)
        selb = self.sb("selbD", [128, 64], BF16)
        pi16 = self.sb("pi16D", [16, 128], F32)
        pi8 = self.sb("pi8D", [8, 128], F32)
        pi16b = self.sb("pi16bD", [16, 64], F32)
        p_sm = self.ps("p_smD", [128, 512], F32)
        p_sm2 = self.ps("p_sm2D", [128, 512], F32)

        def ld0(sync):
            b.w(sync, "V", "S", "P", "ST")
            for g in range(8):
                b.inc(sync.dma_start(out=wrcol[:, g * 8:(g + 1) * 8],
                                     in_=b.di["Wr"][g * 128:(g + 1) * 128, :]), "L", 16)
                b.inc(sync.dma_start(out=x3c[:, g:g + 1],
                                     in_=b.xh3[g * 128:(g + 1) * 128, 0:1],
                                     allow_slow_non_contiguous=True), "L", 16)
            b.inc(sync.dma_start(out=brrow[:], in_=b.di["br"]), "L", 16)
        self.sec("sync", ld0)
        def gio(g):
            b.w(g, "L", "ST", "V", "S", "P")
            g.iota(p16[:], pattern=[[0, 8], [1, 16]], base=0, channel_multiplier=0,
                   allow_small_or_imprecise_dtypes=True)
            g.iota(rep16[:], pattern=[[1, 8], [0, 16]], base=0, channel_multiplier=0,
                   allow_small_or_imprecise_dtypes=True)
            g.iota(r16[:], pattern=[[0, 2], [1, 32]], base=0, channel_multiplier=0,
                   allow_small_or_imprecise_dtypes=True)
            g.iota(pi16[:], pattern=[[0, 128]], base=0, channel_multiplier=1,
                   allow_small_or_imprecise_dtypes=True)
            g.iota(pi8[:], pattern=[[0, 128]], base=0, channel_multiplier=1,
                   allow_small_or_imprecise_dtypes=True)
            b.inc(g.iota(pi16b[:], pattern=[[0, 64]], base=0, channel_multiplier=1,
                         allow_small_or_imprecise_dtypes=True), "G")
        self.sec("gpsimd", gio)
        def vio(v):
            b.w(v, "G")
            v.memset(of128[:], 1.0)
            v.memset(on8[:], 1.0)
            v.memset(ind2[:], 0.0)
            v.tensor_tensor(p16[:], p16[:], pi16[:], op=ALU.is_equal)
            v.tensor_tensor(rep16[:], rep16[:], pi8[:], op=ALU.is_equal)
            b.inc(v.tensor_tensor(r16[:], r16[:], pi16b[:], op=ALU.is_equal), "V")
        self.sec("vector", vio)
        def tm16(t):
            b.w(t, "V")
            b.inc(t.matmul(p_sm[:, 0:64], p16[:], r16[:], start=True, stop=True), "P")
        self.sec("tensor", tm16)
        def vm16(v):
            b.w(v, "P")
            b.inc(v.tensor_copy(m16[:], p_sm[:, 0:64]), "V")
        self.sec("vector", vm16)
        def ssq(s):
            b.w(s, "L")
            b.inc(s.square(x3sqc[:], x3c[:]), "S")
        self.sec("scalar", ssq)
        def tp1(t):
            b.w(t, "S", "V")
            b.inc(t.matmul(p_sm[0:8, 0:1], x3c[:], of128[:], start=True, stop=True), "P")
            b.inc(t.matmul(p_sm2[0:8, 0:1], x3sqc[:], of128[:], start=True, stop=True), "P")
        self.sec("tensor", tp1)
        def vp1(v):
            b.w(v, "P")
            v.tensor_copy(s8ab[:, 0:1], p_sm[0:8, 0:1])
            b.inc(v.tensor_copy(s8ab[:, 1:2], p_sm2[0:8, 0:1]), "V")
        self.sec("vector", vp1)
        def tp2(t):
            b.w(t, "V")
            b.inc(t.matmul(p_sm[0:1, 0:2], on8[:], s8ab[:], start=True, stop=True), "P")
        self.sec("tensor", tp2)
        def vp2(v):
            b.w(v, "P")
            b.inc(v.tensor_copy(tot2[:], p_sm[0:1, 0:2]), "V")
            b.w(v, "V")
            v.tensor_scalar(mr2[:, 0:1], tot2[:, 0:1], self.c_invD[0:1, 0:1], None,
                            op0=ALU.mult)
            b.inc(v.tensor_scalar(mr2[:, 1:2], tot2[:, 1:2], self.c_invD[0:1, 0:1],
                                  None, op0=ALU.mult), "V")
            b.w(v, "V")
            b.inc(v.tensor_tensor(tot2[:, 0:1], mr2[:, 0:1], mr2[:, 0:1],
                                  op=ALU.mult), "V")
            b.w(v, "V")
            b.inc(v.tensor_tensor(mr2[:, 1:2], mr2[:, 1:2], tot2[:, 0:1],
                                  op=ALU.subtract), "V")
        self.sec("vector", vp2)
        def vdbg(v):
            b.w(v, "V")
            v.tensor_copy(dbg4[:, 0:2], tot2[:])
            b.inc(v.tensor_copy(dbg4[:, 2:4], mr2[:]), "V")
        self.sec("vector", vdbg)
        def sr(s):
            b.w(s, "V")
            s.activation(mr2[:, 1:2], mr2[:, 1:2], AF.Ln, bias=self.c_eps[0:1, 0:1])
            b.inc(s.activation(mr2[:, 1:2], mr2[:, 1:2], AF.Exp,
                               scale=self.c_mhalf[0:1, 0:1]), "S")
        self.sec("scalar", sr)
        def tbc(t):
            b.w(t, "S")
            b.inc(t.matmul(p_sm[:, 0:2], self.ones1f[:], mr2[:], start=True, stop=True), "P")
        self.sec("tensor", tbc)
        def vz3c(v):
            b.w(v, "P")
            b.inc(v.tensor_copy(mrcol[:], p_sm[:, 0:2]), "V")
            b.w(v, "V")
            v.tensor_scalar(x3c[:], x3c[:], mrcol[:, 0:1], None, op0=ALU.subtract)
            b.inc(v.tensor_scalar(x3c[:], x3c[:], mrcol[:, 1:2], None, op0=ALU.mult), "V")
        self.sec("vector", vz3c)
        def tsc(t):
            b.w(t, "V")
            for g in range(8):
                b.inc(t.matmul(p_sm[0:8, 0:1], wrcol[:, g * 8:(g + 1) * 8],
                               x3c[:, g:g + 1], start=(g == 0), stop=(g == 7)), "P")
        self.sec("tensor", tsc)
        def vsc(v):
            b.w(v, "P")
            b.inc(v.tensor_copy(s8a[:], p_sm[0:8, 0:1]), "V")
        self.sec("vector", vsc)
        def trw(t):
            b.w(t, "V")
            b.inc(t.matmul(p_sm[0:1, 0:8], s8a[:], self.ident_f[0:8, 0:8],
                           start=True, stop=True), "P")
        self.sec("tensor", trw)
        def vtop(v):
            b.w(v, "P")
            b.inc(v.tensor_copy(srow[:], p_sm[0:1, 0:8]), "V")
            b.w(v, "V")
            b.inc(v.tensor_tensor(srow[:], srow[:], brrow[:], op=ALU.add), "V")
            b.w(v, "V")
            b.inc(v.reduce_max(m1s[:, 0:1], srow[:], axis=AX.X), "V")
            b.w(v, "V")
            b.inc(v.tensor_single_scalar(ind0[:], srow[:], m1s[:, 0:1], op=ALU.is_ge), "V")
            b.w(v, "V")
            b.inc(v.scalar_tensor_tensor(ind1[:], ind0[:], -1e30, srow[:],
                                         op0=ALU.mult, op1=ALU.add), "V")
            b.w(v, "V")
            b.inc(v.reduce_max(m1s[:, 1:2], ind1[:], axis=AX.X), "V")
            b.w(v, "V")
            b.inc(v.tensor_single_scalar(ind1[:], ind1[:], m1s[:, 1:2], op=ALU.is_ge), "V")
            b.w(v, "V")
        self.sec("vector", vtop)
        def tic(t):
            b.w(t, "V")
            b.inc(t.matmul(p_sm[0:8, 0:1], ind0[:], self.ones1f[0:1, 0:1],
                           start=True, stop=True), "P")
            b.inc(t.matmul(p_sm2[0:8, 0:1], ind1[:], self.ones1f[0:1, 0:1],
                           start=True, stop=True), "P")
        self.sec("tensor", tic)
        def vic(v):
            b.w(v, "P")
            v.tensor_copy(ind0c[:], p_sm[0:8, 0:1])
            b.inc(v.tensor_copy(ind1c[:], p_sm2[0:8, 0:1]), "V")
            b.w(v, "V")
            v.tensor_copy(indc2[:, 0:1], ind0c[:])
            v.tensor_copy(indc2[:, 1:2], ind1c[:])
            for r in range(16):
                v.tensor_copy(ind2[:, r:r + 1], ind0c[:])
                v.tensor_copy(ind2[:, 32 + r:32 + r + 1], ind1c[:])
            b.inc(v.engine_nop(), "V")
        self.sec("vector", vic)
        def tir(t):
            b.w(t, "V")
            b.inc(t.matmul(p_sm[:, 0:64], rep16[:], ind2[:], start=True, stop=True), "P")
        self.sec("tensor", tir)
        def vsel(v):
            b.w(v, "P")
            b.inc(v.tensor_tensor(m16[:], m16[:], p_sm[:, 0:64], op=ALU.mult), "V")
            b.w(v, "V")
            b.inc(v.tensor_copy(selb[:], m16[:]), "V")
        self.sec("vector", vsel)

        W1g = self.sb("W1gD", [128, 2 * 8192], BF16)
        W2g = self.sb("W2gD", [128, 2 * 8192], BF16)
        b1g = self.sb("b1gD", [128, 16], F32)
        b2g = self.sb("b2gD", [128, 16], F32)
        b2h = self.sb("b2hD", [128, 8], F32)
        wslab = self.sb("wslabD", [128, 1024], F32)
        wtmp = self.sb("wtmpD", [64, 1024], BF16)
        wslabb = self.sb("wslabbD", [128, 1024], BF16)
        bsl = self.sb("bslD", [8, 128], F32)
        bsl2 = self.sb("bsl2D", [2, 128], F32)
        p_gw = self.ps("p_gwD", [64, 512], F32)
        p_gw2 = self.ps("p_gw2D", [64, 512], F32)

        for wi, (wname, wreg) in enumerate([("W1", W1g), ("W2", W2g)]):
            for sl in range(64):
                r0_ = sl * 16
                def lws(sync, wname=wname, r0_=r0_):
                    b.w(sync, "S", "P")
                    b.inc(sync.dma_start(out=wslab[:],
                                         in_=b.di[wname][:, r0_:r0_ + 16, :]), "L", 16)
                self.sec("sync", lws)
                def scast(s, wname=wname):
                    b.w(s, "L", "P")
                    b.inc(s.copy(wslabb[:], wslab[:]), "S")
                self.sec("scalar", scast)
                def tg(t, wname=wname):
                    b.w(t, "S")
                    b.inc(t.matmul(p_gw[:], selb[:], wslabb[:, 0:512],
                                   start=True, stop=True), "P")
                    b.inc(t.matmul(p_gw2[:], selb[:], wslabb[:, 512:1024],
                                   start=True, stop=True), "P")
                self.sec("tensor", tg)
                def sg(s, r0_=r0_):
                    b.w(s, "P", "ST")
                    for j in range(2):
                        s.copy(wtmp[j * 32:j * 32 + 16, 0:512], p_gw[j * 32:j * 32 + 16, :])
                        b.inc(s.copy(wtmp[j * 32:j * 32 + 16, 512:1024],
                                     p_gw2[j * 32:j * 32 + 16, :]), "S")
                self.sec("scalar", sg)
                def stw(sync, wreg=wreg, r0_=r0_):
                    b.w(sync, "S")
                    g = r0_ // 128
                    pr = r0_ % 128
                    for j in range(2):
                        base = j * 8192 + g * 1024
                        b.inc(sync.dma_start(out=wreg[pr:pr + 16, base:base + 1024],
                                             in_=wtmp[j * 32:j * 32 + 16, :]), "ST", 16)
                self.sec("sync", stw)
        for bi, (bname, breg) in enumerate([("b1", b1g), ("b2", b2g)]):
            for g in range(8):
                def lbs(sync, bname=bname, g=g):
                    b.w(sync, "V", "P")
                    b.inc(sync.dma_start(out=bsl[:], in_=b.di[bname][:, g, :]), "L", 16)
                self.sec("sync", lbs)
                def tb1(t, g=g, p_sm=p_sm):
                    b.w(t, "L", "V")
                    b.inc(t.matmul(p_gw[0:2, 0:128], indc2[:], bsl[:],
                                   start=True, stop=True), "P")
                self.sec("tensor", tb1)
                def vb1(v, g=g):
                    b.w(v, "P")
                    b.inc(v.tensor_copy(bsl2[:], p_gw[0:2, 0:128]), "V")
                self.sec("vector", vb1)
                def tb2(t, g=g, p_sm=p_sm):
                    b.w(t, "V")
                    b.inc(t.matmul(p_sm2[:, 0:2], bsl2[:], self.ident_f[0:2, 0:2],
                                   start=True, stop=True), "P")
                self.sec("tensor", tb2)
                def vb2(v, breg=breg, g=g, p_sm=p_sm):
                    b.w(v, "P")
                    b.inc(v.tensor_copy(breg[:, g * 2:g * 2 + 2], p_sm2[:, 0:2]), "V")
                self.sec("vector", vb2)
        def vb2h(v):
            b.w(v, "V")
            for g in range(8):
                v.tensor_tensor(b2h[:, g:g + 1], b2g[:, g * 2:g * 2 + 1],
                                b2g[:, g * 2 + 1:g * 2 + 2], op=ALU.add)
            b.inc(v.engine_nop(), "V")
            b.w(v, "V")
            b.inc(v.tensor_scalar(b2h[:], b2h[:], 0.5, None, op0=ALU.mult), "V")
        self.sec("vector", vb2h)
        self.free("p_gwD", "p_gw2D", "wslabD", "wtmpD", "wslabbD", "bslD", "bsl2D")

        xf4 = self.sb("xf4D", [128, 4096], F32)
        z3b = self.sb("z3bD", [128, 4096], BF16)
        hjb = self.sb("hjbD", [128, 2 * 4096], BF16)
        x4 = self.sb("x4D", [128, 4096], F32)
        mrow4 = self.sb("mrow4D", [1, 512], F32)
        rrow4 = self.sb("rrow4D", [1, 512], F32)
        p_bc1 = self.ps("p_bc1D", [128, 512], F32)
        p_bc2 = self.ps("p_bc2D", [128, 512], F32)
        p_h = [self.ps("p_hD0", [128, 512], F32), self.ps("p_hD1", [128, 512], F32)]

        pipeD = {"Pend": 0, "Vend": 0, "Send": 0, "sh": [], "so": [], "st4": 0}
        for ch in range(NCH):
            t0 = ch * T
            def ld(sync, t0=t0, ch=ch):
                if ch == 0:
                    b.w(sync, "V", "S", "P", "ST")
                else:
                    b.wv(sync, "P", pipeD["Pend"])
                    b.wv(sync, "V", pipeD["Vend"])
                    b.wv(sync, "S", pipeD["Send"])
                for g in range(8):
                    b.inc(sync.dma_start(out=xf4[:, g * 512:(g + 1) * 512],
                                         in_=b.xh3[g * 128:(g + 1) * 128, t0:t0 + T]), "L", 16)
                b.inc(sync.dma_start(out=mrow4[:], in_=b.sth[4:5, t0:t0 + T]), "L", 16)
                b.inc(sync.dma_start(out=rrow4[:], in_=b.sth[5:6, t0:t0 + T]), "L", 16)
            self.sec("sync", ld)
            def tb(t, ch=ch):
                b.w(t, "L")
                if ch == 0:
                    b.w(t, "V", "S")
                b.inc(t.matmul(p_bc1[:], self.ones1f[:], mrow4[:], start=True, stop=True), "P")
                b.inc(t.matmul(p_bc2[:], self.ones1f[:], rrow4[:], start=True, stop=True), "P")
            self.sec("tensor", tb)
            PtbD = self.c["P"]
            def vz4(v, PtbD=PtbD):
                b.wv(v, "P", PtbD)
                for g in range(8):
                    gsl = slice(g * 512, (g + 1) * 512)
                    v.tensor_tensor(x4[:, gsl], xf4[:, gsl], p_bc1[:], op=ALU.subtract)
                    b.inc(v.tensor_tensor(x4[:, gsl], x4[:, gsl], p_bc2[:], op=ALU.mult), "V")
            self.sec("vector", vz4)
            def sz4(s):
                b.w(s, "V")
                for g in range(8):
                    gsl = slice(g * 512, (g + 1) * 512)
                    b.inc(s.copy(z3b[:, gsl], x4[:, gsl]), "S")
            self.sec("scalar", sz4)
            Ssz4 = self.c["S"]
            for j in range(2):
                for ec in range(8):
                    pp = p_h[ec % 2]
                    def th(t, j=j, ec=ec, pp=pp, Ssz4=Ssz4):
                        b.wv(t, "S", Ssz4)
                        if len(pipeD["sh"]) >= 2:
                            b.wv(t, "S", pipeD["sh"][-2])
                        for g in range(8):
                            b.inc(t.matmul(pp[:],
                                           W1g[:, j * 8192 + g * 1024 + ec * 128:
                                               j * 8192 + g * 1024 + (ec + 1) * 128],
                                           z3b[:, g * 512:(g + 1) * 512],
                                           start=(g == 0), stop=(g == 7)), "P")
                    self.sec("tensor", th)
                    Pth = self.c["P"]
                    def shh(s, j=j, ec=ec, pp=pp, Pth=Pth):
                        b.wv(s, "P", Pth)
                        b.inc(s.activation(hjb[:, j * 4096 + ec * 512:j * 4096 + (ec + 1) * 512],
                                           pp[:], AF.Gelu, bias=b1g[:, ec * 2 + j:ec * 2 + j + 1]),
                              "S")
                    self.sec("scalar", shh)
                    pipeD["sh"].append(self.c["S"])
            Ssh15 = self.c["S"]
            for fc in range(8):
                pp = p_h[fc % 2]
                def to(t, fc=fc, pp=pp, Ssh15=Ssh15):
                    b.wv(t, "S", Ssh15)
                    if len(pipeD["so"]) >= 2:
                        b.wv(t, "S", pipeD["so"][-2])
                    for j in range(2):
                        for g in range(8):
                            b.inc(t.matmul(pp[:],
                                           W2g[:, j * 8192 + g * 1024 + fc * 128:
                                               j * 8192 + g * 1024 + (fc + 1) * 128],
                                           hjb[:, j * 4096 + g * 512:j * 4096 + (g + 1) * 512],
                                           start=(j == 0 and g == 0),
                                           stop=(j == 1 and g == 7)), "P")
                self.sec("tensor", to)
                PtoD = self.c["P"]
                def so(s, fc=fc, pp=pp, PtoD=PtoD):
                    b.wv(s, "P", PtoD)
                    b.wv(s, "ST", pipeD["st4"])
                    b.inc(s.activation(x4[:, fc * 512:(fc + 1) * 512], pp[:], AF.Identity,
                                       scale=self.c_half, bias=b2h[:, fc:fc + 1]), "S")
                self.sec("scalar", so)
                SsoD = self.c["S"]
                pipeD["so"].append(SsoD)
                def vo(v, fc=fc, SsoD=SsoD):
                    b.wv(v, "S", SsoD)
                    b.inc(v.tensor_tensor(x4[:, fc * 512:(fc + 1) * 512],
                                          x4[:, fc * 512:(fc + 1) * 512],
                                          xf4[:, fc * 512:(fc + 1) * 512], op=ALU.add), "V")
                self.sec("vector", vo)
            def st4(sync, t0=t0):
                b.w(sync, "V")
                for g in range(8):
                    b.inc(sync.dma_start(out=b.xh1[g * 128:(g + 1) * 128, t0:t0 + T],
                                         in_=x4[:, g * 512:(g + 1) * 512]), "ST", 16)
            self.sec("sync", st4)
            pipeD["Pend"] = self.c["P"]
            pipeD["Vend"] = self.c["V"]
            pipeD["Send"] = self.c["S"]
            pipeD["st4"] = self.c["ST"]
            if self.dbg and ch == 0:
                self.dump([("D_srow", srow[:], [1, 8], F32),
                           ("D_z3c", x3c[:], [128, 8], F32),
                           ("D_s8b", s8ab[:, 1:2], [8, 1], F32),
                           ("D_tot2", tot2[:], [1, 2], F32),
                           ("D_dbg4", dbg4[:], [1, 8], F32),
                           ("D_on8", on8[:], [8, 1], F32),
                           ("D_x3sqc", x3sqc[:], [128, 8], F32),
                           ("D_mr2", mr2[:], [1, 2], F32),
                           ("D_wrcol", wrcol[:], [128, 64], F32),
                           ("D_ind0", ind0[:], [1, 8], F32),
                           ("D_ind1", ind1[:], [1, 8], F32),
                           ("D_selb", selb[:], [128, 64], BF16),
                           ("D_b1g", b1g[:], [128, 16], F32),
                           ("D_w1g", W1g[:, 0:4096], [128, 4096], BF16),
                           ("D_hjb", hjb[:], [128, 8192], BF16),
                           ("D_x4", x4[:], [128, 4096], F32)])

        self.free("p_bc1D", "p_bc2D", "p_hD0", "p_hD1")
        self.free("xf4D", "z3bD", "hjbD", "x4D", "mrow4D", "rrow4D")
        self.free("W1gD", "W2gD", "b1gD", "b2gD", "b2hD")
        self.free("wrcolD", "x3cD", "x3sqcD", "of128D", "on8D", "s8aD", "s8bD", "s8abD",
                  "tot2D", "dbg4D", "srowD", "brrowD", "mr2D", "mrcolD", "m1sD", "ind0D",
                  "ind1D", "ind0cD", "ind1cD", "indc2D", "ind2D", "rep16D", "p16D",
                  "r16D", "m16D", "selbD", "pi16D", "pi8D", "pi16bD", "p_smD",
                  "p_sm2D")

    def epilogue(self, src):
        b = self
        efm = self.sb("efm", [128, 4096], F32)
        etm = self.sb("etm", [128, 1024], F32)
        p_ep = self.ps("p_ep", [128, 512], F32)
        for ch in range(NCH):
            t0 = ch * T
            def ld(sync, t0=t0):
                b.w(sync, "P", "ST")
                for g in range(8):
                    b.inc(sync.dma_start(out=efm[:, g * 512:(g + 1) * 512],
                                         in_=src[g * 128:(g + 1) * 128, t0:t0 + T]), "L", 16)
            self.sec("sync", ld)
            for tt in range(4):
                for half in range(2):
                    def ttr(t, tt=tt, half=half):
                        b.w(t, "L", "S", "ST")
                        for j in range(4):
                            g = half * 4 + j
                            b.inc(t.transpose(
                                p_ep[:, j * 128:(j + 1) * 128],
                                efm[:, g * 512 + tt * 128: g * 512 + tt * 128 + 128],
                                self.ident_f[:]), "P")
                    self.sec("tensor", ttr)
                    def sev(s, half=half):
                        b.w(s, "P", "ST")
                        b.inc(s.copy(etm[:, half * 512:(half + 1) * 512], p_ep[:]), "S")
                    self.sec("scalar", sev)
                def stt(sync, tt=tt, t0=t0):
                    b.w(sync, "S")
                    b.inc(sync.dma_start(out=b.out[t0 + tt * 128: t0 + (tt + 1) * 128, :],
                                         in_=etm[:]), "ST", 16)
                self.sec("sync", stt)
        def fin_(sync):
            b.w(sync, "ST")
        self.sec("sync", fin_)
        self.free("efm", "etm", "p_ep")

    def build(self):
        self.prep_common()
        self.phase_A()
        src = self.xh1
        if self.nphase >= 2:
            self.phase_B()
            src = self.xh2
        if self.nphase >= 3:
            self.phase_C()
            src = self.xh3
        if self.nphase >= 4:
            self.phase_D()
            src = self.xh1
        self.epilogue(src)
        return self.fin()


def _host_prep(inputs):
    f = np.ascontiguousarray
    w = {
        "A_log": inputs["L0_attn_A_log"].reshape(1, 16),
        "xpw": inputs["L0_attn_x_proj_w"],
        "dtw": inputs["L0_attn_dt_proj_w"],
        "dtb": inputs["L0_attn_dt_proj_b"].reshape(8, 128),
        "Dv": inputs["L0_attn_D"].reshape(8, 128),
        "Win": inputs["L0_ffn_W_in"],
        "bin": inputs["L0_ffn_b_in"].reshape(32, 128),
        "Wout": inputs["L0_ffn_W_out"],
        "bout": inputs["L0_ffn_b_out"].reshape(8, 128),
        "Wq": inputs["L1_attn_Wq"], "Wk": inputs["L1_attn_Wk"],
        "Wv": inputs["L1_attn_Wv"], "Wo": inputs["L1_attn_Wo"],
        "bo": inputs["L1_attn_bo"].reshape(8, 128),
        "gam": inputs["L1_attn_gamma"].reshape(8, 128),
        "bet": inputs["L1_attn_beta"].reshape(8, 128),
        "Wr": inputs["L1_ffn_Wr"], "br": inputs["L1_ffn_br"].reshape(1, 8),
        "W1": inputs["L1_ffn_W1"], "b1": inputs["L1_ffn_b1"].reshape(NE, 8, 128),
        "W2": inputs["L1_ffn_W2"], "b2": inputs["L1_ffn_b2"].reshape(NE, 8, 128),
    }
    w = {k: f(np.asarray(v, dtype=np.float32)) for k, v in w.items()}
    u = np.asarray(inputs["L0_attn_u"], dtype=np.float32)
    return [dict(w, u=f(u[c])) for c in range(8)]


def kernel(**inputs):
    kb = KB(nphase=_NPHASE)
    nc = kb.build()
    in_maps = _host_prep(inputs)
    res = run_bass_kernel_spmd(nc, in_maps, core_ids=list(range(8)))
    out = np.stack([np.asarray(r["out"]) for r in res.results], axis=0)
    return out
